# version 06
# baseline (speedup 1.0000x reference)
import numpy as np
import math
import os
from contextlib import ExitStack
DEBUG = bool(int(os.environ.get("MK_DEBUG", "0")))

import concourse.bass as bass
import concourse.mybir as mybir
from concourse.bass_utils import run_bass_kernel_spmd

F32 = mybir.dt.float32
BF16 = mybir.dt.bfloat16
I32 = mybir.dt.int32
AF = mybir.ActivationFunctionType
ALU = mybir.AluOpType
AX = mybir.AxisListType

D = 4096
T = 1024
NT = 8
S = 2048
NQ = 16
H = 16
FI = 32
NE = 28
ER = 224
SCALE = 1.0 / 128.0
EPS = 1e-5
MASKNEG = -1e30
FB = 256
NACC = 6

ENGS = ("pe", "act", "dve", "pool", "sp")


def _alibi_slopes(n_heads):
    cp2 = 2 ** math.floor(math.log2(n_heads))
    base = 2 ** (-(2 ** (-(math.log2(cp2) - 3))))
    slopes = base ** np.arange(1, cp2 + 1, dtype=np.float32)
    if cp2 != n_heads:
        extra_base = 2 ** (-(2 ** (-(math.log2(2 * cp2) - 3))))
        n_rem = min(cp2, n_heads - cp2)
        extra = extra_base ** np.arange(1, 1 + 2 * n_rem, 2, dtype=np.float32)
        slopes = np.concatenate([slopes, extra])
    return slopes.astype(np.float32)


class P:
    def __init__(self, nc):
        self.nc = nc
        self.q = {e: [] for e in ENGS}
        self.cnt = {}

    def emit(self, eng, fn):
        self.q[eng].append(fn)

    def wait(self, eng, sem, val):
        if val and val > 0:
            self.emit(eng, lambda E, s=sem, v=val: E.wait_ge(s, v))

    def op(self, eng, fn, sem=None, n=1):
        if sem is None:
            self.emit(eng, fn)
            return None
        self.cnt[sem] = self.cnt.get(sem, 0) + n
        c = self.cnt[sem]
        self.emit(eng, lambda E, f=fn, s=sem, k=n: f(E).then_inc(s, k))
        return c

    def dma(self, eng, out, in_, sem, **kw):
        return self.op(eng, lambda E, o=out, i=in_, k=dict(kw): E.dma_start(out=o, in_=i, **k),
                       sem=sem, n=16)

    def c(self, sem):
        return self.cnt.get(sem, 0)

    def opw(self, eng, fn, sem):
        c = self.op(eng, fn, sem)
        self.wait(eng, sem, c)
        return c


def build_nc():
    nc = bass.Bass("TRN2", target_bir_lowering=False)

    x0 = nc.dram_tensor("x0", [T, D], F32, kind="ExternalInput")
    wattn = nc.dram_tensor("wattn", [D, 3 * D], F32, kind="ExternalInput")
    battn = nc.dram_tensor("battn", [1, 3 * D], F32, kind="ExternalInput")
    wproj = nc.dram_tensor("wproj", [2048, D], F32, kind="ExternalInput")
    bproj = nc.dram_tensor("bproj", [1, D], F32, kind="ExternalInput")
    bwt = nc.dram_tensor("bwt", [D, D], F32, kind="ExternalInput")
    gwt = nc.dram_tensor("gwt", [D, NE], F32, kind="ExternalInput")
    lat = nc.dram_tensor("lat", [D, ER], F32, kind="ExternalInput")
    lbp = nc.dram_tensor("lbp", [ER, D], F32, kind="ExternalInput")
    slopes_i = nc.dram_tensor("slopes_i", [128, H], F32, kind="ExternalInput")
    strip_i = nc.dram_tensor("strip_i", [128, 2176], F32, kind="ExternalInput")
    ident_i = nc.dram_tensor("ident_i", [128, 128], F32, kind="ExternalInput")
    rank_i = nc.dram_tensor("rank_i", [1, 1], I32, kind="ExternalInput")
    y = nc.dram_tensor("y", [T, D], F32, kind="ExternalOutput")

    if DEBUG:
        dbg_xres1 = nc.dram_tensor("dbg_xres1", [T, D], F32, kind="ExternalOutput")
        dbg_xnT = nc.dram_tensor("dbg_xnT", [128, NT * FI * 128], BF16, kind="ExternalOutput")
        dbg_stats = nc.dram_tensor("dbg_stats", [128, 8 * NT], F32, kind="ExternalOutput")
        dbg_xin = nc.dram_tensor("dbg_xin", [128, 2 * D], F32, kind="ExternalOutput")
        dbg_xn = nc.dram_tensor("dbg_xn", [128, 2 * D], BF16, kind="ExternalOutput")
        dbg_hgT1 = nc.dram_tensor("dbg_hgT1", [128, NT * 128], BF16, kind="ExternalOutput")
        dbg_wblk = nc.dram_tensor("dbg_wblk", [128, 2 * FI * FB], BF16, kind="ExternalOutput")
        dbg_qkvall = nc.dram_tensor("dbg_qkvall", [S, 6144], BF16, kind="ExternalOutput")
        dbg_rsa = nc.dram_tensor("dbg_rsa", [T, 2048], F32, kind="ExternalOutput")
        dbg_rsb = nc.dram_tensor("dbg_rsb", [T, 2048], F32, kind="ExternalOutput")
        dbg_xres2 = nc.dram_tensor("dbg_xres2", [T, D], F32, kind="ExternalOutput")
        dbg_attno = nc.dram_tensor("dbg_attno", [128, NT * FI * 128], BF16, kind="ExternalOutput")
        dbg_rsina = nc.dram_tensor("dbg_rsina", [S, 2048], F32, kind="ExternalOutput")
        dbg_rsinb = nc.dram_tensor("dbg_rsinb", [S, 2048], F32, kind="ExternalOutput")
        dbg_small = nc.dram_tensor("dbg_small", [128, 64], F32, kind="ExternalOutput")
    x_res = nc.dram_tensor("x_res", [T, D], F32)
    qkv_own = nc.dram_tensor("qkv_own", [T, 6144], BF16)
    cc_in = nc.dram_tensor("cc_in", [T, 6144], BF16)
    cc_out = nc.dram_tensor("cc_out", [S, 6144], BF16)
    delta1 = nc.dram_tensor("delta1", [T, D], F32)
    rs_in = [nc.dram_tensor("rs_in_a", [S, 2048], F32),
             nc.dram_tensor("rs_in_b", [S, 2048], F32)]
    rs_out = [nc.dram_tensor("rs_out_a", [T, 2048], F32),
              nc.dram_tensor("rs_out_b", [T, 2048], F32)]
    battn_bf = nc.dram_tensor("battn_bf", [1, 3 * D], BF16)
    lrow_d = nc.dram_tensor("lrow_d", [16, 128], F32)
    qkv_all = nc.dram_tensor("qkv_all", [S, 6144], BF16)

    groups = [[0, 1], [2, 3], [4, 5], [6, 7]]

    es = ExitStack()
    sb = lambda n, s, d=F32: es.enter_context(nc.sbuf_tensor(n, s, d))
    sem = lambda n: es.enter_context(nc.semaphore(n))

    sWd = sem("sWd")
    sXd = sem("sXd")
    sSt = sem("sSt")
    sPe = sem("sPe")
    sAc = sem("sAc")
    sDv = sem("sDv")
    sCc = sem("sCc")
    sPd = sem("sPd")
    sXo = sem("sXo")
    ALLSEMS = (sWd, sXd, sSt, sPe, sAc, sDv, sCc, sPd, sXo)

    p = P(nc)

    def phase_barrier(skip=()):
        vals = [(s, p.c(s)) for s in ALLSEMS if s not in skip]
        for eng in ENGS:
            for s, v in vals:
                p.wait(eng, s, v)

    ident_b = sb("ident_b", [128, 128], BF16)
    ident_f = sb("ident_f", [128, 128], F32)
    slopes_s = sb("slopes_s", [128, H])
    xnT = sb("xnT", [128, NT * FI * 128], BF16)
    wblk = sb("wblk", [128, 2 * FI * FB], BF16)
    st_sum = sb("st_sum", [128, NT])
    st_ssq = sb("st_ssq", [128, NT])
    st_mu = sb("st_mu", [128, NT])
    st_e2 = sb("st_e2", [128, NT])
    st_m2 = sb("st_m2", [128, NT])
    st_var = sb("st_var", [128, NT])
    st_ln = sb("st_ln", [128, NT])
    st_rstd = sb("st_rstd", [128, NT])

    WBW = FI * FB

    p.dma("pool", ident_b[:, :], ident_i[:, :], sWd)
    p.dma("pool", slopes_s[:, :], slopes_i[:, :], sWd)
    c_const = p.c(sWd)
    c_identf = p.dma("sp", ident_f[:, :], ident_i[:, :], sXd)

    regctx = nc.sync.register(name="rankreg")
    rreg = regctx.__enter__()
    rv_holder = {}

    def _loadrank(E):
        E.reg_load(rreg, rank_i[0:1, 0:1])
        rv_holder["rv"] = E.snap(rreg, min_val=0, max_val=1)
        return E.nop()

    p.emit("sp", _loadrank)
    p.wait("pe", sWd, c_const)

    def pred_dma(out, in_, sem_, half, eq):
        p.cnt[sem_] = p.cnt.get(sem_, 0) + 16
        c = p.cnt[sem_]

        def f(E, o=out, i=in_, s=sem_, h=half, e=eq):
            rv = rv_holder["rv"]
            cond = (rv == h) if e else (rv != h)
            return E.dma_start(out=o, in_=i, cond=cond).then_inc(s, 16)

        p.emit("sp", f)
        return c

    def prepass_tile(bufs, t, x_src, add_rs, ffn, track):
        xin, xn_b = bufs["xin"], bufs["xn_b"]
        tp_ps = bufs["tp_ps"]
        slot = t % 2
        xsl = xin[:, slot * D:(slot + 1) * D]
        xnsl = xn_b[:, slot * D:(slot + 1) * D]
        p.wait("sp", sDv, track["slot_free"][slot])
        cx = p.dma("sp", xsl, x_src[t * 128:(t + 1) * 128, :], sXd)
        if add_rs:
            rsl = xin[:, 2 * D:3 * D]
            p.wait("sp", sDv, track["rs_free"])
            if bufs.get("add_delta"):
                cr = p.dma("sp", rsl, delta1[t * 128:(t + 1) * 128, :], sXd)
            else:
                p.dma("sp", xin[:, 2 * D:2 * D + 2048],
                      rs_out[0][t * 128:(t + 1) * 128, :], sXd)
                cr = p.dma("sp", xin[:, 2 * D + 2048:3 * D],
                           rs_out[1][t * 128:(t + 1) * 128, :], sXd)
            p.wait("dve", sXd, cr)
            c_add = p.opw("dve", lambda E, xs=xsl, rs=rsl, tt=t: E.scalar_tensor_tensor(
                xs, xs, 1.0, rs, op0=ALU.mult, op1=ALU.add,
                accum_out=st_sum[:, tt:tt + 1]), sDv)
            track["rs_free"] = c_add
            p.wait("sp", sDv, c_add)
            cst = p.dma("sp", x_res[t * 128:(t + 1) * 128, :], xsl, sSt)
            track["xres_st"][t] = cst
        else:
            p.wait("dve", sXd, cx)
            c_add = p.op("dve", lambda E, xs=xsl, tt=t: E.reduce_sum(
                st_sum[:, tt:tt + 1], xs, axis=AX.X), sDv)
        p.wait("dve", sPe, track["xn_free"][slot])
        p.opw("dve", lambda E, xs=xsl, o=xnsl, tt=t: E.scalar_tensor_tensor(
            o, xs, 1.0, xs, op0=ALU.mult, op1=ALU.mult,
            accum_out=st_ssq[:, tt:tt + 1]), sDv)
        p.opw("dve", lambda E, tt=t: E.tensor_scalar(
            st_mu[:, tt:tt + 1], st_sum[:, tt:tt + 1], 1.0 / D, None, op0=ALU.mult), sDv)
        p.opw("dve", lambda E, tt=t: E.tensor_scalar(
            st_e2[:, tt:tt + 1], st_ssq[:, tt:tt + 1], 1.0 / D, None, op0=ALU.mult), sDv)
        p.opw("dve", lambda E, tt=t: E.tensor_tensor(
            st_m2[:, tt:tt + 1], st_mu[:, tt:tt + 1], st_mu[:, tt:tt + 1], op=ALU.mult), sDv)
        p.opw("dve", lambda E, tt=t: E.scalar_tensor_tensor(
            st_var[:, tt:tt + 1], st_m2[:, tt:tt + 1], -1.0, st_e2[:, tt:tt + 1],
            op0=ALU.mult, op1=ALU.add), sDv)
        c_var = p.opw("dve", lambda E, tt=t: E.tensor_scalar(
            st_var[:, tt:tt + 1], st_var[:, tt:tt + 1], 1.0, EPS,
            op0=ALU.mult, op1=ALU.add), sDv)
        p.wait("act", sDv, c_var)
        p.opw("act", lambda E, tt=t: E.activation(
            st_ln[:, tt:tt + 1], st_var[:, tt:tt + 1], AF.Ln), sAc)
        c_rstd = p.op("act", lambda E, tt=t: E.activation(
            st_rstd[:, tt:tt + 1], st_ln[:, tt:tt + 1], AF.Exp, scale=-0.5), sAc)
        p.wait("dve", sAc, c_rstd)
        c_xn = p.op("dve", lambda E, xs=xsl, o=xnsl, tt=t: E.tensor_scalar(
            o, xs, st_mu[:, tt:tt + 1], st_rstd[:, tt:tt + 1],
            op0=ALU.subtract, op1=ALU.mult), sDv)
        track["slot_free"][slot] = c_xn
        p.wait("pe", sDv, c_xn)
        last_ev = 0
        for g in range(4):
            bank = tp_ps[g % 2]
            p.wait("pe", sAc, track["tp_free"][g % 2])
            for i in range(8):
                fi = g * 8 + i
                cpe = p.op("pe", lambda E, b=bank, ii=i, f=fi, o=xnsl: E.transpose(
                    b[:, ii * 128:(ii + 1) * 128], o[:, f * 128:(f + 1) * 128],
                    ident_b[:, :]), sPe)
            p.wait("act", sPe, cpe)
            cev = p.op("act", lambda E, b=bank, tt=t, g_=g: E.copy(
                xnT[:, (tt * FI + g_ * 8) * 128:(tt * FI + g_ * 8 + 8) * 128], b[:, :]), sAc)
            track["tp_free"][g % 2] = cev
            last_ev = cev
        track["xn_free"][slot] = cpe
        track["xnT_done"][t] = last_ev
        if not ffn:
            return
        latb, gwtb, hg = bufs["latb"], bufs["gwtb"], bufs["hg"]
        hgacc = bufs["acc"][:, 0:ER + NE]
        p.wait("pe", sAc, last_ev)
        p.wait("pe", sDv, track["hg_free"])
        p.wait("pe", sWd, track["c_lora"])
        for fi in range(FI):
            lhsT = xnT[:, (t * FI + fi) * 128:(t * FI + fi + 1) * 128]
            p.op("pe", lambda E, l=lhsT, f=fi, a=hgacc: E.matmul(
                a[:, 0:ER], l, latb[:, f * ER:(f + 1) * ER],
                start=(f == 0), stop=(f == FI - 1), skip_group_check=True), sPe)
            c_hg = p.op("pe", lambda E, l=lhsT, f=fi, a=hgacc: E.matmul(
                a[:, ER:ER + NE], l, gwtb[:, f * NE:(f + 1) * NE],
                start=(f == 0), stop=(f == FI - 1), skip_group_check=True), sPe)
        ga, m1, m2c = bufs["ga"], bufs["m1"], bufs["m2"]
        p.wait("dve", sPe, c_hg)
        p.opw("dve", lambda E, a=hgacc: E.reduce_max(m1[:, 0:1], a[:, ER:ER + NE], axis=AX.X), sDv)
        p.opw("dve", lambda E, a=hgacc: E.tensor_scalar(
            ga[:, 0:NE], a[:, ER:ER + NE], m1[:, 0:1], None, op0=ALU.is_ge), sDv)
        p.opw("dve", lambda E, a=hgacc: E.scalar_tensor_tensor(
            ga[:, 32:32 + NE], ga[:, 0:NE], MASKNEG, a[:, ER:ER + NE],
            op0=ALU.mult, op1=ALU.add), sDv)
        p.opw("dve", lambda E: E.reduce_max(m2c[:, 0:1], ga[:, 32:32 + NE], axis=AX.X), sDv)
        p.opw("dve", lambda E: E.tensor_scalar(
            ga[:, 32:32 + NE], ga[:, 32:32 + NE], m2c[:, 0:1], None, op0=ALU.is_ge), sDv)
        c_d = p.opw("dve", lambda E: E.tensor_sub(m2c[:, 1:2], m1[:, 0:1], m2c[:, 0:1]), sDv)
        p.wait("act", sDv, c_d)
        c_e = p.op("act", lambda E: E.activation(m1[:, 1:2], m2c[:, 1:2], AF.Exp,
                                                 scale=-1.0), sAc)
        p.wait("dve", sAc, c_e)
        p.opw("dve", lambda E: E.tensor_scalar(m1[:, 2:3], m1[:, 1:2], 1.0, 1.0,
                                               op0=ALU.mult, op1=ALU.add), sDv)
        p.opw("dve", lambda E: E.reciprocal(m2c[:, 2:3], m1[:, 2:3]), sDv)
        p.opw("dve", lambda E: E.tensor_scalar(m1[:, 3:4], m2c[:, 2:3], 2.0, None,
                                               op0=ALU.mult), sDv)
        p.opw("dve", lambda E: E.tensor_tensor(m2c[:, 3:4], m1[:, 1:2], m1[:, 3:4],
                                               op=ALU.mult), sDv)
        p.opw("dve", lambda E: E.tensor_scalar(ga[:, 0:NE], ga[:, 0:NE], m1[:, 3:4], None,
                                               op0=ALU.mult), sDv)
        p.opw("dve", lambda E: E.scalar_tensor_tensor(
            ga[:, 0:NE], ga[:, 32:32 + NE], m2c[:, 3:4], ga[:, 0:NE],
            op0=ALU.mult, op1=ALU.add), sDv)
        gexp = ga[:, 0:NE].unsqueeze(2).broadcast_to([128, NE, 8])
        c_hgd = p.op("dve", lambda E, a=hgacc, ge=gexp: E.tensor_tensor(
            hg[:, :].rearrange("p (e r) -> p e r", r=8),
            a[:, 0:ER].rearrange("p (e r) -> p e r", r=8), ge, op=ALU.mult), sDv)
        track["hg_free"] = c_hgd
        hgT1, hgT2 = bufs["hgT1"], bufs["hgT2"]
        p.wait("pe", sDv, c_hgd)
        bank = tp_ps[0]
        p.wait("pe", sAc, track["tp_free"][0])
        p.op("pe", lambda E, b=bank: E.transpose(
            b[:, 0:128], hg[:, 0:128], ident_b[:, :]), sPe)
        c_t2 = p.op("pe", lambda E, b=bank: E.transpose(
            b[0:96, 128:256], hg[:, 128:224], ident_b[:, :]), sPe)
        p.wait("act", sPe, c_t2)
        p.op("act", lambda E, b=bank, tt=t: E.copy(
            hgT1[:, tt * 128:(tt + 1) * 128], b[:, 0:128]), sAc)
        c_ev2 = p.op("act", lambda E, b=bank, tt=t: E.copy(
            hgT2[:, tt * 128:(tt + 1) * 128], b[0:96, 128:256]), sAc)
        track["tp_free"][0] = c_ev2
        track["hgT_done"][t] = c_ev2

    def weight_sweep(bufs, phase, x_src, x_dst, track):
        acc = bufs["acc"]
        if phase == "qkv":
            nb_b = 6144 // FB
            order = list(range(nb_b, 2 * nb_b)) + list(range(nb_b))
            wsrc = wattn
        else:
            order = list(range(D // FB))
            wsrc = bwt
        stf, xold, stb = bufs.get("stf"), bufs.get("xold"), bufs.get("stb")
        battn_t = bufs.get("battn_t")
        hgT1, hgT2 = bufs.get("hgT1"), bufs.get("hgT2")
        lbp1, lbp2 = bufs.get("lbp1"), bufs.get("lbp2")
        p.wait("pe", sAc, max(track["xnT_done"]))
        if phase.startswith("ffn"):
            p.wait("pe", sAc, max(track["hgT_done"]))
        else:
            p.wait("dve", sXd, track["c_battn"])
        nslot = 4
        cnt = 0
        cc_store = None
        for bi, blk in enumerate(order):
            c0 = blk * FB
            wb = wblk[:, (bi % 2) * WBW:(bi % 2) * WBW + WBW]
            p.wait("pool", sPe, track["wblk_free"][bi % 2])
            cw = p.dma("pool", wb.rearrange("p (c f) -> p c f", c=FI),
                       wsrc[:, c0:c0 + FB].rearrange("(c p) f -> p c f", p=128), sWd)
            accsem = sAc if phase == "ffn_delta" else sDv
            for t in range(NT):
                r = t % NACC
                reg = acc[:, r * 512:r * 512 + FB]
                p.wait("pe", sWd, cw)
                p.wait("pe", accsem, track["acc_free"][r])
                for fi in range(FI):
                    lhsT = xnT[:, (t * FI + fi) * 128:(t * FI + fi + 1) * 128]
                    st = (fi == 0)
                    last = (fi == FI - 1) and not phase.startswith("ffn")
                    c_mm = p.op("pe", lambda E, rg=reg, l=lhsT, w=wb, f=fi, s=st, e=last:
                                E.matmul(rg, l, w[:, f * FB:(f + 1) * FB], start=s, stop=e,
                                         skip_group_check=True), sPe)
                if phase.startswith("ffn"):
                    p.op("pe", lambda E, rg=reg, tt=t, cc=c0: E.matmul(
                        rg, hgT1[:, tt * 128:(tt + 1) * 128], lbp1[:, cc:cc + FB],
                        start=False, stop=False, skip_group_check=True), sPe)
                    c_mm = p.op("pe", lambda E, rg=reg, tt=t, cc=c0: E.matmul(
                        rg, hgT2[:, tt * 128:(tt + 1) * 128], lbp2[:, cc:cc + FB],
                        start=False, stop=True, skip_group_check=True), sPe)
                track["wblk_free"][bi % 2] = c_mm
                slot = cnt % nslot
                cnt += 1
                if phase == "ffn_delta":
                    ssl = stf[:, slot * FB:(slot + 1) * FB]
                    p.wait("act", sPe, c_mm)
                    p.wait("act", sSt, track["stf_free"][slot])
                    cev = p.op("act", lambda E, ss=ssl, rg=reg: E.activation(
                        ss, rg, AF.Copy, scale=0.5), sAc)
                    track["acc_free"][r] = cev
                    p.wait("sp", sAc, cev)
                    cst = p.dma("sp", x_dst[t * 128:(t + 1) * 128, c0:c0 + FB], ssl, sSt)
                    track["stf_free"][slot] = cst
                    track["row_stored"][t] = cst
                elif phase == "ffn":
                    xsl = xold[:, slot * FB:(slot + 1) * FB]
                    p.wait("sp", sDv, track["xold_free"][slot])
                    p.wait("sp", sSt, track["xres_st"][t])
                    cxo = p.dma("sp", xsl, x_src[t * 128:(t + 1) * 128, c0:c0 + FB], sXd)
                    ssl = stf[:, slot * FB:(slot + 1) * FB]
                    p.wait("dve", sPe, c_mm)
                    p.wait("dve", sXd, cxo)
                    p.wait("dve", sSt, track["stf_free"][slot])
                    cev = p.op("dve", lambda E, ss=ssl, rg=reg, xs=xsl: E.scalar_tensor_tensor(
                        ss, rg, 0.5, xs, op0=ALU.mult, op1=ALU.add), sDv)
                    track["acc_free"][r] = cev
                    track["xold_free"][slot] = cev
                    p.wait("sp", sDv, cev)
                    cst = p.dma("sp", x_dst[t * 128:(t + 1) * 128, c0:c0 + FB], ssl, sSt)
                    track["stf_free"][slot] = cst
                    track["row_stored"][t] = cst
                else:
                    ssl = stb[:, slot * FB:(slot + 1) * FB]
                    p.wait("dve", sPe, c_mm)
                    p.wait("dve", sSt, track["stf_free"][slot])
                    cev = p.op("dve", lambda E, ss=ssl, rg=reg, cc=c0: E.scalar_tensor_tensor(
                        ss, rg, 1.0, battn_t[:, cc:cc + FB], op0=ALU.mult, op1=ALU.add), sDv)
                    track["acc_free"][r] = cev
                    p.wait("sp", sDv, cev)
                    if c0 < 6144:
                        dst = qkv_own[t * 128:(t + 1) * 128, c0:c0 + FB]
                    else:
                        dst = cc_in[t * 128:(t + 1) * 128, c0 - 6144:c0 - 6144 + FB]
                    cst = p.dma("sp", dst, ssl, sSt)
                    track["stf_free"][slot] = cst
            if phase == "qkv" and bi == nb_b - 1:
                cc_store = p.c(sSt)
        return cc_store

    def new_track(c_lora=0, c_battn=0, acc_init=0):
        return dict(slot_free={0: 0, 1: 0}, xn_free={0: 0, 1: 0}, tp_free={0: 0, 1: 0},
                    xnT_done=[0] * NT, hgT_done=[0] * NT, hg_free=0,
                    wblk_free={0: 0, 1: 0}, acc_free=[acc_init] * NACC,
                    xold_free=[0] * 4, stf_free=[0] * 4,
                    row_stored=[0] * NT, xres_st=[0] * NT, rs_free=0,
                    c_battn=c_battn, c_lora=c_lora)

    def ffn_sublayer(x_src, x_dst, add_rs, delta_mode=False):
        ph = ExitStack()
        psb = lambda n, s, d=F32: ph.enter_context(nc.sbuf_tensor(n, s, d))
        pps = lambda n, s, d=F32: ph.enter_context(nc.psum_tensor(n, s, d))
        sfx = "2" if add_rs else "1"
        nxin = 3 if add_rs else 2
        bufs = dict(
            xin=psb("f_xin" + sfx, [128, nxin * D]),
            xn_b=psb("f_xn" + sfx, [128, 2 * D], BF16),
            stf=psb("f_stf" + sfx, [128, 4 * FB]),
            xold=psb("f_xold" + sfx, [128, 4 * FB]),
            latb=psb("f_latb" + sfx, [128, FI * ER], BF16),
            gwtb=psb("f_gwtb" + sfx, [128, FI * NE], BF16),
            lbp1=psb("f_lbp1" + sfx, [128, D], BF16),
            lbp2=psb("f_lbp2" + sfx, [96, D], BF16),
            hgT1=psb("f_hgT1" + sfx, [128, NT * 128], BF16),
            hgT2=psb("f_hgT2" + sfx, [96, NT * 128], BF16),
            hg=psb("f_hg" + sfx, [128, ER], BF16),
            ga=psb("f_ga" + sfx, [128, 64]),
            m1=psb("f_m1" + sfx, [128, 4]),
            m2=psb("f_m2" + sfx, [128, 4]),
            tp_ps=[pps("f_tp0" + sfx, [128, 1024], BF16),
                   pps("f_tp1" + sfx, [128, 1024], BF16)],
            acc=pps("f_acc" + sfx, [128, NACC * 512]),
        )
        track = new_track()
        p.dma("pool", bufs["latb"][:, :].rearrange("p (c e) -> p c e", c=FI),
              lat[:, :].rearrange("(c p) e -> p c e", p=128), sWd)
        p.dma("pool", bufs["gwtb"][:, :].rearrange("p (c e) -> p c e", c=FI),
              gwt[:, :].rearrange("(c p) e -> p c e", p=128), sWd)
        p.dma("pool", bufs["lbp1"][:, :], lbp[0:128, :], sWd)
        track["c_lora"] = p.dma("pool", bufs["lbp2"][:, :], lbp[128:224, :], sWd)
        for t in range(NT):
            prepass_tile(bufs, t, x_src, add_rs, True, track)
        track["acc_free"] = [track["hg_free"]] * NACC
        if delta_mode:
            p.wait("pe", sDv, track["hg_free"])
            track["acc_free"] = [0] * NACC
        weight_sweep(bufs, "ffn_delta" if delta_mode else "ffn",
                     x_res if add_rs else x_src, x_dst, track)
        if DEBUG and not add_rs:
            phase_barrier()
            p.dma("sp", dbg_xnT[:, :], xnT[:, :], sSt)
            for nm, tt in (("mu", st_mu), ("rstd", st_rstd), ("sum", st_sum),
                           ("ssq", st_ssq), ("var", st_var), ("ln", st_ln),
                           ("e2", st_e2), ("m2", st_m2)):
                pass
            p.dma("sp", dbg_stats[:, 0:NT], st_mu[:, :], sSt)
            p.dma("sp", dbg_stats[:, NT:2 * NT], st_rstd[:, :], sSt)
            p.dma("sp", dbg_stats[:, 2 * NT:3 * NT], st_sum[:, :], sSt)
            p.dma("sp", dbg_stats[:, 3 * NT:4 * NT], st_ssq[:, :], sSt)
            p.dma("sp", dbg_stats[:, 4 * NT:5 * NT], st_var[:, :], sSt)
            p.dma("sp", dbg_stats[:, 5 * NT:6 * NT], st_ln[:, :], sSt)
            p.dma("sp", dbg_xin[:, :], bufs["xin"][:, 0:2 * D], sSt)
            p.dma("sp", dbg_xn[:, :], bufs["xn_b"][:, :], sSt)
            p.dma("sp", dbg_hgT1[:, :], bufs["hgT1"][:, :], sSt)
            p.dma("sp", dbg_wblk[:, :], wblk[:, :], sSt)
            p.wait("sp", sSt, p.c(sSt))
        res = dict(row_stored=list(track["row_stored"]))
        ph.close()
        return res

    def attn_sublayer(prev):
        ph = ExitStack()
        psb = lambda n, s, d=F32: ph.enter_context(nc.sbuf_tensor(n, s, d))
        pps = lambda n, s, d=F32: ph.enter_context(nc.psum_tensor(n, s, d))
        bufs = dict(
            add_delta=True,
            xin=psb("a_xin", [128, 3 * D]),
            xn_b=psb("a_xn", [128, 2 * D], BF16),
            stb=psb("a_stb", [128, 4 * FB], BF16),
            battn_t=psb("a_battn", [128, 3 * D], BF16),
            tp_ps=[pps("a_tp0", [128, 1024], BF16), pps("a_tp1", [128, 1024], BF16)],
            acc=pps("a_acc", [128, NACC * 512]),
        )
        track = new_track()
        c_bcast = p.dma("pool", battn_bf[0:1, :], battn[0:1, :], sWd)
        p.wait("sp", sWd, c_bcast)
        track["c_battn"] = p.dma("sp", bufs["battn_t"][:, :],
                                 battn_bf[0:1, :].partition_broadcast(128).squeeze(1), sXd)
        for t in range(NT):
            p.wait("sp", sSt, prev["row_stored"][t])
            prepass_tile(bufs, t, x0, True, False, track)
        cc_cnt = weight_sweep(bufs, "qkv", None, None, track)
        p.wait("pool", sSt, cc_cnt)
        c_ag = p.op("pool", lambda E: E.collective_compute(
            "AllGather", ALU.bypass, replica_groups=groups,
            ins=[cc_in[:, :].opt()], outs=[cc_out[:, :].opt()]), sCc)
        qkv_all_stored = p.c(sSt)
        p.wait("sp", sSt, qkv_all_stored)
        p.wait("sp", sCc, p.c(sCc))

        for r_ in range(2):
            pred_dma(qkv_all[r_ * T:(r_ + 1) * T, :], qkv_own[:, :], sXd, r_, True)
            pr = 1 - r_
            pred_dma(qkv_all[pr * T:(pr + 1) * T, :], cc_out[pr * T:(pr + 1) * T, :],
                     sXd, r_, True)
        qkv_norm = p.c(sXd)
        ph.close()

        phase_barrier()
        ph2 = ExitStack()
        psb2 = lambda n, s, d=F32: ph2.enter_context(nc.sbuf_tensor(n, s, d))
        pps2 = lambda n, s, d=F32: ph2.enter_context(nc.psum_tensor(n, s, d))
        strip_m = psb2("c_stripm", [128, 2176])
        strip_h = psb2("c_strip", [128, 2176])
        c_stripm = p.dma("pool", strip_m[:, :], strip_i[:, :], sWd)
        qtok = psb2("c_qtok", [128, 2 * S], BF16)
        ktok = psb2("c_ktok", [128, 2 * S], BF16)
        vtok = psb2("c_vtok", [128, 2 * S], BF16)
        qfm = psb2("c_qfm", [128, S], BF16)
        kfm = psb2("c_kfm", [128, S], BF16)
        sbS = psb2("c_S", [128, 2 * S])
        p_b = psb2("c_pb", [128, 2 * S], BF16)
        pt_sb = psb2("c_pt", [128, 2 * S], BF16)
        l_all = psb2("c_l", [128, NQ])
        linv_all = psb2("c_linv", [128, NQ])
        negm = psb2("c_negm", [128, 2])
        lT_sb = psb2("c_lT", [16, 128])
        linvrow = psb2("c_lrow", [128, S])
        s_ps = pps2("c_sps", [128, 1024])
        av_ps = pps2("c_avps", [128, S])
        pt_ps = [pps2("c_ptps0", [128, 1024], BF16), pps2("c_ptps1", [128, 1024], BF16)]

        attnout = xnT
        tok_free = {0: 0, 1: 0}
        c_ptsb_free = {0: 0, 1: 0}
        c_ptsb_free_eng = {0: None, 1: None}
        av_evac_dve = 0
        c_lrow_prev = 0
        c_lt_prev = 0
        c_cc = p.c(sCc)

        for h in range(H):
            hb = h % 2
            p.wait("dve", sWd, c_stripm)
            p.opw("dve", lambda E, hh=h, sm=strip_m: E.tensor_scalar(
                strip_h[:, :], sm[:, :], slopes_s[:, hh:hh + 1], None,
                op0=ALU.mult), sDv)
            p.wait("sp", sXd, qkv_norm)
            p.wait("sp", sPe, tok_free[hb])
            for (xt, base) in ((qtok, 0), (ktok, S), (vtok, 2 * S)):
                col = base + h * 128
                dst3 = xt[:, hb * S:(hb + 1) * S].rearrange("p (j d) -> p j d", d=128)
                p.dma("sp", dst3,
                      qkv_all[:, col:col + 128].rearrange("(j p) d -> p j d", p=128), sXd)
            c_load = p.c(sXd)
            p.wait("pe", sXd, c_load)
            for (src, dstf) in ((qtok, qfm), (ktok, kfm)):
                for g in range(2):
                    bank = pt_ps[g % 2]
                    fe = c_ptsb_free_eng[g % 2]
                    if fe is not None:
                        p.wait("pe", fe, c_ptsb_free[g % 2])
                    for i in range(8):
                        jj = g * 8 + i
                        cpe = p.op("pe", lambda E, b=bank, ii=i, s_=src, j_=jj, hb_=hb:
                                   E.transpose(
                            b[:, ii * 128:(ii + 1) * 128],
                            s_[:, hb_ * S + j_ * 128: hb_ * S + (j_ + 1) * 128],
                            ident_b[:, :]), sPe)
                    p.wait("act", sPe, cpe)
                    cev = p.op("act", lambda E, b=bank, d_=dstf, g_=g: E.copy(
                        d_[:, g_ * 1024:(g_ + 1) * 1024], b[:, :]), sAc)
                    c_ptsb_free[g % 2] = cev
                    c_ptsb_free_eng[g % 2] = sAc
            c_qk_done = p.c(sAc)
            p.wait("pe", sAc, c_qk_done)

            c_comb = [[0] * 4 for _ in range(NQ)]
            c_av = [0] * NQ
            c_sb_exp = [0] * NQ
            c_pt_read = [0] * NQ

            def emit_S_and_combine(j):
                kmax = 128 * (j + 1)
                nkc = (kmax + 511) // 512
                if j >= 1:
                    p.wait("pe", sDv, c_comb[j - 1][(128 * j + 511) // 512 - 1])
                off = 2048 - j * 128
                if j >= 2:
                    p.wait("dve", sAc, c_sb_exp[j - 2])
                for kc in range(nkc):
                    n = min(512, kmax - kc * 512)
                    if kc >= 2:
                        p.wait("pe", sDv, c_comb[j][kc - 2])
                    cpe = p.op("pe", lambda E, j_=j, k_=kc, n_=n: E.matmul(
                        s_ps[:, (k_ % 2) * 512:(k_ % 2) * 512 + n_],
                        qfm[:, j_ * 128:(j_ + 1) * 128],
                        kfm[:, k_ * 512:k_ * 512 + n_], start=True, stop=True), sPe)
                    p.wait("dve", sPe, cpe)
                    c_comb[j][kc] = p.op("dve", lambda E, j_=j, k_=kc, n_=n, o=off:
                                         E.scalar_tensor_tensor(
                        sbS[:, (j_ % 2) * S + k_ * 512:(j_ % 2) * S + k_ * 512 + n_],
                        s_ps[:, (k_ % 2) * 512:(k_ % 2) * 512 + n_], SCALE,
                        strip_h[:, o + k_ * 512:o + k_ * 512 + n_],
                        op0=ALU.mult, op1=ALU.add), sDv)

            c_negm = [0] * NQ

            def emit_max_negm(j):
                kmax = 128 * (j + 1)
                if j >= 2:
                    p.wait("dve", sPe, c_pt_read[j - 2])
                p.wait("dve", sDv, c_comb[j][(kmax + 511) // 512 - 1])
                p.opw("dve", lambda E, j_=j, km=kmax: E.tensor_scalar(
                    p_b[:, (j_ % 2) * S:(j_ % 2) * S + km],
                    sbS[:, (j_ % 2) * S:(j_ % 2) * S + km], 1.0, MASKNEG,
                    op0=ALU.mult, op1=ALU.max, accum_out=l_all[:, j_:j_ + 1]), sDv)
                c_negm[j] = p.op("dve", lambda E, j_=j: E.tensor_scalar(
                    negm[:, j_ % 2:j_ % 2 + 1], l_all[:, j_:j_ + 1], -1.0, None,
                    op0=ALU.mult), sDv)

            def emit_exp(j):
                kmax = 128 * (j + 1)
                p.wait("act", sDv, c_negm[j])
                ce = p.op("act", lambda E, j_=j, km=kmax: E.activation(
                    p_b[:, (j_ % 2) * S:(j_ % 2) * S + km],
                    sbS[:, (j_ % 2) * S:(j_ % 2) * S + km], AF.Exp,
                    bias=negm[:, j_ % 2:j_ % 2 + 1],
                    accum_out=l_all[:, j_:j_ + 1]), sAc)
                c_sb_exp[j] = ce
                return ce

            def emit_pt_av(j, c_exp):
                nkt = j + 1
                p.wait("pe", sAc, c_exp)
                if h > 0 and j == 0:
                    p.wait("pe", sDv, av_evac_dve)
                ngr = (nkt + 7) // 8
                for g in range(ngr):
                    bank = pt_ps[g % 2]
                    fe = c_ptsb_free_eng[g % 2]
                    if fe is not None:
                        p.wait("pe", fe, c_ptsb_free[g % 2])
                    lo, hi = g * 8, min(nkt, g * 8 + 8)
                    for kt in range(lo, hi):
                        cpe = p.op("pe", lambda E, b=bank, k_=kt, j_=j: E.transpose(
                            b[:, (k_ % 8) * 128:(k_ % 8 + 1) * 128],
                            p_b[:, (j_ % 2) * S + k_ * 128:(j_ % 2) * S + (k_ + 1) * 128],
                            ident_b[:, :]), sPe)
                    p.wait("dve", sPe, cpe)
                    if j >= 2:
                        p.wait("dve", sPe, c_av[j - 2])
                    cev = p.op("dve", lambda E, b=bank, lo_=lo, hi_=hi, j_=j: E.tensor_copy(
                        pt_sb[:, (j_ % 2) * S + lo_ * 128:(j_ % 2) * S + hi_ * 128],
                        b[:, 0:(hi_ - lo_) * 128]), sDv)
                    c_ptsb_free[g % 2] = cev
                    c_ptsb_free_eng[g % 2] = sDv
                    p.wait("pe", sDv, cev)
                    for kt in range(lo, hi):
                        cav = p.op("pe", lambda E, k_=kt, j_=j, hb_=hb: E.matmul(
                            av_ps[:, j_ * 128:(j_ + 1) * 128],
                            vtok[:, hb_ * S + k_ * 128:hb_ * S + (k_ + 1) * 128],
                            pt_sb[:, (j_ % 2) * S + k_ * 128:(j_ % 2) * S + (k_ + 1) * 128],
                            start=(k_ == 0), stop=(k_ == j_),
                            skip_group_check=True), sPe)
                c_pt_read[j] = cpe
                c_av[j] = cav
                return cav

            emit_S_and_combine(0)
            emit_max_negm(0)
            for j in range(NQ):
                if j + 1 < NQ:
                    emit_S_and_combine(j + 1)
                    emit_max_negm(j + 1)
                ce = emit_exp(j)
                emit_pt_av(j, ce)
            tok_free[hb] = c_av[NQ - 1]

            p.wait("dve", sAc, c_sb_exp[NQ - 1])
            p.wait("dve", sPe, c_lt_prev)
            c_li = p.opw("dve", lambda E: E.reciprocal(linv_all[:, :], l_all[:, :]), sDv)
            p.wait("pe", sDv, c_li)
            p.wait("pe", sDv, c_comb[NQ - 1][3])
            p.wait("pe", sXd, c_identf)
            c_lt = p.op("pe", lambda E: E.transpose(
                s_ps[0:16, 0:128], linv_all[:, :], ident_f[:, :]), sPe)
            c_lt_prev = c_lt
            p.wait("act", sPe, c_lt)
            p.wait("act", sXd, c_lrow_prev)
            c_lte = p.op("act", lambda E: E.copy(lT_sb[:, :], s_ps[0:16, 0:128]), sAc)
            p.wait("sp", sAc, c_lte)
            c_lst = p.dma("sp", lrow_d[:, :], lT_sb[:, :], sSt)
            p.wait("sp", sSt, c_lst)
            p.wait("sp", sDv, av_evac_dve)
            c_lrow = p.dma("sp", linvrow[:, :].rearrange("p (j q) -> p j q", q=128),
                           lrow_d[:, :].partition_broadcast(128), sXd)
            c_lrow_prev = c_lrow
            p.wait("dve", sXd, c_lrow)
            p.wait("dve", sPe, c_av[NQ - 1])
            for g in range(2):
                cavd = p.op("dve", lambda E, g_=g, hh=h: E.tensor_tensor(
                    attnout[:, hh * S + g_ * 1024: hh * S + (g_ + 1) * 1024],
                    av_ps[:, g_ * 1024:(g_ + 1) * 1024],
                    linvrow[:, g_ * 1024:(g_ + 1) * 1024], op=ALU.mult), sDv)
            av_evac_dve = cavd

        ph2.close()

        if DEBUG:
            phase_barrier()
            p.dma("sp", dbg_attno[:, :], attnout[:, :], sSt)
            p.dma("sp", dbg_small[:, 0:NQ], l_all[:, :], sSt)
            p.dma("sp", dbg_small[:, NQ:2 * NQ], linv_all[:, :], sSt)
            p.dma("sp", dbg_small[:, 32:32 + NQ],
                  linvrow[:, :].rearrange("p (j q) -> p j q", q=128)[:, :, 0:1].squeeze(2),
                  sSt, allow_slow_non_contiguous=True)
            p.wait("sp", sSt, p.c(sSt))
        phase_barrier()
        ph3 = ExitStack()
        psb3 = lambda n, s, d=F32: ph3.enter_context(nc.sbuf_tensor(n, s, d))
        pps3 = lambda n, s, d=F32: ph3.enter_context(nc.psum_tensor(n, s, d))
        stp = psb3("p_stp", [128, 8 * FB])
        bprojh = psb3("p_bh", [128, D])
        acc3 = pps3("p_acc", [128, 4096])
        c_bh = 0
        for half in range(2):
            if half == 1:
                p.wait("sp", sDv, c_bh)
            cb = p.dma("sp", stp[:, 0:2048],
                       bproj[0:1, half * 2048:(half + 1) * 2048].partition_broadcast(
                           128).squeeze(1), sXd)
            p.wait("dve", sXd, cb)
            c_bh = p.op("dve", lambda E, hh=half: E.tensor_scalar(
                bprojh[:, hh * 2048:(hh + 1) * 2048], stp[:, 0:2048], 0.5, None,
                op0=ALU.mult), sDv)
        NBP = D // FB
        PFI = 16
        wfree = {0: 0, 1: 0}
        accfree = [0] * 8
        stfree = [0] * 4
        cnt = 0
        for bi in range(NBP):
            c0 = bi * FB
            wb = wblk[:, (bi % 2) * WBW:(bi % 2) * WBW + PFI * FB]
            p.wait("pool", sPe, wfree[bi % 2])
            cw = p.dma("pool", wb.rearrange("p (c f) -> p c f", c=PFI),
                       wproj[:, c0:c0 + FB].rearrange("(c p) f -> p c f", p=128), sWd)
            for qj in range(NQ):
                reg = acc3[:, (qj % 8) * 512:(qj % 8) * 512 + FB]
                p.wait("pe", sWd, cw)
                p.wait("pe", sDv, accfree[qj % 8])
                for fi in range(PFI):
                    lhsT = attnout[:, (fi * NQ + qj) * 128:(fi * NQ + qj + 1) * 128]
                    c_mm = p.op("pe", lambda E, rg=reg, l=lhsT, w=wb, f=fi: E.matmul(
                        rg, l, w[:, f * FB:(f + 1) * FB], start=(f == 0),
                        stop=(f == PFI - 1), skip_group_check=True), sPe)
                wfree[bi % 2] = c_mm
                slot = cnt % 4
                cnt += 1
                p.wait("dve", sPe, c_mm)
                p.wait("dve", sDv, c_bh)
                p.wait("dve", sSt, stfree[slot])
                cev = p.op("dve", lambda E, s_=slot, rg=reg, cc=c0: E.scalar_tensor_tensor(
                    stp[:, s_ * FB:(s_ + 1) * FB], rg, 1.0, bprojh[:, cc:cc + FB],
                    op0=ALU.mult, op1=ALU.add), sDv)
                accfree[qj % 8] = cev
                p.wait("sp", sDv, cev)
                rsd = rs_in[c0 // 2048]
                cc2 = c0 % 2048
                cst = p.dma("sp", rsd[qj * 128:(qj + 1) * 128, cc2:cc2 + FB],
                            stp[:, slot * FB:(slot + 1) * FB], sSt)
                stfree[slot] = cst
        c_all_st = p.c(sSt)
        if DEBUG:
            p.wait("sp", sSt, c_all_st)
            p.dma("sp", dbg_rsina[:, :], rs_in[0][:, :], sSt)
            p.dma("sp", dbg_rsinb[:, :], rs_in[1][:, :], sSt)
            c_all_st = p.c(sSt)
        p.wait("pool", sSt, c_all_st)
        for half in range(2):
            p.op("pool", lambda E, hh=half: E.collective_compute(
                "ReduceScatter", ALU.add, replica_groups=groups,
                ins=[rs_in[hh][:, :].opt()],
                outs=[rs_out[hh][:, :].opt()]), sCc)
        c_rs = p.c(sCc)
        ph3.close()
        return dict(c_rs=c_rs)

    d1 = ffn_sublayer(x0, delta1, add_rs=False, delta_mode=True)
    phase_barrier()
    if DEBUG:
        p.dma("sp", dbg_xres1[:, :], x_res[:, :], sSt)
    at = attn_sublayer(d1)
    phase_barrier()
    p.wait("sp", sCc, at["c_rs"])
    if DEBUG:
        p.dma("sp", dbg_qkvall[:, :], qkv_all[:, :], sSt)
        p.dma("sp", dbg_rsa[:, :], rs_out[0][:, :], sSt)
        p.dma("sp", dbg_rsb[:, :], rs_out[1][:, :], sSt)
    ffn_sublayer(x_res, y, add_rs=True)
    if DEBUG:
        p.dma("sp", dbg_xres2[:, :], x_res[:, :], sSt)
    p.wait("sp", sSt, p.c(sSt))

    with nc.Block() as block:
        @block.tensor
        def _(E):
            for f in p.q["pe"]:
                f(E)

        @block.scalar
        def _(E):
            for f in p.q["act"]:
                f(E)

        @block.vector
        def _(E):
            for f in p.q["dve"]:
                f(E)

        @block.gpsimd
        def _(E):
            for f in p.q["pool"]:
                f(E)

        @block.sync
        def _(E):
            for f in p.q["sp"]:
                f(E)

    es.close()
    return nc


def _host_inputs(L0_attn_hidden_states, L0_attn_w_attn, L0_attn_b_attn, L0_attn_w_proj,
                 L0_attn_b_proj, L0_ffn_base_W, L0_ffn_gate_W, L0_ffn_lora_A, L0_ffn_lora_B):
    x = np.ascontiguousarray(np.asarray(L0_attn_hidden_states, dtype=np.float32))
    w_attn = np.asarray(L0_attn_w_attn, dtype=np.float32)
    b_attn = np.asarray(L0_attn_b_attn, dtype=np.float32)
    w_proj = np.asarray(L0_attn_w_proj, dtype=np.float32)
    b_proj = np.asarray(L0_attn_b_proj, dtype=np.float32)
    base_W = np.asarray(L0_ffn_base_W, dtype=np.float32)
    gate_W = np.asarray(L0_ffn_gate_W, dtype=np.float32)
    lora_A = np.asarray(L0_ffn_lora_A, dtype=np.float32)
    lora_B = np.asarray(L0_ffn_lora_B, dtype=np.float32)

    bwt = np.ascontiguousarray(base_W.T)
    gwt = np.ascontiguousarray(gate_W.T)
    lat = np.ascontiguousarray(lora_A.reshape(ER, D).T)
    lbp = np.ascontiguousarray(lora_B.transpose(0, 2, 1).reshape(ER, D))

    u = np.arange(2176, dtype=np.float32)[None, :]
    i = np.arange(128, dtype=np.float32)[:, None]
    rel = u - 2048.0 - i
    strip = np.where(rel <= 0, rel, np.float32(MASKNEG)).astype(np.float32)
    ident = np.eye(128, dtype=np.float32)
    slopes_full = _alibi_slopes(32)

    in_maps = []
    for c in range(8):
        b, r = c // 2, c % 2
        my = np.arange(r * 16, r * 16 + 16)
        peer = np.arange((1 - r) * 16, (1 - r) * 16 + 16)

        def cols(heads):
            qs = np.concatenate([np.arange(h * 128, (h + 1) * 128) for h in heads])
            return np.concatenate([qs, D + qs, 2 * D + qs])

        perm = np.concatenate([cols(my), cols(peer)])
        in_maps.append(dict(
            x0=np.ascontiguousarray(x[b, r * T:(r + 1) * T, :]),
            wattn=np.ascontiguousarray(w_attn[:, perm]),
            battn=np.ascontiguousarray(b_attn[perm])[None, :],
            wproj=np.ascontiguousarray(w_proj[r * 2048:(r + 1) * 2048, :]),
            bproj=b_proj[None, :].copy(),
            bwt=bwt, gwt=gwt, lat=lat, lbp=lbp,
            slopes_i=np.ascontiguousarray(np.broadcast_to(
                slopes_full[my][None, :], (128, 16))),
            strip_i=strip, ident_i=ident,
            rank_i=np.array([[r]], dtype=np.int32),
        ))
    return in_maps


_NC_CACHE = {}


def kernel(**inputs):
    in_maps = _host_inputs(**inputs)
    if "nc" not in _NC_CACHE:
        _NC_CACHE["nc"] = build_nc()
    nc = _NC_CACHE["nc"]
    res = run_bass_kernel_spmd(nc, in_maps, core_ids=list(range(8)))
    out = np.empty((4, 2048, 4096), dtype=np.float32)
    for c in range(8):
        b, r = c // 2, c % 2
        out[b, r * T:(r + 1) * T, :] = res.results[c]["y"]
    return out


# version 07
# speedup vs baseline: 1.0117x; 1.0117x over previous
import numpy as np
import math
import os
from contextlib import ExitStack
DEBUG = bool(int(os.environ.get("MK_DEBUG", "0")))

import concourse.bass as bass
import concourse.mybir as mybir
from concourse.bass_utils import run_bass_kernel_spmd

F32 = mybir.dt.float32
BF16 = mybir.dt.bfloat16
I32 = mybir.dt.int32
AF = mybir.ActivationFunctionType
ALU = mybir.AluOpType
AX = mybir.AxisListType

D = 4096
T = 1024
NT = 8
S = 2048
NQ = 16
H = 16
FI = 32
NE = 28
ER = 224
SCALE = 1.0 / 128.0
EPS = 1e-5
MASKNEG = -1e30
FB = 256
NACC = 6

ENGS = ("pe", "act", "dve", "pool", "sp")


def _alibi_slopes(n_heads):
    cp2 = 2 ** math.floor(math.log2(n_heads))
    base = 2 ** (-(2 ** (-(math.log2(cp2) - 3))))
    slopes = base ** np.arange(1, cp2 + 1, dtype=np.float32)
    if cp2 != n_heads:
        extra_base = 2 ** (-(2 ** (-(math.log2(2 * cp2) - 3))))
        n_rem = min(cp2, n_heads - cp2)
        extra = extra_base ** np.arange(1, 1 + 2 * n_rem, 2, dtype=np.float32)
        slopes = np.concatenate([slopes, extra])
    return slopes.astype(np.float32)


class P:
    def __init__(self, nc):
        self.nc = nc
        self.q = {e: [] for e in ENGS}
        self.cnt = {}

    def emit(self, eng, fn):
        self.q[eng].append(fn)

    def wait(self, eng, sem, val):
        if val and val > 0:
            self.emit(eng, lambda E, s=sem, v=val: E.wait_ge(s, v))

    def op(self, eng, fn, sem=None, n=1):
        if sem is None:
            self.emit(eng, fn)
            return None
        self.cnt[sem] = self.cnt.get(sem, 0) + n
        c = self.cnt[sem]
        self.emit(eng, lambda E, f=fn, s=sem, k=n: f(E).then_inc(s, k))
        return c

    def dma(self, eng, out, in_, sem, **kw):
        return self.op(eng, lambda E, o=out, i=in_, k=dict(kw): E.dma_start(out=o, in_=i, **k),
                       sem=sem, n=16)

    def c(self, sem):
        return self.cnt.get(sem, 0)

    def opw(self, eng, fn, sem):
        c = self.op(eng, fn, sem)
        self.wait(eng, sem, c)
        return c


def build_nc():
    nc = bass.Bass("TRN2", target_bir_lowering=False)

    x0 = nc.dram_tensor("x0", [T, D], F32, kind="ExternalInput")
    wattn = nc.dram_tensor("wattn", [D, 3 * D], F32, kind="ExternalInput")
    battn = nc.dram_tensor("battn", [1, 3 * D], F32, kind="ExternalInput")
    wproj = nc.dram_tensor("wproj", [2048, D], F32, kind="ExternalInput")
    bproj = nc.dram_tensor("bproj", [1, D], F32, kind="ExternalInput")
    bwt = nc.dram_tensor("bwt", [D, D], F32, kind="ExternalInput")
    gwt = nc.dram_tensor("gwt", [D, NE], F32, kind="ExternalInput")
    lat = nc.dram_tensor("lat", [D, ER], F32, kind="ExternalInput")
    lbp = nc.dram_tensor("lbp", [ER, D], F32, kind="ExternalInput")
    slopes_i = nc.dram_tensor("slopes_i", [128, H], F32, kind="ExternalInput")
    strip_i = nc.dram_tensor("strip_i", [128, 2176], F32, kind="ExternalInput")
    ident_i = nc.dram_tensor("ident_i", [128, 128], F32, kind="ExternalInput")
    rank_i = nc.dram_tensor("rank_i", [1, 1], I32, kind="ExternalInput")
    y = nc.dram_tensor("y", [T, D], F32, kind="ExternalOutput")

    if DEBUG:
        dbg_xres1 = nc.dram_tensor("dbg_xres1", [T, D], F32, kind="ExternalOutput")
        dbg_xnT = nc.dram_tensor("dbg_xnT", [128, NT * FI * 128], BF16, kind="ExternalOutput")
        dbg_stats = nc.dram_tensor("dbg_stats", [128, 8 * NT], F32, kind="ExternalOutput")
        dbg_xin = nc.dram_tensor("dbg_xin", [128, 2 * D], F32, kind="ExternalOutput")
        dbg_xn = nc.dram_tensor("dbg_xn", [128, 2 * D], BF16, kind="ExternalOutput")
        dbg_hgT1 = nc.dram_tensor("dbg_hgT1", [128, NT * 128], BF16, kind="ExternalOutput")
        dbg_wblk = nc.dram_tensor("dbg_wblk", [128, 2 * FI * FB], BF16, kind="ExternalOutput")
        dbg_qkvall = nc.dram_tensor("dbg_qkvall", [S, 6144], BF16, kind="ExternalOutput")
        dbg_rsa = nc.dram_tensor("dbg_rsa", [T, 2048], F32, kind="ExternalOutput")
        dbg_rsb = nc.dram_tensor("dbg_rsb", [T, 2048], F32, kind="ExternalOutput")
        dbg_xres2 = nc.dram_tensor("dbg_xres2", [T, D], F32, kind="ExternalOutput")
        dbg_attno = nc.dram_tensor("dbg_attno", [128, NT * FI * 128], BF16, kind="ExternalOutput")
        dbg_rsina = nc.dram_tensor("dbg_rsina", [S, 2048], F32, kind="ExternalOutput")
        dbg_rsinb = nc.dram_tensor("dbg_rsinb", [S, 2048], F32, kind="ExternalOutput")
        dbg_small = nc.dram_tensor("dbg_small", [128, 64], F32, kind="ExternalOutput")
    x_res = nc.dram_tensor("x_res", [T, D], F32)
    qkv_own = nc.dram_tensor("qkv_own", [T, 6144], BF16)
    cc_in = nc.dram_tensor("cc_in", [T, 6144], BF16)
    cc_out = nc.dram_tensor("cc_out", [S, 6144], BF16)
    delta1 = nc.dram_tensor("delta1", [T, D], F32)
    rs_in = [nc.dram_tensor("rs_in_a", [S, 2048], F32),
             nc.dram_tensor("rs_in_b", [S, 2048], F32)]
    rs_out = [nc.dram_tensor("rs_out_a", [T, 2048], F32),
              nc.dram_tensor("rs_out_b", [T, 2048], F32)]
    battn_bf = nc.dram_tensor("battn_bf", [1, 3 * D], BF16)
    lrow_d = nc.dram_tensor("lrow_d", [16, 128], F32)
    qkv_all = nc.dram_tensor("qkv_all", [S, 6144], BF16)

    groups = [[0, 1], [2, 3], [4, 5], [6, 7]]

    es = ExitStack()
    sb = lambda n, s, d=F32: es.enter_context(nc.sbuf_tensor(n, s, d))
    sem = lambda n: es.enter_context(nc.semaphore(n))

    sWd = sem("sWd")
    sXd = sem("sXd")
    sSt = sem("sSt")
    sPe = sem("sPe")
    sAc = sem("sAc")
    sDv = sem("sDv")
    sCc = sem("sCc")
    sPd = sem("sPd")
    sXo = sem("sXo")
    ALLSEMS = (sWd, sXd, sSt, sPe, sAc, sDv, sCc, sPd, sXo)

    p = P(nc)

    def phase_barrier(skip=()):
        vals = [(s, p.c(s)) for s in ALLSEMS if s not in skip]
        for eng in ENGS:
            for s, v in vals:
                p.wait(eng, s, v)

    ident_b = sb("ident_b", [128, 128], BF16)
    ident_f = sb("ident_f", [128, 128], F32)
    slopes_s = sb("slopes_s", [128, H])
    xnT = sb("xnT", [128, NT * FI * 128], BF16)
    wblk = sb("wblk", [128, 2 * FI * FB], BF16)
    st_sum = sb("st_sum", [128, NT])
    st_ssq = sb("st_ssq", [128, NT])
    st_mu = sb("st_mu", [128, NT])
    st_e2 = sb("st_e2", [128, NT])
    st_m2 = sb("st_m2", [128, NT])
    st_var = sb("st_var", [128, NT])
    st_ln = sb("st_ln", [128, NT])
    st_rstd = sb("st_rstd", [128, NT])

    WBW = FI * FB

    p.dma("pool", ident_b[:, :], ident_i[:, :], sWd)
    p.dma("pool", slopes_s[:, :], slopes_i[:, :], sWd)
    c_const = p.c(sWd)
    c_identf = p.dma("sp", ident_f[:, :], ident_i[:, :], sXd)

    regctx = nc.sync.register(name="rankreg")
    rreg = regctx.__enter__()
    rv_holder = {}

    def _loadrank(E):
        E.reg_load(rreg, rank_i[0:1, 0:1])
        rv_holder["rv"] = E.snap(rreg, min_val=0, max_val=1)
        return E.nop()

    p.emit("sp", _loadrank)
    p.wait("pe", sWd, c_const)

    def pred_dma(out, in_, sem_, half, eq):
        p.cnt[sem_] = p.cnt.get(sem_, 0) + 16
        c = p.cnt[sem_]

        def f(E, o=out, i=in_, s=sem_, h=half, e=eq):
            rv = rv_holder["rv"]
            cond = (rv == h) if e else (rv != h)
            return E.dma_start(out=o, in_=i, cond=cond).then_inc(s, 16)

        p.emit("sp", f)
        return c

    def prepass_tile(bufs, t, x_src, add_rs, ffn, track):
        xin, xn_b = bufs["xin"], bufs["xn_b"]
        tp_ps = bufs["tp_ps"]
        slot = t % 2
        xsl = xin[:, slot * D:(slot + 1) * D]
        xnsl = xn_b[:, slot * D:(slot + 1) * D]
        p.wait("sp", sDv, track["slot_free"][slot])
        cx = p.dma("sp", xsl, x_src[t * 128:(t + 1) * 128, :], sXd)
        if add_rs:
            rsl = xin[:, 2 * D:3 * D]
            p.wait("sp", sDv, track["rs_free"])
            if bufs.get("add_delta"):
                cr = p.dma("sp", rsl, delta1[t * 128:(t + 1) * 128, :], sXd)
            else:
                p.dma("sp", xin[:, 2 * D:2 * D + 2048],
                      rs_out[0][t * 128:(t + 1) * 128, :], sXd)
                cr = p.dma("sp", xin[:, 2 * D + 2048:3 * D],
                           rs_out[1][t * 128:(t + 1) * 128, :], sXd)
            p.wait("dve", sXd, cr)
            c_add = p.opw("dve", lambda E, xs=xsl, rs=rsl, tt=t: E.scalar_tensor_tensor(
                xs, xs, 1.0, rs, op0=ALU.mult, op1=ALU.add,
                accum_out=st_sum[:, tt:tt + 1]), sDv)
            track["rs_free"] = c_add
            p.wait("sp", sDv, c_add)
            cst = p.dma("sp", x_res[t * 128:(t + 1) * 128, :], xsl, sSt)
            track["xres_st"][t] = cst
        else:
            p.wait("dve", sXd, cx)
            c_add = p.op("dve", lambda E, xs=xsl, tt=t: E.reduce_sum(
                st_sum[:, tt:tt + 1], xs, axis=AX.X), sDv)
        p.wait("dve", sPe, track["xn_free"][slot])
        p.opw("dve", lambda E, xs=xsl, o=xnsl, tt=t: E.scalar_tensor_tensor(
            o, xs, 1.0, xs, op0=ALU.mult, op1=ALU.mult,
            accum_out=st_ssq[:, tt:tt + 1]), sDv)
        p.opw("dve", lambda E, tt=t: E.tensor_scalar(
            st_mu[:, tt:tt + 1], st_sum[:, tt:tt + 1], 1.0 / D, None, op0=ALU.mult), sDv)
        p.opw("dve", lambda E, tt=t: E.tensor_scalar(
            st_e2[:, tt:tt + 1], st_ssq[:, tt:tt + 1], 1.0 / D, None, op0=ALU.mult), sDv)
        p.opw("dve", lambda E, tt=t: E.tensor_tensor(
            st_m2[:, tt:tt + 1], st_mu[:, tt:tt + 1], st_mu[:, tt:tt + 1], op=ALU.mult), sDv)
        p.opw("dve", lambda E, tt=t: E.scalar_tensor_tensor(
            st_var[:, tt:tt + 1], st_m2[:, tt:tt + 1], -1.0, st_e2[:, tt:tt + 1],
            op0=ALU.mult, op1=ALU.add), sDv)
        c_var = p.opw("dve", lambda E, tt=t: E.tensor_scalar(
            st_var[:, tt:tt + 1], st_var[:, tt:tt + 1], 1.0, EPS,
            op0=ALU.mult, op1=ALU.add), sDv)
        p.wait("act", sDv, c_var)
        p.opw("act", lambda E, tt=t: E.activation(
            st_ln[:, tt:tt + 1], st_var[:, tt:tt + 1], AF.Ln), sAc)
        c_rstd = p.op("act", lambda E, tt=t: E.activation(
            st_rstd[:, tt:tt + 1], st_ln[:, tt:tt + 1], AF.Exp, scale=-0.5), sAc)
        p.wait("dve", sAc, c_rstd)
        c_xn = p.op("dve", lambda E, xs=xsl, o=xnsl, tt=t: E.tensor_scalar(
            o, xs, st_mu[:, tt:tt + 1], st_rstd[:, tt:tt + 1],
            op0=ALU.subtract, op1=ALU.mult), sDv)
        track["slot_free"][slot] = c_xn
        p.wait("pe", sDv, c_xn)
        last_ev = 0
        for g in range(4):
            bank = tp_ps[g % 2]
            p.wait("pe", sAc, track["tp_free"][g % 2])
            for i in range(8):
                fi = g * 8 + i
                cpe = p.op("pe", lambda E, b=bank, ii=i, f=fi, o=xnsl: E.transpose(
                    b[:, ii * 128:(ii + 1) * 128], o[:, f * 128:(f + 1) * 128],
                    ident_b[:, :]), sPe)
            p.wait("act", sPe, cpe)
            cev = p.op("act", lambda E, b=bank, tt=t, g_=g: E.copy(
                xnT[:, (tt * FI + g_ * 8) * 128:(tt * FI + g_ * 8 + 8) * 128], b[:, :]), sAc)
            track["tp_free"][g % 2] = cev
            last_ev = cev
        track["xn_free"][slot] = cpe
        track["xnT_done"][t] = last_ev
        if not ffn:
            return
        latb, gwtb, hg = bufs["latb"], bufs["gwtb"], bufs["hg"]
        hgacc = bufs["acc"][:, 0:ER + NE]
        p.wait("pe", sAc, last_ev)
        p.wait("pe", sDv, track["hg_free"])
        p.wait("pe", sWd, track["c_lora"])
        for fi in range(FI):
            lhsT = xnT[:, (t * FI + fi) * 128:(t * FI + fi + 1) * 128]
            p.op("pe", lambda E, l=lhsT, f=fi, a=hgacc: E.matmul(
                a[:, 0:ER], l, latb[:, f * ER:(f + 1) * ER],
                start=(f == 0), stop=(f == FI - 1), skip_group_check=True), sPe)
            c_hg = p.op("pe", lambda E, l=lhsT, f=fi, a=hgacc: E.matmul(
                a[:, ER:ER + NE], l, gwtb[:, f * NE:(f + 1) * NE],
                start=(f == 0), stop=(f == FI - 1), skip_group_check=True), sPe)
        ga, m1, m2c = bufs["ga"], bufs["m1"], bufs["m2"]
        p.wait("dve", sPe, c_hg)
        p.opw("dve", lambda E, a=hgacc: E.reduce_max(m1[:, 0:1], a[:, ER:ER + NE], axis=AX.X), sDv)
        p.opw("dve", lambda E, a=hgacc: E.tensor_scalar(
            ga[:, 0:NE], a[:, ER:ER + NE], m1[:, 0:1], None, op0=ALU.is_ge), sDv)
        p.opw("dve", lambda E, a=hgacc: E.scalar_tensor_tensor(
            ga[:, 32:32 + NE], ga[:, 0:NE], MASKNEG, a[:, ER:ER + NE],
            op0=ALU.mult, op1=ALU.add), sDv)
        p.opw("dve", lambda E: E.reduce_max(m2c[:, 0:1], ga[:, 32:32 + NE], axis=AX.X), sDv)
        p.opw("dve", lambda E: E.tensor_scalar(
            ga[:, 32:32 + NE], ga[:, 32:32 + NE], m2c[:, 0:1], None, op0=ALU.is_ge), sDv)
        c_d = p.opw("dve", lambda E: E.tensor_sub(m2c[:, 1:2], m1[:, 0:1], m2c[:, 0:1]), sDv)
        p.wait("act", sDv, c_d)
        c_e = p.op("act", lambda E: E.activation(m1[:, 1:2], m2c[:, 1:2], AF.Exp,
                                                 scale=-1.0), sAc)
        p.wait("dve", sAc, c_e)
        p.opw("dve", lambda E: E.tensor_scalar(m1[:, 2:3], m1[:, 1:2], 1.0, 1.0,
                                               op0=ALU.mult, op1=ALU.add), sDv)
        p.opw("dve", lambda E: E.reciprocal(m2c[:, 2:3], m1[:, 2:3]), sDv)
        p.opw("dve", lambda E: E.tensor_scalar(m1[:, 3:4], m2c[:, 2:3], 2.0, None,
                                               op0=ALU.mult), sDv)
        p.opw("dve", lambda E: E.tensor_tensor(m2c[:, 3:4], m1[:, 1:2], m1[:, 3:4],
                                               op=ALU.mult), sDv)
        p.opw("dve", lambda E: E.tensor_scalar(ga[:, 0:NE], ga[:, 0:NE], m1[:, 3:4], None,
                                               op0=ALU.mult), sDv)
        p.opw("dve", lambda E: E.scalar_tensor_tensor(
            ga[:, 0:NE], ga[:, 32:32 + NE], m2c[:, 3:4], ga[:, 0:NE],
            op0=ALU.mult, op1=ALU.add), sDv)
        gexp = ga[:, 0:NE].unsqueeze(2).broadcast_to([128, NE, 8])
        c_hgd = p.op("dve", lambda E, a=hgacc, ge=gexp: E.tensor_tensor(
            hg[:, :].rearrange("p (e r) -> p e r", r=8),
            a[:, 0:ER].rearrange("p (e r) -> p e r", r=8), ge, op=ALU.mult), sDv)
        track["hg_free"] = c_hgd
        hgT1, hgT2 = bufs["hgT1"], bufs["hgT2"]
        p.wait("pe", sDv, c_hgd)
        bank = tp_ps[0]
        p.wait("pe", sAc, track["tp_free"][0])
        p.op("pe", lambda E, b=bank: E.transpose(
            b[:, 0:128], hg[:, 0:128], ident_b[:, :]), sPe)
        c_t2 = p.op("pe", lambda E, b=bank: E.transpose(
            b[0:96, 128:256], hg[:, 128:224], ident_b[:, :]), sPe)
        p.wait("act", sPe, c_t2)
        p.op("act", lambda E, b=bank, tt=t: E.copy(
            hgT1[:, tt * 128:(tt + 1) * 128], b[:, 0:128]), sAc)
        c_ev2 = p.op("act", lambda E, b=bank, tt=t: E.copy(
            hgT2[:, tt * 128:(tt + 1) * 128], b[0:96, 128:256]), sAc)
        track["tp_free"][0] = c_ev2
        track["hgT_done"][t] = c_ev2

    def weight_sweep(bufs, phase, x_src, x_dst, track):
        acc = bufs["acc"]
        if phase == "qkv":
            nb_b = 6144 // FB
            order = list(range(nb_b, 2 * nb_b)) + list(range(nb_b))
            wsrc = wattn
        else:
            order = list(range(D // FB))
            wsrc = bwt
        stf, xold, stb = bufs.get("stf"), bufs.get("xold"), bufs.get("stb")
        battn_t = bufs.get("battn_t")
        hgT1, hgT2 = bufs.get("hgT1"), bufs.get("hgT2")
        lbp1, lbp2 = bufs.get("lbp1"), bufs.get("lbp2")
        p.wait("pe", sAc, max(track["xnT_done"]))
        if phase.startswith("ffn"):
            p.wait("pe", sAc, max(track["hgT_done"]))
        else:
            p.wait("dve", sXd, track["c_battn"])
        nslot = 4
        cnt = 0
        cc_store = None
        for bi, blk in enumerate(order):
            c0 = blk * FB
            wb = wblk[:, (bi % 2) * WBW:(bi % 2) * WBW + WBW]
            p.wait("pool", sPe, track["wblk_free"][bi % 2])
            cw = p.dma("pool", wb.rearrange("p (c f) -> p c f", c=FI),
                       wsrc[:, c0:c0 + FB].rearrange("(c p) f -> p c f", p=128), sWd)
            accsem = sAc if phase == "ffn_delta" else sDv
            for t in range(NT):
                r = t % NACC
                reg = acc[:, r * 512:r * 512 + FB]
                p.wait("pe", sWd, cw)
                p.wait("pe", accsem, track["acc_free"][r])
                for fi in range(FI):
                    lhsT = xnT[:, (t * FI + fi) * 128:(t * FI + fi + 1) * 128]
                    st = (fi == 0)
                    last = (fi == FI - 1) and not phase.startswith("ffn")
                    c_mm = p.op("pe", lambda E, rg=reg, l=lhsT, w=wb, f=fi, s=st, e=last:
                                E.matmul(rg, l, w[:, f * FB:(f + 1) * FB], start=s, stop=e,
                                         skip_group_check=True), sPe)
                if phase.startswith("ffn"):
                    p.op("pe", lambda E, rg=reg, tt=t, cc=c0: E.matmul(
                        rg, hgT1[:, tt * 128:(tt + 1) * 128], lbp1[:, cc:cc + FB],
                        start=False, stop=False, skip_group_check=True), sPe)
                    c_mm = p.op("pe", lambda E, rg=reg, tt=t, cc=c0: E.matmul(
                        rg, hgT2[:, tt * 128:(tt + 1) * 128], lbp2[:, cc:cc + FB],
                        start=False, stop=True, skip_group_check=True), sPe)
                track["wblk_free"][bi % 2] = c_mm
                slot = cnt % nslot
                cnt += 1
                if phase == "ffn_delta":
                    ssl = stf[:, slot * FB:(slot + 1) * FB]
                    p.wait("act", sPe, c_mm)
                    p.wait("act", sSt, track["stf_free"][slot])
                    cev = p.op("act", lambda E, ss=ssl, rg=reg: E.activation(
                        ss, rg, AF.Copy, scale=0.5), sAc)
                    track["acc_free"][r] = cev
                    p.wait("sp", sAc, cev)
                    cst = p.dma("sp", x_dst[t * 128:(t + 1) * 128, c0:c0 + FB], ssl, sSt)
                    track["stf_free"][slot] = cst
                    track["row_stored"][t] = cst
                elif phase == "ffn":
                    xsl = xold[:, slot * FB:(slot + 1) * FB]
                    p.wait("sp", sDv, track["xold_free"][slot])
                    p.wait("sp", sSt, track["xres_st"][t])
                    cxo = p.dma("sp", xsl, x_src[t * 128:(t + 1) * 128, c0:c0 + FB], sXd)
                    ssl = stf[:, slot * FB:(slot + 1) * FB]
                    p.wait("dve", sPe, c_mm)
                    p.wait("dve", sXd, cxo)
                    p.wait("dve", sSt, track["stf_free"][slot])
                    cev = p.op("dve", lambda E, ss=ssl, rg=reg, xs=xsl: E.scalar_tensor_tensor(
                        ss, rg, 0.5, xs, op0=ALU.mult, op1=ALU.add), sDv)
                    track["acc_free"][r] = cev
                    track["xold_free"][slot] = cev
                    p.wait("sp", sDv, cev)
                    cst = p.dma("sp", x_dst[t * 128:(t + 1) * 128, c0:c0 + FB], ssl, sSt)
                    track["stf_free"][slot] = cst
                    track["row_stored"][t] = cst
                else:
                    ssl = stb[:, slot * FB:(slot + 1) * FB]
                    p.wait("dve", sPe, c_mm)
                    p.wait("dve", sSt, track["stf_free"][slot])
                    cev = p.op("dve", lambda E, ss=ssl, rg=reg, cc=c0: E.scalar_tensor_tensor(
                        ss, rg, 1.0, battn_t[:, cc:cc + FB], op0=ALU.mult, op1=ALU.add), sDv)
                    track["acc_free"][r] = cev
                    p.wait("sp", sDv, cev)
                    if c0 < 6144:
                        dst = qkv_own[t * 128:(t + 1) * 128, c0:c0 + FB]
                    else:
                        dst = cc_in[t * 128:(t + 1) * 128, c0 - 6144:c0 - 6144 + FB]
                    cst = p.dma("sp", dst, ssl, sSt)
                    track["stf_free"][slot] = cst
            if phase == "qkv" and bi == nb_b - 1:
                cc_store = p.c(sSt)
        return cc_store

    def new_track(c_lora=0, c_battn=0, acc_init=0):
        return dict(slot_free={0: 0, 1: 0}, xn_free={0: 0, 1: 0}, tp_free={0: 0, 1: 0},
                    xnT_done=[0] * NT, hgT_done=[0] * NT, hg_free=0,
                    wblk_free={0: 0, 1: 0}, acc_free=[acc_init] * NACC,
                    xold_free=[0] * 4, stf_free=[0] * 4,
                    row_stored=[0] * NT, xres_st=[0] * NT, rs_free=0,
                    c_battn=c_battn, c_lora=c_lora)

    def ffn_sublayer(x_src, x_dst, add_rs, delta_mode=False):
        ph = ExitStack()
        psb = lambda n, s, d=F32: ph.enter_context(nc.sbuf_tensor(n, s, d))
        pps = lambda n, s, d=F32: ph.enter_context(nc.psum_tensor(n, s, d))
        sfx = "2" if add_rs else "1"
        nxin = 3 if add_rs else 2
        bufs = dict(
            xin=psb("f_xin" + sfx, [128, nxin * D]),
            xn_b=psb("f_xn" + sfx, [128, 2 * D], BF16),
            stf=psb("f_stf" + sfx, [128, 4 * FB]),
            xold=psb("f_xold" + sfx, [128, 4 * FB]),
            latb=psb("f_latb" + sfx, [128, FI * ER], BF16),
            gwtb=psb("f_gwtb" + sfx, [128, FI * NE], BF16),
            lbp1=psb("f_lbp1" + sfx, [128, D], BF16),
            lbp2=psb("f_lbp2" + sfx, [96, D], BF16),
            hgT1=psb("f_hgT1" + sfx, [128, NT * 128], BF16),
            hgT2=psb("f_hgT2" + sfx, [96, NT * 128], BF16),
            hg=psb("f_hg" + sfx, [128, ER], BF16),
            ga=psb("f_ga" + sfx, [128, 64]),
            m1=psb("f_m1" + sfx, [128, 4]),
            m2=psb("f_m2" + sfx, [128, 4]),
            tp_ps=[pps("f_tp0" + sfx, [128, 1024], BF16),
                   pps("f_tp1" + sfx, [128, 1024], BF16)],
            acc=pps("f_acc" + sfx, [128, NACC * 512]),
        )
        track = new_track()
        p.dma("pool", bufs["latb"][:, :].rearrange("p (c e) -> p c e", c=FI),
              lat[:, :].rearrange("(c p) e -> p c e", p=128), sWd)
        p.dma("pool", bufs["gwtb"][:, :].rearrange("p (c e) -> p c e", c=FI),
              gwt[:, :].rearrange("(c p) e -> p c e", p=128), sWd)
        p.dma("pool", bufs["lbp1"][:, :], lbp[0:128, :], sWd)
        track["c_lora"] = p.dma("pool", bufs["lbp2"][:, :], lbp[128:224, :], sWd)
        for t in range(NT):
            prepass_tile(bufs, t, x_src, add_rs, True, track)
        track["acc_free"] = [track["hg_free"]] * NACC
        if delta_mode:
            p.wait("pe", sDv, track["hg_free"])
            track["acc_free"] = [0] * NACC
        weight_sweep(bufs, "ffn_delta" if delta_mode else "ffn",
                     x_res if add_rs else x_src, x_dst, track)
        if DEBUG and not add_rs:
            phase_barrier()
            p.dma("sp", dbg_xnT[:, :], xnT[:, :], sSt)
            for nm, tt in (("mu", st_mu), ("rstd", st_rstd), ("sum", st_sum),
                           ("ssq", st_ssq), ("var", st_var), ("ln", st_ln),
                           ("e2", st_e2), ("m2", st_m2)):
                pass
            p.dma("sp", dbg_stats[:, 0:NT], st_mu[:, :], sSt)
            p.dma("sp", dbg_stats[:, NT:2 * NT], st_rstd[:, :], sSt)
            p.dma("sp", dbg_stats[:, 2 * NT:3 * NT], st_sum[:, :], sSt)
            p.dma("sp", dbg_stats[:, 3 * NT:4 * NT], st_ssq[:, :], sSt)
            p.dma("sp", dbg_stats[:, 4 * NT:5 * NT], st_var[:, :], sSt)
            p.dma("sp", dbg_stats[:, 5 * NT:6 * NT], st_ln[:, :], sSt)
            p.dma("sp", dbg_xin[:, :], bufs["xin"][:, 0:2 * D], sSt)
            p.dma("sp", dbg_xn[:, :], bufs["xn_b"][:, :], sSt)
            p.dma("sp", dbg_hgT1[:, :], bufs["hgT1"][:, :], sSt)
            p.dma("sp", dbg_wblk[:, :], wblk[:, :], sSt)
            p.wait("sp", sSt, p.c(sSt))
        res = dict(row_stored=list(track["row_stored"]))
        ph.close()
        return res

    def attn_sublayer(prev):
        ph = ExitStack()
        psb = lambda n, s, d=F32: ph.enter_context(nc.sbuf_tensor(n, s, d))
        pps = lambda n, s, d=F32: ph.enter_context(nc.psum_tensor(n, s, d))
        bufs = dict(
            add_delta=True,
            xin=psb("a_xin", [128, 3 * D]),
            xn_b=psb("a_xn", [128, 2 * D], BF16),
            stb=psb("a_stb", [128, 4 * FB], BF16),
            battn_t=psb("a_battn", [128, 3 * D], BF16),
            tp_ps=[pps("a_tp0", [128, 1024], BF16), pps("a_tp1", [128, 1024], BF16)],
            acc=pps("a_acc", [128, NACC * 512]),
        )
        track = new_track()
        c_bcast = p.dma("pool", battn_bf[0:1, :], battn[0:1, :], sWd)
        p.wait("sp", sWd, c_bcast)
        track["c_battn"] = p.dma("sp", bufs["battn_t"][:, :],
                                 battn_bf[0:1, :].partition_broadcast(128).squeeze(1), sXd)
        for t in range(NT):
            p.wait("sp", sSt, prev["row_stored"][t])
            prepass_tile(bufs, t, x0, True, False, track)
        cc_cnt = weight_sweep(bufs, "qkv", None, None, track)
        p.wait("pool", sSt, cc_cnt)
        c_ag = p.op("pool", lambda E: E.collective_compute(
            "AllGather", ALU.bypass, replica_groups=groups,
            ins=[cc_in[:, :].opt()], outs=[cc_out[:, :].opt()]), sCc)
        qkv_all_stored = p.c(sSt)
        p.wait("sp", sSt, qkv_all_stored)
        p.wait("sp", sCc, p.c(sCc))

        for r_ in range(2):
            pred_dma(qkv_all[r_ * T:(r_ + 1) * T, :], qkv_own[:, :], sXd, r_, True)
            pr = 1 - r_
            pred_dma(qkv_all[pr * T:(pr + 1) * T, :], cc_out[pr * T:(pr + 1) * T, :],
                     sXd, r_, True)
        qkv_norm = p.c(sXd)
        ph.close()

        phase_barrier()
        ph2 = ExitStack()
        psb2 = lambda n, s, d=F32: ph2.enter_context(nc.sbuf_tensor(n, s, d))
        pps2 = lambda n, s, d=F32: ph2.enter_context(nc.psum_tensor(n, s, d))
        strip_m = psb2("c_stripm", [128, 2176])
        strip_h = psb2("c_strip", [128, 2176])
        c_stripm = p.dma("pool", strip_m[:, :], strip_i[:, :], sWd)
        qtok = psb2("c_qtok", [128, 2 * S], BF16)
        ktok = psb2("c_ktok", [128, 2 * S], BF16)
        vtok = psb2("c_vtok", [128, 2 * S], BF16)
        qfm = psb2("c_qfm", [128, S], BF16)
        kfm = psb2("c_kfm", [128, S], BF16)
        sbS = psb2("c_S", [128, 2 * S])
        p_b = psb2("c_pb", [128, 2 * S], BF16)
        pt_sb = psb2("c_pt", [128, 2 * S], BF16)
        l_all = psb2("c_l", [128, NQ])
        linv_all = psb2("c_linv", [128, NQ])
        negm = psb2("c_negm", [128, 2])
        lT_sb = psb2("c_lT", [16, 128])
        linvrow = psb2("c_lrow", [128, S])
        s_ps = pps2("c_sps", [128, 1024])
        av_ps = pps2("c_avps", [128, S])
        pt_ps = [pps2("c_ptps0", [128, 1024], BF16), pps2("c_ptps1", [128, 1024], BF16)]

        attnout = xnT
        tok_free = {0: 0, 1: 0}
        c_ptsb_free = {0: 0, 1: 0}
        c_ptsb_free_eng = {0: None, 1: None}
        av_evac_dve = 0
        c_lrow_prev = 0
        c_lt_prev = 0
        prev_comb = 0
        prev_exp = [0, 0]
        prev_pt = [0, 0]
        prev_av = 0
        c_cc = p.c(sCc)

        for h in range(H):
            hb = h % 2
            p.wait("dve", sWd, c_stripm)
            p.opw("dve", lambda E, hh=h, sm=strip_m: E.tensor_scalar(
                strip_h[:, :], sm[:, :], slopes_s[:, hh:hh + 1], None,
                op0=ALU.mult), sDv)
            p.wait("sp", sXd, qkv_norm)
            p.wait("sp", sPe, tok_free[hb])
            for (xt, base) in ((qtok, 0), (ktok, S), (vtok, 2 * S)):
                col = base + h * 128
                dst3 = xt[:, hb * S:(hb + 1) * S].rearrange("p (j d) -> p j d", d=128)
                p.dma("sp", dst3,
                      qkv_all[:, col:col + 128].rearrange("(j p) d -> p j d", p=128), sXd)
            c_load = p.c(sXd)
            p.wait("pe", sXd, c_load)
            for (src, dstf) in ((qtok, qfm), (ktok, kfm)):
                for g in range(2):
                    bank = pt_ps[g % 2]
                    fe = c_ptsb_free_eng[g % 2]
                    if fe is not None:
                        p.wait("pe", fe, c_ptsb_free[g % 2])
                    for i in range(8):
                        jj = g * 8 + i
                        cpe = p.op("pe", lambda E, b=bank, ii=i, s_=src, j_=jj, hb_=hb:
                                   E.transpose(
                            b[:, ii * 128:(ii + 1) * 128],
                            s_[:, hb_ * S + j_ * 128: hb_ * S + (j_ + 1) * 128],
                            ident_b[:, :]), sPe)
                    p.wait("act", sPe, cpe)
                    cev = p.op("act", lambda E, b=bank, d_=dstf, g_=g: E.copy(
                        d_[:, g_ * 1024:(g_ + 1) * 1024], b[:, :]), sAc)
                    c_ptsb_free[g % 2] = cev
                    c_ptsb_free_eng[g % 2] = sAc
            c_qk_done = p.c(sAc)
            p.wait("pe", sAc, c_qk_done)

            c_comb = [[0] * 4 for _ in range(NQ)]
            c_av = [0] * NQ
            c_sb_exp = [0] * NQ
            c_pt_read = [0] * NQ

            def emit_S_and_combine(j):
                kmax = 128 * (j + 1)
                nkc = (kmax + 511) // 512
                if j >= 1:
                    p.wait("pe", sDv, c_comb[j - 1][(128 * j + 511) // 512 - 1])
                else:
                    p.wait("pe", sDv, prev_comb)
                off = 2048 - j * 128
                if j >= 2:
                    p.wait("dve", sAc, c_sb_exp[j - 2])
                else:
                    p.wait("dve", sAc, prev_exp[j % 2])
                for kc in range(nkc):
                    n = min(512, kmax - kc * 512)
                    if kc >= 2:
                        p.wait("pe", sDv, c_comb[j][kc - 2])
                    cpe = p.op("pe", lambda E, j_=j, k_=kc, n_=n: E.matmul(
                        s_ps[:, (k_ % 2) * 512:(k_ % 2) * 512 + n_],
                        qfm[:, j_ * 128:(j_ + 1) * 128],
                        kfm[:, k_ * 512:k_ * 512 + n_], start=True, stop=True), sPe)
                    p.wait("dve", sPe, cpe)
                    c_comb[j][kc] = p.op("dve", lambda E, j_=j, k_=kc, n_=n, o=off:
                                         E.scalar_tensor_tensor(
                        sbS[:, (j_ % 2) * S + k_ * 512:(j_ % 2) * S + k_ * 512 + n_],
                        s_ps[:, (k_ % 2) * 512:(k_ % 2) * 512 + n_], SCALE,
                        strip_h[:, o + k_ * 512:o + k_ * 512 + n_],
                        op0=ALU.mult, op1=ALU.add), sDv)

            c_negm = [0] * NQ

            def emit_max_negm(j):
                kmax = 128 * (j + 1)
                if j >= 2:
                    p.wait("dve", sPe, c_pt_read[j - 2])
                else:
                    p.wait("dve", sPe, prev_pt[j % 2])
                p.wait("dve", sDv, c_comb[j][(kmax + 511) // 512 - 1])
                p.opw("dve", lambda E, j_=j, km=kmax: E.tensor_scalar(
                    p_b[:, (j_ % 2) * S:(j_ % 2) * S + km],
                    sbS[:, (j_ % 2) * S:(j_ % 2) * S + km], 1.0, MASKNEG,
                    op0=ALU.mult, op1=ALU.max, accum_out=l_all[:, j_:j_ + 1]), sDv)
                c_negm[j] = p.op("dve", lambda E, j_=j: E.tensor_scalar(
                    negm[:, j_ % 2:j_ % 2 + 1], l_all[:, j_:j_ + 1], -1.0, None,
                    op0=ALU.mult), sDv)

            def emit_exp(j):
                kmax = 128 * (j + 1)
                p.wait("act", sDv, c_negm[j])
                ce = p.op("act", lambda E, j_=j, km=kmax: E.activation(
                    p_b[:, (j_ % 2) * S:(j_ % 2) * S + km],
                    sbS[:, (j_ % 2) * S:(j_ % 2) * S + km], AF.Exp,
                    bias=negm[:, j_ % 2:j_ % 2 + 1],
                    accum_out=l_all[:, j_:j_ + 1]), sAc)
                c_sb_exp[j] = ce
                return ce

            def emit_pt_av(j, c_exp):
                nkt = j + 1
                p.wait("pe", sAc, c_exp)
                if h > 0 and j == 0:
                    p.wait("pe", sDv, av_evac_dve)
                ngr = (nkt + 7) // 8
                for g in range(ngr):
                    bank = pt_ps[g % 2]
                    fe = c_ptsb_free_eng[g % 2]
                    if fe is not None:
                        p.wait("pe", fe, c_ptsb_free[g % 2])
                    lo, hi = g * 8, min(nkt, g * 8 + 8)
                    for kt in range(lo, hi):
                        cpe = p.op("pe", lambda E, b=bank, k_=kt, j_=j: E.transpose(
                            b[:, (k_ % 8) * 128:(k_ % 8 + 1) * 128],
                            p_b[:, (j_ % 2) * S + k_ * 128:(j_ % 2) * S + (k_ + 1) * 128],
                            ident_b[:, :]), sPe)
                    p.wait("dve", sPe, cpe)
                    p.wait("dve", sPe, c_av[j - 2] if j >= 2 else prev_av)
                    cev = p.op("dve", lambda E, b=bank, lo_=lo, hi_=hi, j_=j: E.tensor_copy(
                        pt_sb[:, (j_ % 2) * S + lo_ * 128:(j_ % 2) * S + hi_ * 128],
                        b[:, 0:(hi_ - lo_) * 128]), sDv)
                    c_ptsb_free[g % 2] = cev
                    c_ptsb_free_eng[g % 2] = sDv
                    p.wait("pe", sDv, cev)
                    for kt in range(lo, hi):
                        cav = p.op("pe", lambda E, k_=kt, j_=j, hb_=hb: E.matmul(
                            av_ps[:, j_ * 128:(j_ + 1) * 128],
                            vtok[:, hb_ * S + k_ * 128:hb_ * S + (k_ + 1) * 128],
                            pt_sb[:, (j_ % 2) * S + k_ * 128:(j_ % 2) * S + (k_ + 1) * 128],
                            start=(k_ == 0), stop=(k_ == j_),
                            skip_group_check=True), sPe)
                c_pt_read[j] = cpe
                c_av[j] = cav
                return cav

            emit_S_and_combine(0)
            emit_max_negm(0)
            for j in range(NQ):
                if j + 1 < NQ:
                    emit_S_and_combine(j + 1)
                    emit_max_negm(j + 1)
                ce = emit_exp(j)
                emit_pt_av(j, ce)
            prev_comb = c_comb[NQ - 1][3]
            prev_exp = [c_sb_exp[NQ - 2], c_sb_exp[NQ - 1]]
            prev_pt = [c_pt_read[NQ - 2], c_pt_read[NQ - 1]]
            prev_av = c_av[NQ - 1]
            tok_free[hb] = c_av[NQ - 1]

            p.wait("dve", sAc, c_sb_exp[NQ - 1])
            p.wait("dve", sPe, c_lt_prev)
            c_li = p.opw("dve", lambda E: E.reciprocal(linv_all[:, :], l_all[:, :]), sDv)
            p.wait("pe", sDv, c_li)
            p.wait("pe", sDv, c_comb[NQ - 1][3])
            p.wait("pe", sXd, c_identf)
            c_lt = p.op("pe", lambda E: E.transpose(
                s_ps[0:16, 0:128], linv_all[:, :], ident_f[:, :]), sPe)
            c_lt_prev = c_lt
            p.wait("act", sPe, c_lt)
            p.wait("act", sXd, c_lrow_prev)
            c_lte = p.op("act", lambda E: E.copy(lT_sb[:, :], s_ps[0:16, 0:128]), sAc)
            p.wait("sp", sAc, c_lte)
            c_lst = p.dma("sp", lrow_d[:, :], lT_sb[:, :], sSt)
            p.wait("sp", sSt, c_lst)
            p.wait("sp", sDv, av_evac_dve)
            c_lrow = p.dma("sp", linvrow[:, :].rearrange("p (j q) -> p j q", q=128),
                           lrow_d[:, :].partition_broadcast(128), sXd)
            c_lrow_prev = c_lrow
            p.wait("dve", sXd, c_lrow)
            p.wait("dve", sPe, c_av[NQ - 1])
            for g in range(2):
                cavd = p.op("dve", lambda E, g_=g, hh=h: E.tensor_tensor(
                    attnout[:, hh * S + g_ * 1024: hh * S + (g_ + 1) * 1024],
                    av_ps[:, g_ * 1024:(g_ + 1) * 1024],
                    linvrow[:, g_ * 1024:(g_ + 1) * 1024], op=ALU.mult), sDv)
            av_evac_dve = cavd

        ph2.close()

        if DEBUG:
            phase_barrier()
            p.dma("sp", dbg_attno[:, :], attnout[:, :], sSt)
            p.dma("sp", dbg_small[:, 0:NQ], l_all[:, :], sSt)
            p.dma("sp", dbg_small[:, NQ:2 * NQ], linv_all[:, :], sSt)
            p.dma("sp", dbg_small[:, 32:32 + NQ],
                  linvrow[:, :].rearrange("p (j q) -> p j q", q=128)[:, :, 0:1].squeeze(2),
                  sSt, allow_slow_non_contiguous=True)
            p.wait("sp", sSt, p.c(sSt))
        phase_barrier()
        ph3 = ExitStack()
        psb3 = lambda n, s, d=F32: ph3.enter_context(nc.sbuf_tensor(n, s, d))
        pps3 = lambda n, s, d=F32: ph3.enter_context(nc.psum_tensor(n, s, d))
        stp = psb3("p_stp", [128, 8 * FB])
        bprojh = psb3("p_bh", [128, D])
        acc3 = pps3("p_acc", [128, 4096])
        c_bh = 0
        for half in range(2):
            if half == 1:
                p.wait("sp", sDv, c_bh)
            cb = p.dma("sp", stp[:, 0:2048],
                       bproj[0:1, half * 2048:(half + 1) * 2048].partition_broadcast(
                           128).squeeze(1), sXd)
            p.wait("dve", sXd, cb)
            c_bh = p.op("dve", lambda E, hh=half: E.tensor_scalar(
                bprojh[:, hh * 2048:(hh + 1) * 2048], stp[:, 0:2048], 0.5, None,
                op0=ALU.mult), sDv)
        NBP = D // FB
        PFI = 16
        wfree = {0: 0, 1: 0}
        accfree = [0] * 8
        stfree = [0] * 4
        cnt = 0
        for bi in range(NBP):
            c0 = bi * FB
            wb = wblk[:, (bi % 2) * WBW:(bi % 2) * WBW + PFI * FB]
            p.wait("pool", sPe, wfree[bi % 2])
            cw = p.dma("pool", wb.rearrange("p (c f) -> p c f", c=PFI),
                       wproj[:, c0:c0 + FB].rearrange("(c p) f -> p c f", p=128), sWd)
            for qj in range(NQ):
                reg = acc3[:, (qj % 8) * 512:(qj % 8) * 512 + FB]
                p.wait("pe", sWd, cw)
                p.wait("pe", sDv, accfree[qj % 8])
                for fi in range(PFI):
                    lhsT = attnout[:, (fi * NQ + qj) * 128:(fi * NQ + qj + 1) * 128]
                    c_mm = p.op("pe", lambda E, rg=reg, l=lhsT, w=wb, f=fi: E.matmul(
                        rg, l, w[:, f * FB:(f + 1) * FB], start=(f == 0),
                        stop=(f == PFI - 1), skip_group_check=True), sPe)
                wfree[bi % 2] = c_mm
                slot = cnt % 4
                cnt += 1
                p.wait("dve", sPe, c_mm)
                p.wait("dve", sDv, c_bh)
                p.wait("dve", sSt, stfree[slot])
                cev = p.op("dve", lambda E, s_=slot, rg=reg, cc=c0: E.scalar_tensor_tensor(
                    stp[:, s_ * FB:(s_ + 1) * FB], rg, 1.0, bprojh[:, cc:cc + FB],
                    op0=ALU.mult, op1=ALU.add), sDv)
                accfree[qj % 8] = cev
                p.wait("sp", sDv, cev)
                rsd = rs_in[c0 // 2048]
                cc2 = c0 % 2048
                cst = p.dma("sp", rsd[qj * 128:(qj + 1) * 128, cc2:cc2 + FB],
                            stp[:, slot * FB:(slot + 1) * FB], sSt)
                stfree[slot] = cst
        c_all_st = p.c(sSt)
        if DEBUG:
            p.wait("sp", sSt, c_all_st)
            p.dma("sp", dbg_rsina[:, :], rs_in[0][:, :], sSt)
            p.dma("sp", dbg_rsinb[:, :], rs_in[1][:, :], sSt)
            c_all_st = p.c(sSt)
        p.wait("pool", sSt, c_all_st)
        for half in range(2):
            p.op("pool", lambda E, hh=half: E.collective_compute(
                "ReduceScatter", ALU.add, replica_groups=groups,
                ins=[rs_in[hh][:, :].opt()],
                outs=[rs_out[hh][:, :].opt()]), sCc)
        c_rs = p.c(sCc)
        ph3.close()
        return dict(c_rs=c_rs)

    d1 = ffn_sublayer(x0, delta1, add_rs=False, delta_mode=True)
    phase_barrier()
    if DEBUG:
        p.dma("sp", dbg_xres1[:, :], x_res[:, :], sSt)
    at = attn_sublayer(d1)
    phase_barrier()
    p.wait("sp", sCc, at["c_rs"])
    if DEBUG:
        p.dma("sp", dbg_qkvall[:, :], qkv_all[:, :], sSt)
        p.dma("sp", dbg_rsa[:, :], rs_out[0][:, :], sSt)
        p.dma("sp", dbg_rsb[:, :], rs_out[1][:, :], sSt)
    ffn_sublayer(x_res, y, add_rs=True)
    if DEBUG:
        p.dma("sp", dbg_xres2[:, :], x_res[:, :], sSt)
    p.wait("sp", sSt, p.c(sSt))

    with nc.Block() as block:
        @block.tensor
        def _(E):
            for f in p.q["pe"]:
                f(E)

        @block.scalar
        def _(E):
            for f in p.q["act"]:
                f(E)

        @block.vector
        def _(E):
            for f in p.q["dve"]:
                f(E)

        @block.gpsimd
        def _(E):
            for f in p.q["pool"]:
                f(E)

        @block.sync
        def _(E):
            for f in p.q["sp"]:
                f(E)

    es.close()
    return nc


def _host_inputs(L0_attn_hidden_states, L0_attn_w_attn, L0_attn_b_attn, L0_attn_w_proj,
                 L0_attn_b_proj, L0_ffn_base_W, L0_ffn_gate_W, L0_ffn_lora_A, L0_ffn_lora_B):
    x = np.ascontiguousarray(np.asarray(L0_attn_hidden_states, dtype=np.float32))
    w_attn = np.asarray(L0_attn_w_attn, dtype=np.float32)
    b_attn = np.asarray(L0_attn_b_attn, dtype=np.float32)
    w_proj = np.asarray(L0_attn_w_proj, dtype=np.float32)
    b_proj = np.asarray(L0_attn_b_proj, dtype=np.float32)
    base_W = np.asarray(L0_ffn_base_W, dtype=np.float32)
    gate_W = np.asarray(L0_ffn_gate_W, dtype=np.float32)
    lora_A = np.asarray(L0_ffn_lora_A, dtype=np.float32)
    lora_B = np.asarray(L0_ffn_lora_B, dtype=np.float32)

    bwt = np.ascontiguousarray(base_W.T)
    gwt = np.ascontiguousarray(gate_W.T)
    lat = np.ascontiguousarray(lora_A.reshape(ER, D).T)
    lbp = np.ascontiguousarray(lora_B.transpose(0, 2, 1).reshape(ER, D))

    u = np.arange(2176, dtype=np.float32)[None, :]
    i = np.arange(128, dtype=np.float32)[:, None]
    rel = u - 2048.0 - i
    strip = np.where(rel <= 0, rel, np.float32(MASKNEG)).astype(np.float32)
    ident = np.eye(128, dtype=np.float32)
    slopes_full = _alibi_slopes(32)

    in_maps = []
    for c in range(8):
        b, r = c // 2, c % 2
        my = np.arange(r * 16, r * 16 + 16)
        peer = np.arange((1 - r) * 16, (1 - r) * 16 + 16)

        def cols(heads):
            qs = np.concatenate([np.arange(h * 128, (h + 1) * 128) for h in heads])
            return np.concatenate([qs, D + qs, 2 * D + qs])

        perm = np.concatenate([cols(my), cols(peer)])
        in_maps.append(dict(
            x0=np.ascontiguousarray(x[b, r * T:(r + 1) * T, :]),
            wattn=np.ascontiguousarray(w_attn[:, perm]),
            battn=np.ascontiguousarray(b_attn[perm])[None, :],
            wproj=np.ascontiguousarray(w_proj[r * 2048:(r + 1) * 2048, :]),
            bproj=b_proj[None, :].copy(),
            bwt=bwt, gwt=gwt, lat=lat, lbp=lbp,
            slopes_i=np.ascontiguousarray(np.broadcast_to(
                slopes_full[my][None, :], (128, 16))),
            strip_i=strip, ident_i=ident,
            rank_i=np.array([[r]], dtype=np.int32),
        ))
    return in_maps


_NC_CACHE = {}


def kernel(**inputs):
    in_maps = _host_inputs(**inputs)
    if "nc" not in _NC_CACHE:
        _NC_CACHE["nc"] = build_nc()
    nc = _NC_CACHE["nc"]
    res = run_bass_kernel_spmd(nc, in_maps, core_ids=list(range(8)))
    out = np.empty((4, 2048, 4096), dtype=np.float32)
    for c in range(8):
        b, r = c // 2, c % 2
        out[b, r * T:(r + 1) * T, :] = res.results[c]["y"]
    return out


# version 09
# speedup vs baseline: 1.0285x; 1.0166x over previous
import numpy as np
import math
import os
from contextlib import ExitStack
DEBUG = bool(int(os.environ.get("MK_DEBUG", "0")))

import concourse.bass as bass
import concourse.mybir as mybir
from concourse.bass_utils import run_bass_kernel_spmd

F32 = mybir.dt.float32
BF16 = mybir.dt.bfloat16
I32 = mybir.dt.int32
AF = mybir.ActivationFunctionType
ALU = mybir.AluOpType
AX = mybir.AxisListType

D = 4096
T = 1024
NT = 8
S = 2048
NQ = 16
H = 16
FI = 32
NE = 28
ER = 224
SCALE = 1.0 / 128.0
EPS = 1e-5
MASKNEG = -1e30
FB = 256
NACC = 6

ENGS = ("pe", "act", "dve", "pool", "sp")


def _alibi_slopes(n_heads):
    cp2 = 2 ** math.floor(math.log2(n_heads))
    base = 2 ** (-(2 ** (-(math.log2(cp2) - 3))))
    slopes = base ** np.arange(1, cp2 + 1, dtype=np.float32)
    if cp2 != n_heads:
        extra_base = 2 ** (-(2 ** (-(math.log2(2 * cp2) - 3))))
        n_rem = min(cp2, n_heads - cp2)
        extra = extra_base ** np.arange(1, 1 + 2 * n_rem, 2, dtype=np.float32)
        slopes = np.concatenate([slopes, extra])
    return slopes.astype(np.float32)


class P:
    def __init__(self, nc):
        self.nc = nc
        self.q = {e: [] for e in ENGS}
        self.cnt = {}

    def emit(self, eng, fn):
        self.q[eng].append(fn)

    def wait(self, eng, sem, val):
        if val and val > 0:
            self.emit(eng, lambda E, s=sem, v=val: E.wait_ge(s, v))

    def op(self, eng, fn, sem=None, n=1):
        if sem is None:
            self.emit(eng, fn)
            return None
        self.cnt[sem] = self.cnt.get(sem, 0) + n
        c = self.cnt[sem]
        self.emit(eng, lambda E, f=fn, s=sem, k=n: f(E).then_inc(s, k))
        return c

    def dma(self, eng, out, in_, sem, **kw):
        return self.op(eng, lambda E, o=out, i=in_, k=dict(kw): E.dma_start(out=o, in_=i, **k),
                       sem=sem, n=16)

    def c(self, sem):
        return self.cnt.get(sem, 0)

    def opw(self, eng, fn, sem):
        c = self.op(eng, fn, sem)
        self.wait(eng, sem, c)
        return c


def build_nc():
    nc = bass.Bass("TRN2", target_bir_lowering=False)

    x0 = nc.dram_tensor("x0", [T, D], F32, kind="ExternalInput")
    wattn = nc.dram_tensor("wattn", [D, 3 * D], F32, kind="ExternalInput")
    battn = nc.dram_tensor("battn", [1, 3 * D], F32, kind="ExternalInput")
    wproj = nc.dram_tensor("wproj", [2048, D], F32, kind="ExternalInput")
    bproj = nc.dram_tensor("bproj", [1, D], F32, kind="ExternalInput")
    bwt = nc.dram_tensor("bwt", [D, D], F32, kind="ExternalInput")
    gwt = nc.dram_tensor("gwt", [D, NE], F32, kind="ExternalInput")
    lat = nc.dram_tensor("lat", [D, ER], F32, kind="ExternalInput")
    lbp = nc.dram_tensor("lbp", [ER, D], F32, kind="ExternalInput")
    slopes_i = nc.dram_tensor("slopes_i", [128, H], F32, kind="ExternalInput")
    strip_i = nc.dram_tensor("strip_i", [128, 2176], F32, kind="ExternalInput")
    ident_i = nc.dram_tensor("ident_i", [128, 128], F32, kind="ExternalInput")
    rank_i = nc.dram_tensor("rank_i", [1, 1], I32, kind="ExternalInput")
    y = nc.dram_tensor("y", [T, D], F32, kind="ExternalOutput")

    if DEBUG:
        dbg_xres1 = nc.dram_tensor("dbg_xres1", [T, D], F32, kind="ExternalOutput")
        dbg_xnT = nc.dram_tensor("dbg_xnT", [128, NT * FI * 128], BF16, kind="ExternalOutput")
        dbg_stats = nc.dram_tensor("dbg_stats", [128, 8 * NT], F32, kind="ExternalOutput")
        dbg_xin = nc.dram_tensor("dbg_xin", [128, 2 * D], F32, kind="ExternalOutput")
        dbg_xn = nc.dram_tensor("dbg_xn", [128, 2 * D], BF16, kind="ExternalOutput")
        dbg_hgT1 = nc.dram_tensor("dbg_hgT1", [128, NT * 128], BF16, kind="ExternalOutput")
        dbg_wblk = nc.dram_tensor("dbg_wblk", [128, 2 * FI * FB], BF16, kind="ExternalOutput")
        dbg_qkvall = nc.dram_tensor("dbg_qkvall", [S, 6144], BF16, kind="ExternalOutput")
        dbg_rsa = nc.dram_tensor("dbg_rsa", [T, 2048], F32, kind="ExternalOutput")
        dbg_rsb = nc.dram_tensor("dbg_rsb", [T, 2048], F32, kind="ExternalOutput")
        dbg_xres2 = nc.dram_tensor("dbg_xres2", [T, D], F32, kind="ExternalOutput")
        dbg_attno = nc.dram_tensor("dbg_attno", [128, NT * FI * 128], BF16, kind="ExternalOutput")
        dbg_rsina = nc.dram_tensor("dbg_rsina", [S, 2048], F32, kind="ExternalOutput")
        dbg_rsinb = nc.dram_tensor("dbg_rsinb", [S, 2048], F32, kind="ExternalOutput")
        dbg_small = nc.dram_tensor("dbg_small", [128, 64], F32, kind="ExternalOutput")
    x_res = nc.dram_tensor("x_res", [T, D], F32)
    qkv_own = nc.dram_tensor("qkv_own", [T, 6144], BF16)
    cc_in = nc.dram_tensor("cc_in", [T, 6144], BF16)
    cc_out = nc.dram_tensor("cc_out", [S, 6144], BF16)
    delta1 = nc.dram_tensor("delta1", [T, D], F32)
    rs_in = [nc.dram_tensor("rs_in_a", [S, 2048], F32),
             nc.dram_tensor("rs_in_b", [S, 2048], F32)]
    rs_out = [nc.dram_tensor("rs_out_a", [T, 2048], F32),
              nc.dram_tensor("rs_out_b", [T, 2048], F32)]
    battn_bf = nc.dram_tensor("battn_bf", [1, 3 * D], BF16)
    lrow_d = nc.dram_tensor("lrow_d", [16, 128], F32)
    qkv_all = nc.dram_tensor("qkv_all", [S, 6144], BF16)

    groups = [[0, 1], [2, 3], [4, 5], [6, 7]]

    es = ExitStack()
    sb = lambda n, s, d=F32: es.enter_context(nc.sbuf_tensor(n, s, d))
    sem = lambda n: es.enter_context(nc.semaphore(n))

    sWd = sem("sWd")
    sXd = sem("sXd")
    sSt = sem("sSt")
    sPe = sem("sPe")
    sAc = sem("sAc")
    sDv = sem("sDv")
    sCc = sem("sCc")
    sPd = sem("sPd")
    sXo = sem("sXo")
    sW2 = [sem("sW20"), sem("sW21")]
    sL2 = [sem("sL20"), sem("sL21")]
    sxo4 = [sem(f"sxo{i}") for i in range(4)]
    sxi2 = [sem("sxi0"), sem("sxi1")]
    srs = sem("srs")
    slr = sem("slr")
    sbt = sem("sbt")
    sst4 = [sem(f"sst{i}") for i in range(4)]
    sxr = sem("sxr")
    scst = sem("scst")
    ALLSEMS = tuple([sWd, sXd, sSt, sPe, sAc, sDv, sCc, sPd, sXo]
                    + sW2 + sL2 + sxo4 + sxi2 + [srs, slr, sbt, sxr, scst] + sst4)

    p = P(nc)

    def phase_barrier(skip=()):
        vals = [(s, p.c(s)) for s in ALLSEMS if s not in skip]
        for eng in ENGS:
            for s, v in vals:
                p.wait(eng, s, v)

    ident_b = sb("ident_b", [128, 128], BF16)
    ident_f = sb("ident_f", [128, 128], F32)
    slopes_s = sb("slopes_s", [128, H])
    xnT = sb("xnT", [128, NT * FI * 128], BF16)
    wblk = sb("wblk", [128, 2 * FI * FB], BF16)
    st_sum = sb("st_sum", [128, NT])
    st_ssq = sb("st_ssq", [128, NT])
    st_mu = sb("st_mu", [128, NT])
    st_e2 = sb("st_e2", [128, NT])
    st_m2 = sb("st_m2", [128, NT])
    st_var = sb("st_var", [128, NT])
    st_ln = sb("st_ln", [128, NT])
    st_rstd = sb("st_rstd", [128, NT])

    WBW = FI * FB

    p.dma("pool", ident_b[:, :], ident_i[:, :], scst)
    p.dma("pool", slopes_s[:, :], slopes_i[:, :], scst)
    c_const = p.c(scst)
    c_identf = p.dma("sp", ident_f[:, :], ident_i[:, :], sXd)

    regctx = nc.sync.register(name="rankreg")
    rreg = regctx.__enter__()
    rv_holder = {}

    def _loadrank(E):
        E.reg_load(rreg, rank_i[0:1, 0:1])
        rv_holder["rv"] = E.snap(rreg, min_val=0, max_val=1)
        return E.nop()

    p.emit("sp", _loadrank)
    p.wait("pe", scst, c_const)

    def pred_dma(out, in_, sem_, half, eq):
        p.cnt[sem_] = p.cnt.get(sem_, 0) + 16
        c = p.cnt[sem_]

        def f(E, o=out, i=in_, s=sem_, h=half, e=eq):
            rv = rv_holder["rv"]
            cond = (rv == h) if e else (rv != h)
            return E.dma_start(out=o, in_=i, cond=cond).then_inc(s, 16)

        p.emit("sp", f)
        return c

    def prepass_tile(bufs, t, x_src, add_rs, ffn, track):
        xin, xn_b = bufs["xin"], bufs["xn_b"]
        tp_ps = bufs["tp_ps"]
        slot = t % 2
        xsl = xin[:, slot * D:(slot + 1) * D]
        xnsl = xn_b[:, slot * D:(slot + 1) * D]
        p.wait("sp", sDv, track["slot_free"][slot])
        cx = p.dma("sp", xsl, x_src[t * 128:(t + 1) * 128, :], sxi2[slot])
        if add_rs:
            rsl = xin[:, 2 * D:3 * D]
            p.wait("sp", sDv, track["rs_free"])
            if bufs.get("add_delta"):
                cr = p.dma("sp", rsl, delta1[t * 128:(t + 1) * 128, :], srs)
            else:
                p.dma("sp", xin[:, 2 * D:2 * D + 2048],
                      rs_out[0][t * 128:(t + 1) * 128, :], srs)
                cr = p.dma("sp", xin[:, 2 * D + 2048:3 * D],
                           rs_out[1][t * 128:(t + 1) * 128, :], srs)
            p.wait("dve", srs, cr)
            p.wait("dve", sxi2[slot], cx)
            c_add = p.opw("dve", lambda E, xs=xsl, rs=rsl, tt=t: E.scalar_tensor_tensor(
                xs, xs, 1.0, rs, op0=ALU.mult, op1=ALU.add,
                accum_out=st_sum[:, tt:tt + 1]), sDv)
            track["rs_free"] = c_add
            p.wait("sp", sDv, c_add)
            cst = p.dma("sp", x_res[t * 128:(t + 1) * 128, :], xsl, sxr)
            track["xres_st"][t] = cst
        else:
            p.wait("dve", sxi2[slot], cx)
            c_add = p.opw("dve", lambda E, xs=xsl, tt=t: E.reduce_sum(
                st_sum[:, tt:tt + 1], xs, axis=AX.X), sDv)
        p.wait("dve", sPe, track["xn_free"][slot])
        p.opw("dve", lambda E, xs=xsl, o=xnsl, tt=t: E.scalar_tensor_tensor(
            o, xs, 1.0, xs, op0=ALU.mult, op1=ALU.mult,
            accum_out=st_ssq[:, tt:tt + 1]), sDv)
        p.opw("dve", lambda E, tt=t: E.tensor_scalar(
            st_mu[:, tt:tt + 1], st_sum[:, tt:tt + 1], 1.0 / D, None, op0=ALU.mult), sDv)
        p.opw("dve", lambda E, tt=t: E.tensor_scalar(
            st_e2[:, tt:tt + 1], st_ssq[:, tt:tt + 1], 1.0 / D, None, op0=ALU.mult), sDv)
        p.opw("dve", lambda E, tt=t: E.tensor_tensor(
            st_m2[:, tt:tt + 1], st_mu[:, tt:tt + 1], st_mu[:, tt:tt + 1], op=ALU.mult), sDv)
        p.opw("dve", lambda E, tt=t: E.scalar_tensor_tensor(
            st_var[:, tt:tt + 1], st_m2[:, tt:tt + 1], -1.0, st_e2[:, tt:tt + 1],
            op0=ALU.mult, op1=ALU.add), sDv)
        c_var = p.opw("dve", lambda E, tt=t: E.tensor_scalar(
            st_var[:, tt:tt + 1], st_var[:, tt:tt + 1], 1.0, EPS,
            op0=ALU.mult, op1=ALU.add), sDv)
        p.wait("act", sDv, c_var)
        p.opw("act", lambda E, tt=t: E.activation(
            st_ln[:, tt:tt + 1], st_var[:, tt:tt + 1], AF.Ln), sAc)
        c_rstd = p.op("act", lambda E, tt=t: E.activation(
            st_rstd[:, tt:tt + 1], st_ln[:, tt:tt + 1], AF.Exp, scale=-0.5), sAc)
        p.wait("dve", sAc, c_rstd)
        c_xn = p.op("dve", lambda E, xs=xsl, o=xnsl, tt=t: E.tensor_scalar(
            o, xs, st_mu[:, tt:tt + 1], st_rstd[:, tt:tt + 1],
            op0=ALU.subtract, op1=ALU.mult), sDv)
        track["slot_free"][slot] = c_xn
        p.wait("pe", sDv, c_xn)
        last_ev = 0
        for g in range(4):
            bank = tp_ps[g % 2]
            p.wait("pe", sAc, track["tp_free"][g % 2])
            for i in range(8):
                fi = g * 8 + i
                cpe = p.op("pe", lambda E, b=bank, ii=i, f=fi, o=xnsl: E.transpose(
                    b[:, ii * 128:(ii + 1) * 128], o[:, f * 128:(f + 1) * 128],
                    ident_b[:, :]), sPe)
            p.wait("act", sPe, cpe)
            cev = p.op("act", lambda E, b=bank, tt=t, g_=g: E.copy(
                xnT[:, (tt * FI + g_ * 8) * 128:(tt * FI + g_ * 8 + 8) * 128], b[:, :]), sAc)
            track["tp_free"][g % 2] = cev
            last_ev = cev
        track["xn_free"][slot] = cpe
        track["xnT_done"][t] = last_ev
        if not ffn:
            return
        latb, gwtb, hg = bufs["latb"], bufs["gwtb"], bufs["hg"]
        hgacc = bufs["acc"][:, 0:ER + NE]
        p.wait("pe", sAc, last_ev)
        p.wait("pe", sDv, track["hg_free"])
        p.wait("pe", sWd, track["c_lora"])
        for fi in range(FI):
            lhsT = xnT[:, (t * FI + fi) * 128:(t * FI + fi + 1) * 128]
            p.op("pe", lambda E, l=lhsT, f=fi, a=hgacc: E.matmul(
                a[:, 0:ER], l, latb[:, f * ER:(f + 1) * ER],
                start=(f == 0), stop=(f == FI - 1), skip_group_check=True), sPe)
            c_hg = p.op("pe", lambda E, l=lhsT, f=fi, a=hgacc: E.matmul(
                a[:, ER:ER + NE], l, gwtb[:, f * NE:(f + 1) * NE],
                start=(f == 0), stop=(f == FI - 1), skip_group_check=True), sPe)
        ga, m1, m2c = bufs["ga"], bufs["m1"], bufs["m2"]
        p.wait("dve", sPe, c_hg)
        p.opw("dve", lambda E, a=hgacc: E.reduce_max(m1[:, 0:1], a[:, ER:ER + NE], axis=AX.X), sDv)
        p.opw("dve", lambda E, a=hgacc: E.tensor_scalar(
            ga[:, 0:NE], a[:, ER:ER + NE], m1[:, 0:1], None, op0=ALU.is_ge), sDv)
        p.opw("dve", lambda E, a=hgacc: E.scalar_tensor_tensor(
            ga[:, 32:32 + NE], ga[:, 0:NE], MASKNEG, a[:, ER:ER + NE],
            op0=ALU.mult, op1=ALU.add), sDv)
        p.opw("dve", lambda E: E.reduce_max(m2c[:, 0:1], ga[:, 32:32 + NE], axis=AX.X), sDv)
        p.opw("dve", lambda E: E.tensor_scalar(
            ga[:, 32:32 + NE], ga[:, 32:32 + NE], m2c[:, 0:1], None, op0=ALU.is_ge), sDv)
        c_d = p.opw("dve", lambda E: E.tensor_sub(m2c[:, 1:2], m1[:, 0:1], m2c[:, 0:1]), sDv)
        p.wait("act", sDv, c_d)
        c_e = p.op("act", lambda E: E.activation(m1[:, 1:2], m2c[:, 1:2], AF.Exp,
                                                 scale=-1.0), sAc)
        p.wait("dve", sAc, c_e)
        p.opw("dve", lambda E: E.tensor_scalar(m1[:, 2:3], m1[:, 1:2], 1.0, 1.0,
                                               op0=ALU.mult, op1=ALU.add), sDv)
        p.opw("dve", lambda E: E.reciprocal(m2c[:, 2:3], m1[:, 2:3]), sDv)
        p.opw("dve", lambda E: E.tensor_scalar(m1[:, 3:4], m2c[:, 2:3], 2.0, None,
                                               op0=ALU.mult), sDv)
        p.opw("dve", lambda E: E.tensor_tensor(m2c[:, 3:4], m1[:, 1:2], m1[:, 3:4],
                                               op=ALU.mult), sDv)
        p.opw("dve", lambda E: E.tensor_scalar(ga[:, 0:NE], ga[:, 0:NE], m1[:, 3:4], None,
                                               op0=ALU.mult), sDv)
        p.opw("dve", lambda E: E.scalar_tensor_tensor(
            ga[:, 0:NE], ga[:, 32:32 + NE], m2c[:, 3:4], ga[:, 0:NE],
            op0=ALU.mult, op1=ALU.add), sDv)
        gexp = ga[:, 0:NE].unsqueeze(2).broadcast_to([128, NE, 8])
        c_hgd = p.op("dve", lambda E, a=hgacc, ge=gexp: E.tensor_tensor(
            hg[:, :].rearrange("p (e r) -> p e r", r=8),
            a[:, 0:ER].rearrange("p (e r) -> p e r", r=8), ge, op=ALU.mult), sDv)
        track["hg_free"] = c_hgd
        hgT1, hgT2 = bufs["hgT1"], bufs["hgT2"]
        p.wait("pe", sDv, c_hgd)
        bank = tp_ps[0]
        p.wait("pe", sAc, track["tp_free"][0])
        p.op("pe", lambda E, b=bank: E.transpose(
            b[:, 0:128], hg[:, 0:128], ident_b[:, :]), sPe)
        c_t2 = p.op("pe", lambda E, b=bank: E.transpose(
            b[0:96, 128:256], hg[:, 128:224], ident_b[:, :]), sPe)
        p.wait("act", sPe, c_t2)
        p.op("act", lambda E, b=bank, tt=t: E.copy(
            hgT1[:, tt * 128:(tt + 1) * 128], b[:, 0:128]), sAc)
        c_ev2 = p.op("act", lambda E, b=bank, tt=t: E.copy(
            hgT2[:, tt * 128:(tt + 1) * 128], b[0:96, 128:256]), sAc)
        track["tp_free"][0] = c_ev2
        track["hgT_done"][t] = c_ev2

    def weight_sweep(bufs, phase, x_src, x_dst, track):
        acc = bufs["acc"]
        if phase == "qkv":
            nb_b = 6144 // FB
            order = list(range(nb_b, 2 * nb_b)) + list(range(nb_b))
            wsrc = wattn
        else:
            order = list(range(D // FB))
            wsrc = bwt
        stf, xold, stb = bufs.get("stf"), bufs.get("xold"), bufs.get("stb")
        battn_t = bufs.get("battn_t")
        hgT1, hgT2 = bufs.get("hgT1"), bufs.get("hgT2")
        lbp1, lbp2 = bufs.get("lbp1"), bufs.get("lbp2")
        p.wait("pe", sAc, max(track["xnT_done"]))
        if phase.startswith("ffn"):
            p.wait("pe", sAc, max(track["hgT_done"]))
        else:
            p.wait("dve", sbt, track["c_battn"])
        nslot = 4
        cnt = 0
        cc_store = None
        for bi, blk in enumerate(order):
            c0 = blk * FB
            wb = wblk[:, (bi % 2) * WBW:(bi % 2) * WBW + WBW]
            p.wait("pool", sPe, track["wblk_free"][bi % 2])
            cw_sem = sW2[bi % 2]
            cw = p.dma("pool", wb.rearrange("p (c f) -> p c f", c=FI),
                       wsrc[:, c0:c0 + FB].rearrange("(c p) f -> p c f", p=128), cw_sem)
            accsem = sAc if phase == "ffn_delta" else sDv
            for t in range(NT):
                r = t % NACC
                reg = acc[:, r * 512:r * 512 + FB]
                p.wait("pe", cw_sem, cw)
                p.wait("pe", accsem, track["acc_free"][r])
                for fi in range(FI):
                    lhsT = xnT[:, (t * FI + fi) * 128:(t * FI + fi + 1) * 128]
                    st = (fi == 0)
                    last = (fi == FI - 1) and not phase.startswith("ffn")
                    c_mm = p.op("pe", lambda E, rg=reg, l=lhsT, w=wb, f=fi, s=st, e=last:
                                E.matmul(rg, l, w[:, f * FB:(f + 1) * FB], start=s, stop=e,
                                         skip_group_check=True), sPe)
                if phase.startswith("ffn"):
                    p.op("pe", lambda E, rg=reg, tt=t, cc=c0: E.matmul(
                        rg, hgT1[:, tt * 128:(tt + 1) * 128], lbp1[:, cc:cc + FB],
                        start=False, stop=False, skip_group_check=True), sPe)
                    c_mm = p.op("pe", lambda E, rg=reg, tt=t, cc=c0: E.matmul(
                        rg, hgT2[:, tt * 128:(tt + 1) * 128], lbp2[:, cc:cc + FB],
                        start=False, stop=True, skip_group_check=True), sPe)
                track["wblk_free"][bi % 2] = c_mm
                slot = cnt % nslot
                cnt += 1
                if phase == "ffn_delta":
                    ssl = stf[:, slot * FB:(slot + 1) * FB]
                    p.wait("act", sPe, c_mm)
                    p.wait("act", sst4[slot], track["stf_free"][slot])
                    cev = p.op("act", lambda E, ss=ssl, rg=reg: E.activation(
                        ss, rg, AF.Copy, scale=0.5), sAc)
                    track["acc_free"][r] = cev
                    p.wait("sp", sAc, cev)
                    cst = p.dma("sp", x_dst[t * 128:(t + 1) * 128, c0:c0 + FB], ssl,
                                sst4[slot])
                    track["stf_free"][slot] = cst
                elif phase == "ffn":
                    xsl = xold[:, slot * FB:(slot + 1) * FB]
                    p.wait("sp", sDv, track["xold_free"][slot])
                    if track["xres_wait"]:
                        p.wait("sp", sxr, track["xres_wait"])
                    cxo = p.dma("sp", xsl, x_src[t * 128:(t + 1) * 128, c0:c0 + FB],
                                sxo4[slot])
                    ssl = stf[:, slot * FB:(slot + 1) * FB]
                    p.wait("dve", sPe, c_mm)
                    p.wait("dve", sxo4[slot], cxo)
                    p.wait("dve", sst4[slot], track["stf_free"][slot])
                    cev = p.op("dve", lambda E, ss=ssl, rg=reg, xs=xsl: E.scalar_tensor_tensor(
                        ss, rg, 0.5, xs, op0=ALU.mult, op1=ALU.add), sDv)
                    track["acc_free"][r] = cev
                    track["xold_free"][slot] = cev
                    p.wait("sp", sDv, cev)
                    cst = p.dma("sp", x_dst[t * 128:(t + 1) * 128, c0:c0 + FB], ssl,
                                sst4[slot])
                    track["stf_free"][slot] = cst
                else:
                    ssl = stb[:, slot * FB:(slot + 1) * FB]
                    p.wait("dve", sPe, c_mm)
                    p.wait("dve", sst4[slot], track["stf_free"][slot])
                    cev = p.op("dve", lambda E, ss=ssl, rg=reg, cc=c0: E.scalar_tensor_tensor(
                        ss, rg, 1.0, battn_t[:, cc:cc + FB], op0=ALU.mult, op1=ALU.add), sDv)
                    track["acc_free"][r] = cev
                    p.wait("sp", sDv, cev)
                    if c0 < 6144:
                        dst = qkv_own[t * 128:(t + 1) * 128, c0:c0 + FB]
                    else:
                        dst = cc_in[t * 128:(t + 1) * 128, c0 - 6144:c0 - 6144 + FB]
                    cst = p.dma("sp", dst, ssl, sst4[slot])
                    track["stf_free"][slot] = cst
            if phase == "qkv" and bi == nb_b - 1:
                cc_store = [p.c(s_) for s_ in sst4]
        return cc_store

    def new_track(c_lora=0, c_battn=0, acc_init=0):
        return dict(slot_free={0: 0, 1: 0}, xn_free={0: 0, 1: 0}, tp_free={0: 0, 1: 0},
                    xnT_done=[0] * NT, hgT_done=[0] * NT, hg_free=0,
                    wblk_free={0: 0, 1: 0}, acc_free=[acc_init] * NACC,
                    xold_free=[0] * 4, stf_free=[0] * 4,
                    row_stored=[0] * NT, xres_st=[0] * NT, rs_free=0,
                    xres_wait=0, c_battn=c_battn, c_lora=c_lora)

    def ffn_sublayer(x_src, x_dst, add_rs, delta_mode=False):
        ph = ExitStack()
        psb = lambda n, s, d=F32: ph.enter_context(nc.sbuf_tensor(n, s, d))
        pps = lambda n, s, d=F32: ph.enter_context(nc.psum_tensor(n, s, d))
        sfx = "2" if add_rs else "1"
        nxin = 3 if add_rs else 2
        bufs = dict(
            xin=psb("f_xin" + sfx, [128, nxin * D]),
            xn_b=psb("f_xn" + sfx, [128, 2 * D], BF16),
            stf=psb("f_stf" + sfx, [128, 4 * FB]),
            xold=psb("f_xold" + sfx, [128, 4 * FB]),
            latb=psb("f_latb" + sfx, [128, FI * ER], BF16),
            gwtb=psb("f_gwtb" + sfx, [128, FI * NE], BF16),
            lbp1=psb("f_lbp1" + sfx, [128, D], BF16),
            lbp2=psb("f_lbp2" + sfx, [96, D], BF16),
            hgT1=psb("f_hgT1" + sfx, [128, NT * 128], BF16),
            hgT2=psb("f_hgT2" + sfx, [96, NT * 128], BF16),
            hg=psb("f_hg" + sfx, [128, ER], BF16),
            ga=psb("f_ga" + sfx, [128, 64]),
            m1=psb("f_m1" + sfx, [128, 4]),
            m2=psb("f_m2" + sfx, [128, 4]),
            tp_ps=[pps("f_tp0" + sfx, [128, 1024], BF16),
                   pps("f_tp1" + sfx, [128, 1024], BF16)],
            acc=pps("f_acc" + sfx, [128, NACC * 512]),
        )
        track = new_track()
        p.dma("pool", bufs["latb"][:, :].rearrange("p (c e) -> p c e", c=FI),
              lat[:, :].rearrange("(c p) e -> p c e", p=128), sWd)
        p.dma("pool", bufs["gwtb"][:, :].rearrange("p (c e) -> p c e", c=FI),
              gwt[:, :].rearrange("(c p) e -> p c e", p=128), sWd)
        p.dma("pool", bufs["lbp1"][:, :], lbp[0:128, :], sWd)
        track["c_lora"] = p.dma("pool", bufs["lbp2"][:, :], lbp[128:224, :], sWd)
        for t in range(NT):
            prepass_tile(bufs, t, x_src, add_rs, True, track)
        track["xres_wait"] = p.c(sxr) if add_rs else 0
        track["acc_free"] = [track["hg_free"]] * NACC
        if delta_mode:
            p.wait("pe", sDv, track["hg_free"])
            track["acc_free"] = [0] * NACC
        weight_sweep(bufs, "ffn_delta" if delta_mode else "ffn",
                     x_res if add_rs else x_src, x_dst, track)
        if DEBUG and not add_rs:
            phase_barrier()
            p.dma("sp", dbg_xnT[:, :], xnT[:, :], sSt)
            for nm, tt in (("mu", st_mu), ("rstd", st_rstd), ("sum", st_sum),
                           ("ssq", st_ssq), ("var", st_var), ("ln", st_ln),
                           ("e2", st_e2), ("m2", st_m2)):
                pass
            p.dma("sp", dbg_stats[:, 0:NT], st_mu[:, :], sSt)
            p.dma("sp", dbg_stats[:, NT:2 * NT], st_rstd[:, :], sSt)
            p.dma("sp", dbg_stats[:, 2 * NT:3 * NT], st_sum[:, :], sSt)
            p.dma("sp", dbg_stats[:, 3 * NT:4 * NT], st_ssq[:, :], sSt)
            p.dma("sp", dbg_stats[:, 4 * NT:5 * NT], st_var[:, :], sSt)
            p.dma("sp", dbg_stats[:, 5 * NT:6 * NT], st_ln[:, :], sSt)
            p.dma("sp", dbg_xin[:, :], bufs["xin"][:, 0:2 * D], sSt)
            p.dma("sp", dbg_xn[:, :], bufs["xn_b"][:, :], sSt)
            p.dma("sp", dbg_hgT1[:, :], bufs["hgT1"][:, :], sSt)
            p.dma("sp", dbg_wblk[:, :], wblk[:, :], sSt)
            p.wait("sp", sSt, p.c(sSt))
        res = dict(row_stored=list(track["row_stored"]),
                   st4=[p.c(s_) for s_ in sst4])
        ph.close()
        return res

    def attn_sublayer(prev):
        ph = ExitStack()
        psb = lambda n, s, d=F32: ph.enter_context(nc.sbuf_tensor(n, s, d))
        pps = lambda n, s, d=F32: ph.enter_context(nc.psum_tensor(n, s, d))
        bufs = dict(
            add_delta=True,
            xin=psb("a_xin", [128, 3 * D]),
            xn_b=psb("a_xn", [128, 2 * D], BF16),
            stb=psb("a_stb", [128, 4 * FB], BF16),
            battn_t=psb("a_battn", [128, 3 * D], BF16),
            tp_ps=[pps("a_tp0", [128, 1024], BF16), pps("a_tp1", [128, 1024], BF16)],
            acc=pps("a_acc", [128, NACC * 512]),
        )
        track = new_track()
        c_bcast = p.dma("pool", battn_bf[0:1, :], battn[0:1, :], sWd)
        p.wait("sp", sWd, c_bcast)
        track["c_battn"] = p.dma("sp", bufs["battn_t"][:, :],
                                 battn_bf[0:1, :].partition_broadcast(128).squeeze(1), sbt)
        for i_, s_ in enumerate(sst4):
            p.wait("sp", s_, prev["st4"][i_])
        for t in range(NT):
            prepass_tile(bufs, t, x0, True, False, track)
        cc_cnt = weight_sweep(bufs, "qkv", None, None, track)
        for i_, s_ in enumerate(sst4):
            p.wait("pool", s_, cc_cnt[i_])
        c_ag = p.op("pool", lambda E: E.collective_compute(
            "AllGather", ALU.bypass, replica_groups=groups,
            ins=[cc_in[:, :].opt()], outs=[cc_out[:, :].opt()]), sCc)
        qkv_all_stored = [p.c(s_) for s_ in sst4]
        for i_, s_ in enumerate(sst4):
            p.wait("sp", s_, qkv_all_stored[i_])
        p.wait("sp", sCc, p.c(sCc))

        for r_ in range(2):
            pred_dma(qkv_all[r_ * T:(r_ + 1) * T, :], qkv_own[:, :], sXd, r_, True)
            pr = 1 - r_
            pred_dma(qkv_all[pr * T:(pr + 1) * T, :], cc_out[pr * T:(pr + 1) * T, :],
                     sXd, r_, True)
        qkv_norm = p.c(sXd)
        ph.close()

        phase_barrier()
        ph2 = ExitStack()
        psb2 = lambda n, s, d=F32: ph2.enter_context(nc.sbuf_tensor(n, s, d))
        pps2 = lambda n, s, d=F32: ph2.enter_context(nc.psum_tensor(n, s, d))
        strip_m = psb2("c_stripm", [128, 2176])
        strip_h = psb2("c_strip", [128, 2176])
        c_stripm = p.dma("pool", strip_m[:, :], strip_i[:, :], scst)
        qtok = psb2("c_qtok", [128, 2 * S], BF16)
        ktok = psb2("c_ktok", [128, 2 * S], BF16)
        vtok = psb2("c_vtok", [128, 2 * S], BF16)
        qfm = psb2("c_qfm", [128, S], BF16)
        kfm = psb2("c_kfm", [128, S], BF16)
        sbS = psb2("c_S", [128, 2 * S])
        p_b = psb2("c_pb", [128, 2 * S], BF16)
        pt_sb = psb2("c_pt", [128, 2 * S], BF16)
        l_all = psb2("c_l", [128, NQ])
        linv_all = psb2("c_linv", [128, NQ])
        negm = psb2("c_negm", [128, 2])
        lT_sb = psb2("c_lT", [16, 128])
        linvrow = psb2("c_lrow", [128, S])
        avtmp = psb2("c_avtmp", [128, S], BF16)
        s_ps = pps2("c_sps", [128, 2048])
        av_ps = pps2("c_avps", [128, 1024])
        pt_ps = [pps2("c_ptps0", [128, 1024], BF16), pps2("c_ptps1", [128, 1024], BF16)]

        attnout = xnT
        tok_free = {0: 0, 1: 0}
        c_ptsb_free = {0: 0, 1: 0}
        c_ptsb_free_eng = {0: None, 1: None}
        c_norm_prev = 0
        c_lrow_prev = 0
        c_lt_prev = 0
        prev_comb = 0
        prev_exp = [0, 0]
        prev_pt = [0, 0]
        prev_av = 0
        prev_avev = [0, 0]
        c_cc = p.c(sCc)

        for h in range(H):
            hb = h % 2
            p.wait("dve", scst, c_stripm)
            p.opw("dve", lambda E, hh=h, sm=strip_m: E.tensor_scalar(
                strip_h[:, :], sm[:, :], slopes_s[:, hh:hh + 1], None,
                op0=ALU.mult), sDv)
            p.wait("sp", sXd, qkv_norm)
            p.wait("sp", sPe, tok_free[hb])
            lsem = sL2[hb]
            for (xt, base) in ((qtok, 0), (ktok, S), (vtok, 2 * S)):
                col = base + h * 128
                dst3 = xt[:, hb * S:(hb + 1) * S].rearrange("p (j d) -> p j d", d=128)
                p.dma("sp", dst3,
                      qkv_all[:, col:col + 128].rearrange("(j p) d -> p j d", p=128), lsem)
            c_load = p.c(lsem)
            p.wait("pe", lsem, c_load)
            for (src, dstf) in ((qtok, qfm), (ktok, kfm)):
                for g in range(2):
                    bank = pt_ps[g % 2]
                    fe = c_ptsb_free_eng[g % 2]
                    if fe is not None:
                        p.wait("pe", fe, c_ptsb_free[g % 2])
                    for i in range(8):
                        jj = g * 8 + i
                        cpe = p.op("pe", lambda E, b=bank, ii=i, s_=src, j_=jj, hb_=hb:
                                   E.transpose(
                            b[:, ii * 128:(ii + 1) * 128],
                            s_[:, hb_ * S + j_ * 128: hb_ * S + (j_ + 1) * 128],
                            ident_b[:, :]), sPe)
                    p.wait("act", sPe, cpe)
                    cev = p.op("act", lambda E, b=bank, d_=dstf, g_=g: E.copy(
                        d_[:, g_ * 1024:(g_ + 1) * 1024], b[:, :]), sAc)
                    c_ptsb_free[g % 2] = cev
                    c_ptsb_free_eng[g % 2] = sAc
            c_qk_done = p.c(sAc)
            p.wait("pe", sAc, c_qk_done)

            c_comb = [[0] * 4 for _ in range(NQ)]
            c_av = [0] * NQ
            c_avev = [0] * NQ
            c_sb_exp = [0] * NQ
            c_pt_read = [0] * NQ

            def emit_S_and_combine(j):
                kmax = 128 * (j + 1)
                nkc = (kmax + 511) // 512
                if j >= 2:
                    p.wait("pe", sDv, c_comb[j - 2][(128 * (j - 1) + 511) // 512 - 1])
                else:
                    p.wait("pe", sDv, prev_comb)
                off = 2048 - j * 128
                if j >= 2:
                    p.wait("dve", sAc, c_sb_exp[j - 2])
                else:
                    p.wait("dve", sAc, prev_exp[j % 2])
                for kc in range(nkc):
                    n = min(512, kmax - kc * 512)
                    sreg = (j % 2) * 1024 + (kc % 2) * 512
                    if kc >= 2:
                        p.wait("pe", sDv, c_comb[j][kc - 2])
                    cpe = p.op("pe", lambda E, sr=sreg, j_=j, k_=kc, n_=n: E.matmul(
                        s_ps[:, sr:sr + n_],
                        qfm[:, j_ * 128:(j_ + 1) * 128],
                        kfm[:, k_ * 512:k_ * 512 + n_], start=True, stop=True), sPe)
                    p.wait("dve", sPe, cpe)
                    c_comb[j][kc] = p.op("dve", lambda E, sr=sreg, j_=j, k_=kc, n_=n, o=off:
                                         E.scalar_tensor_tensor(
                        sbS[:, (j_ % 2) * S + k_ * 512:(j_ % 2) * S + k_ * 512 + n_],
                        s_ps[:, sr:sr + n_], SCALE,
                        strip_h[:, o + k_ * 512:o + k_ * 512 + n_],
                        op0=ALU.mult, op1=ALU.add), sDv)

            c_negm = [0] * NQ

            def emit_max_negm(j):
                kmax = 128 * (j + 1)
                if j >= 2:
                    p.wait("dve", sPe, c_pt_read[j - 2])
                else:
                    p.wait("dve", sPe, prev_pt[j % 2])
                p.wait("dve", sDv, c_comb[j][(kmax + 511) // 512 - 1])
                p.opw("dve", lambda E, j_=j, km=kmax: E.tensor_scalar(
                    p_b[:, (j_ % 2) * S:(j_ % 2) * S + km],
                    sbS[:, (j_ % 2) * S:(j_ % 2) * S + km], 1.0, MASKNEG,
                    op0=ALU.mult, op1=ALU.max, accum_out=l_all[:, j_:j_ + 1]), sDv)
                c_negm[j] = p.op("dve", lambda E, j_=j: E.tensor_scalar(
                    negm[:, j_ % 2:j_ % 2 + 1], l_all[:, j_:j_ + 1], -1.0, None,
                    op0=ALU.mult), sDv)

            def emit_exp(j):
                kmax = 128 * (j + 1)
                p.wait("act", sDv, c_negm[j])
                ce = p.op("act", lambda E, j_=j, km=kmax: E.activation(
                    p_b[:, (j_ % 2) * S:(j_ % 2) * S + km],
                    sbS[:, (j_ % 2) * S:(j_ % 2) * S + km], AF.Exp,
                    bias=negm[:, j_ % 2:j_ % 2 + 1],
                    accum_out=l_all[:, j_:j_ + 1]), sAc)
                c_sb_exp[j] = ce
                return ce

            def emit_pt_av(j, c_exp):
                nkt = j + 1
                p.wait("pe", sAc, c_exp)
                ngr = (nkt + 7) // 8
                for g in range(ngr):
                    bank = pt_ps[g % 2]
                    fe = c_ptsb_free_eng[g % 2]
                    if fe is not None:
                        p.wait("pe", fe, c_ptsb_free[g % 2])
                    lo, hi = g * 8, min(nkt, g * 8 + 8)
                    for kt in range(lo, hi):
                        cpe = p.op("pe", lambda E, b=bank, k_=kt, j_=j: E.transpose(
                            b[:, (k_ % 8) * 128:(k_ % 8 + 1) * 128],
                            p_b[:, (j_ % 2) * S + k_ * 128:(j_ % 2) * S + (k_ + 1) * 128],
                            ident_b[:, :]), sPe)
                    p.wait("dve", sPe, cpe)
                    p.wait("dve", sPe, c_av[j - 2] if j >= 2 else prev_av)
                    cev = p.op("dve", lambda E, b=bank, lo_=lo, hi_=hi, j_=j: E.tensor_copy(
                        pt_sb[:, (j_ % 2) * S + lo_ * 128:(j_ % 2) * S + hi_ * 128],
                        b[:, 0:(hi_ - lo_) * 128]), sDv)
                    c_ptsb_free[g % 2] = cev
                    c_ptsb_free_eng[g % 2] = sDv
                    p.wait("pe", sDv, cev)
                    if g == 0:
                        p.wait("pe", sAc, c_avev[j - 2] if j >= 2 else prev_avev[j % 2])
                    for kt in range(lo, hi):
                        cav = p.op("pe", lambda E, k_=kt, j_=j, hb_=hb: E.matmul(
                            av_ps[:, (j_ % 2) * 512:(j_ % 2) * 512 + 128],
                            vtok[:, hb_ * S + k_ * 128:hb_ * S + (k_ + 1) * 128],
                            pt_sb[:, (j_ % 2) * S + k_ * 128:(j_ % 2) * S + (k_ + 1) * 128],
                            start=(k_ == 0), stop=(k_ == j_),
                            skip_group_check=True), sPe)
                c_pt_read[j] = cpe
                c_av[j] = cav
                return cav

            def emit_avevac(j):
                p.wait("act", sPe, c_av[j])
                p.wait("act", sDv, c_norm_prev)
                c_avev[j] = p.op("act", lambda E, j_=j: E.copy(
                    avtmp[:, j_ * 128:(j_ + 1) * 128],
                    av_ps[:, (j_ % 2) * 512:(j_ % 2) * 512 + 128]), sAc)

            emit_S_and_combine(0)
            emit_max_negm(0)
            for j in range(NQ):
                if j + 1 < NQ:
                    emit_S_and_combine(j + 1)
                    emit_max_negm(j + 1)
                if j >= 2:
                    emit_avevac(j - 2)
                ce = emit_exp(j)
                emit_pt_av(j, ce)
            emit_avevac(NQ - 2)
            emit_avevac(NQ - 1)
            prev_comb = c_comb[NQ - 1][3]
            prev_exp = [c_sb_exp[NQ - 2], c_sb_exp[NQ - 1]]
            prev_pt = [c_pt_read[NQ - 2], c_pt_read[NQ - 1]]
            prev_av = c_av[NQ - 1]
            prev_avev = [c_avev[NQ - 2], c_avev[NQ - 1]]
            tok_free[hb] = c_av[NQ - 1]

            p.wait("dve", sAc, c_sb_exp[NQ - 1])
            p.wait("dve", sPe, c_lt_prev)
            c_li = p.opw("dve", lambda E: E.reciprocal(linv_all[:, :], l_all[:, :]), sDv)
            p.wait("pe", sDv, c_li)
            p.wait("pe", sDv, c_comb[NQ - 1][3])
            p.wait("pe", sXd, c_identf)
            c_lt = p.op("pe", lambda E: E.transpose(
                s_ps[0:16, 0:128], linv_all[:, :], ident_f[:, :]), sPe)
            c_lt_prev = c_lt
            p.wait("act", sPe, c_lt)
            p.wait("act", slr, c_lrow_prev)
            c_lte = p.op("act", lambda E: E.copy(lT_sb[:, :], s_ps[0:16, 0:128]), sAc)
            p.wait("sp", sAc, c_lte)
            c_lst = p.dma("sp", lrow_d[:, :], lT_sb[:, :], sSt)
            p.wait("sp", sSt, c_lst)
            p.wait("sp", sDv, c_norm_prev)
            c_lrow = p.dma("sp", linvrow[:, :].rearrange("p (j q) -> p j q", q=128),
                           lrow_d[:, :].partition_broadcast(128), slr)
            c_lrow_prev = c_lrow
            p.wait("dve", slr, c_lrow)
            p.wait("dve", sAc, c_avev[NQ - 1])
            c_norm_prev = p.op("dve", lambda E, hh=h: E.tensor_tensor(
                attnout[:, hh * S:(hh + 1) * S], avtmp[:, :],
                linvrow[:, :], op=ALU.mult), sDv)

        ph2.close()

        if DEBUG:
            phase_barrier()
            p.dma("sp", dbg_attno[:, :], attnout[:, :], sSt)
            p.dma("sp", dbg_small[:, 0:NQ], l_all[:, :], sSt)
            p.dma("sp", dbg_small[:, NQ:2 * NQ], linv_all[:, :], sSt)
            p.dma("sp", dbg_small[:, 32:32 + NQ],
                  linvrow[:, :].rearrange("p (j q) -> p j q", q=128)[:, :, 0:1].squeeze(2),
                  sSt, allow_slow_non_contiguous=True)
            p.wait("sp", sSt, p.c(sSt))
        phase_barrier()
        ph3 = ExitStack()
        psb3 = lambda n, s, d=F32: ph3.enter_context(nc.sbuf_tensor(n, s, d))
        pps3 = lambda n, s, d=F32: ph3.enter_context(nc.psum_tensor(n, s, d))
        stp = psb3("p_stp", [128, 8 * FB])
        bprojh = psb3("p_bh", [128, D])
        acc3 = pps3("p_acc", [128, 4096])
        c_bh = 0
        for half in range(2):
            if half == 1:
                p.wait("sp", sDv, c_bh)
            cb = p.dma("sp", stp[:, 0:2048],
                       bproj[0:1, half * 2048:(half + 1) * 2048].partition_broadcast(
                           128).squeeze(1), sPd)
            p.wait("dve", sPd, cb)
            c_bh = p.op("dve", lambda E, hh=half: E.tensor_scalar(
                bprojh[:, hh * 2048:(hh + 1) * 2048], stp[:, 0:2048], 0.5, None,
                op0=ALU.mult), sDv)
        NBP = D // FB
        PFI = 16
        wfree = {0: 0, 1: 0}
        accfree = [0] * 8
        stfree = [0] * 4
        cnt = 0
        for bi in range(NBP):
            c0 = bi * FB
            wb = wblk[:, (bi % 2) * WBW:(bi % 2) * WBW + PFI * FB]
            p.wait("pool", sPe, wfree[bi % 2])
            cw_sem = sW2[bi % 2]
            cw = p.dma("pool", wb.rearrange("p (c f) -> p c f", c=PFI),
                       wproj[:, c0:c0 + FB].rearrange("(c p) f -> p c f", p=128), cw_sem)
            for qj in range(NQ):
                reg = acc3[:, (qj % 8) * 512:(qj % 8) * 512 + FB]
                p.wait("pe", cw_sem, cw)
                p.wait("pe", sDv, accfree[qj % 8])
                for fi in range(PFI):
                    lhsT = attnout[:, (fi * NQ + qj) * 128:(fi * NQ + qj + 1) * 128]
                    c_mm = p.op("pe", lambda E, rg=reg, l=lhsT, w=wb, f=fi: E.matmul(
                        rg, l, w[:, f * FB:(f + 1) * FB], start=(f == 0),
                        stop=(f == PFI - 1), skip_group_check=True), sPe)
                wfree[bi % 2] = c_mm
                slot = cnt % 4
                cnt += 1
                p.wait("dve", sPe, c_mm)
                p.wait("dve", sDv, c_bh)
                p.wait("dve", sst4[slot], stfree[slot])
                cev = p.op("dve", lambda E, s_=slot, rg=reg, cc=c0: E.scalar_tensor_tensor(
                    stp[:, s_ * FB:(s_ + 1) * FB], rg, 1.0, bprojh[:, cc:cc + FB],
                    op0=ALU.mult, op1=ALU.add), sDv)
                accfree[qj % 8] = cev
                p.wait("sp", sDv, cev)
                rsd = rs_in[c0 // 2048]
                cc2 = c0 % 2048
                cst = p.dma("sp", rsd[qj * 128:(qj + 1) * 128, cc2:cc2 + FB],
                            stp[:, slot * FB:(slot + 1) * FB], sst4[slot])
                stfree[slot] = cst
        c_all_st4 = [p.c(s_) for s_ in sst4]
        if DEBUG:
            for i_, s_ in enumerate(sst4):
                p.wait("sp", s_, c_all_st4[i_])
            p.dma("sp", dbg_rsina[:, :], rs_in[0][:, :], sSt)
            p.dma("sp", dbg_rsinb[:, :], rs_in[1][:, :], sSt)
        for i_, s_ in enumerate(sst4):
            p.wait("pool", s_, c_all_st4[i_])
        for half in range(2):
            p.op("pool", lambda E, hh=half: E.collective_compute(
                "ReduceScatter", ALU.add, replica_groups=groups,
                ins=[rs_in[hh][:, :].opt()],
                outs=[rs_out[hh][:, :].opt()]), sCc)
        c_rs = p.c(sCc)
        ph3.close()
        return dict(c_rs=c_rs)

    d1 = ffn_sublayer(x0, delta1, add_rs=False, delta_mode=True)
    phase_barrier()
    if DEBUG:
        p.dma("sp", dbg_xres1[:, :], x_res[:, :], sSt)
    at = attn_sublayer(d1)
    phase_barrier()
    p.wait("sp", sCc, at["c_rs"])
    if DEBUG:
        p.dma("sp", dbg_qkvall[:, :], qkv_all[:, :], sSt)
        p.dma("sp", dbg_rsa[:, :], rs_out[0][:, :], sSt)
        p.dma("sp", dbg_rsb[:, :], rs_out[1][:, :], sSt)
    ffn_sublayer(x_res, y, add_rs=True)
    if DEBUG:
        p.dma("sp", dbg_xres2[:, :], x_res[:, :], sSt)
    p.wait("sp", sSt, p.c(sSt))

    with nc.Block() as block:
        @block.tensor
        def _(E):
            for f in p.q["pe"]:
                f(E)

        @block.scalar
        def _(E):
            for f in p.q["act"]:
                f(E)

        @block.vector
        def _(E):
            for f in p.q["dve"]:
                f(E)

        @block.gpsimd
        def _(E):
            for f in p.q["pool"]:
                f(E)

        @block.sync
        def _(E):
            for f in p.q["sp"]:
                f(E)

    es.close()
    return nc


def _host_inputs(L0_attn_hidden_states, L0_attn_w_attn, L0_attn_b_attn, L0_attn_w_proj,
                 L0_attn_b_proj, L0_ffn_base_W, L0_ffn_gate_W, L0_ffn_lora_A, L0_ffn_lora_B):
    x = np.ascontiguousarray(np.asarray(L0_attn_hidden_states, dtype=np.float32))
    w_attn = np.asarray(L0_attn_w_attn, dtype=np.float32)
    b_attn = np.asarray(L0_attn_b_attn, dtype=np.float32)
    w_proj = np.asarray(L0_attn_w_proj, dtype=np.float32)
    b_proj = np.asarray(L0_attn_b_proj, dtype=np.float32)
    base_W = np.asarray(L0_ffn_base_W, dtype=np.float32)
    gate_W = np.asarray(L0_ffn_gate_W, dtype=np.float32)
    lora_A = np.asarray(L0_ffn_lora_A, dtype=np.float32)
    lora_B = np.asarray(L0_ffn_lora_B, dtype=np.float32)

    bwt = np.ascontiguousarray(base_W.T)
    gwt = np.ascontiguousarray(gate_W.T)
    lat = np.ascontiguousarray(lora_A.reshape(ER, D).T)
    lbp = np.ascontiguousarray(lora_B.transpose(0, 2, 1).reshape(ER, D))

    u = np.arange(2176, dtype=np.float32)[None, :]
    i = np.arange(128, dtype=np.float32)[:, None]
    rel = u - 2048.0 - i
    strip = np.where(rel <= 0, rel, np.float32(MASKNEG)).astype(np.float32)
    ident = np.eye(128, dtype=np.float32)
    slopes_full = _alibi_slopes(32)

    in_maps = []
    for c in range(8):
        b, r = c // 2, c % 2
        my = np.arange(r * 16, r * 16 + 16)
        peer = np.arange((1 - r) * 16, (1 - r) * 16 + 16)

        def cols(heads):
            qs = np.concatenate([np.arange(h * 128, (h + 1) * 128) for h in heads])
            return np.concatenate([qs, D + qs, 2 * D + qs])

        perm = np.concatenate([cols(my), cols(peer)])
        in_maps.append(dict(
            x0=np.ascontiguousarray(x[b, r * T:(r + 1) * T, :]),
            wattn=np.ascontiguousarray(w_attn[:, perm]),
            battn=np.ascontiguousarray(b_attn[perm])[None, :],
            wproj=np.ascontiguousarray(w_proj[r * 2048:(r + 1) * 2048, :]),
            bproj=b_proj[None, :].copy(),
            bwt=bwt, gwt=gwt, lat=lat, lbp=lbp,
            slopes_i=np.ascontiguousarray(np.broadcast_to(
                slopes_full[my][None, :], (128, 16))),
            strip_i=strip, ident_i=ident,
            rank_i=np.array([[r]], dtype=np.int32),
        ))
    return in_maps


_NC_CACHE = {}


def kernel(**inputs):
    in_maps = _host_inputs(**inputs)
    if "nc" not in _NC_CACHE:
        _NC_CACHE["nc"] = build_nc()
    nc = _NC_CACHE["nc"]
    res = run_bass_kernel_spmd(nc, in_maps, core_ids=list(range(8)))
    out = np.empty((4, 2048, 4096), dtype=np.float32)
    for c in range(8):
        b, r = c // 2, c % 2
        out[b, r * T:(r + 1) * T, :] = res.results[c]["y"]
    return out


# version 10
# speedup vs baseline: 1.0720x; 1.0423x over previous
import numpy as np
import math
import os
from contextlib import ExitStack
DEBUG = bool(int(os.environ.get("MK_DEBUG", "0")))

import concourse.bass as bass
import concourse.mybir as mybir
from concourse.bass_utils import run_bass_kernel_spmd

F32 = mybir.dt.float32
BF16 = mybir.dt.bfloat16
I32 = mybir.dt.int32
AF = mybir.ActivationFunctionType
ALU = mybir.AluOpType
AX = mybir.AxisListType

D = 4096
T = 1024
NT = 8
S = 2048
NQ = 16
H = 16
FI = 32
NE = 28
ER = 224
SCALE = 1.0 / 128.0
EPS = 1e-5
MASKNEG = -1e30
FB = 256
NACC = 6

ENGS = ("pe", "act", "dve", "pool", "sp")


def _alibi_slopes(n_heads):
    cp2 = 2 ** math.floor(math.log2(n_heads))
    base = 2 ** (-(2 ** (-(math.log2(cp2) - 3))))
    slopes = base ** np.arange(1, cp2 + 1, dtype=np.float32)
    if cp2 != n_heads:
        extra_base = 2 ** (-(2 ** (-(math.log2(2 * cp2) - 3))))
        n_rem = min(cp2, n_heads - cp2)
        extra = extra_base ** np.arange(1, 1 + 2 * n_rem, 2, dtype=np.float32)
        slopes = np.concatenate([slopes, extra])
    return slopes.astype(np.float32)


class P:
    def __init__(self, nc):
        self.nc = nc
        self.q = {e: [] for e in ENGS}
        self.cnt = {}

    def emit(self, eng, fn):
        self.q[eng].append(fn)

    def wait(self, eng, sem, val):
        if val and val > 0:
            self.emit(eng, lambda E, s=sem, v=val: E.wait_ge(s, v))

    def op(self, eng, fn, sem=None, n=1):
        if sem is None:
            self.emit(eng, fn)
            return None
        self.cnt[sem] = self.cnt.get(sem, 0) + n
        c = self.cnt[sem]
        self.emit(eng, lambda E, f=fn, s=sem, k=n: f(E).then_inc(s, k))
        return c

    def dma(self, eng, out, in_, sem, **kw):
        return self.op(eng, lambda E, o=out, i=in_, k=dict(kw): E.dma_start(out=o, in_=i, **k),
                       sem=sem, n=16)

    def c(self, sem):
        return self.cnt.get(sem, 0)

    def opw(self, eng, fn, sem):
        c = self.op(eng, fn, sem)
        self.wait(eng, sem, c)
        return c


def build_nc():
    nc = bass.Bass("TRN2", target_bir_lowering=False)

    x0 = nc.dram_tensor("x0", [T, D], F32, kind="ExternalInput")
    wattn = nc.dram_tensor("wattn", [D, 3 * D], F32, kind="ExternalInput")
    battn = nc.dram_tensor("battn", [1, 3 * D], F32, kind="ExternalInput")
    wproj = nc.dram_tensor("wproj", [2048, D], F32, kind="ExternalInput")
    bproj = nc.dram_tensor("bproj", [1, D], F32, kind="ExternalInput")
    bwt = nc.dram_tensor("bwt", [D, D], F32, kind="ExternalInput")
    gwt = nc.dram_tensor("gwt", [D, NE], F32, kind="ExternalInput")
    lat = nc.dram_tensor("lat", [D, ER], F32, kind="ExternalInput")
    lbp = nc.dram_tensor("lbp", [ER, D], F32, kind="ExternalInput")
    slopes_i = nc.dram_tensor("slopes_i", [128, H], F32, kind="ExternalInput")
    strip_i = nc.dram_tensor("strip_i", [128, 2176], F32, kind="ExternalInput")
    ident_i = nc.dram_tensor("ident_i", [128, 128], F32, kind="ExternalInput")
    rank_i = nc.dram_tensor("rank_i", [1, 1], I32, kind="ExternalInput")
    y = nc.dram_tensor("y", [T, D], F32, kind="ExternalOutput")

    if DEBUG:
        dbg_xres1 = nc.dram_tensor("dbg_xres1", [T, D], F32, kind="ExternalOutput")
        dbg_xnT = nc.dram_tensor("dbg_xnT", [128, NT * FI * 128], BF16, kind="ExternalOutput")
        dbg_stats = nc.dram_tensor("dbg_stats", [128, 8 * NT], F32, kind="ExternalOutput")
        dbg_xin = nc.dram_tensor("dbg_xin", [128, 2 * D], F32, kind="ExternalOutput")
        dbg_xn = nc.dram_tensor("dbg_xn", [128, 2 * D], BF16, kind="ExternalOutput")
        dbg_hgT1 = nc.dram_tensor("dbg_hgT1", [128, NT * 128], BF16, kind="ExternalOutput")
        dbg_wblk = nc.dram_tensor("dbg_wblk", [128, 2 * FI * FB], BF16, kind="ExternalOutput")
        dbg_qkvall = nc.dram_tensor("dbg_qkvall", [S, 6144], BF16, kind="ExternalOutput")
        dbg_rsa = nc.dram_tensor("dbg_rsa", [T, 2048], F32, kind="ExternalOutput")
        dbg_rsb = nc.dram_tensor("dbg_rsb", [T, 2048], F32, kind="ExternalOutput")
        dbg_xres2 = nc.dram_tensor("dbg_xres2", [T, D], F32, kind="ExternalOutput")
        dbg_attno = nc.dram_tensor("dbg_attno", [128, NT * FI * 128], BF16, kind="ExternalOutput")
        dbg_rsina = nc.dram_tensor("dbg_rsina", [S, 2048], F32, kind="ExternalOutput")
        dbg_rsinb = nc.dram_tensor("dbg_rsinb", [S, 2048], F32, kind="ExternalOutput")
        dbg_small = nc.dram_tensor("dbg_small", [128, 64], F32, kind="ExternalOutput")
    x_res = nc.dram_tensor("x_res", [T, D], F32)
    qkv_own = nc.dram_tensor("qkv_own", [T, 6144], BF16)
    cc_in = nc.dram_tensor("cc_in", [T, 6144], BF16)
    cc_out = nc.dram_tensor("cc_out", [S, 6144], BF16)
    delta1 = nc.dram_tensor("delta1", [T, D], F32)
    rs_in = [nc.dram_tensor("rs_in_a", [S, 2048], F32),
             nc.dram_tensor("rs_in_b", [S, 2048], F32)]
    rs_out = [nc.dram_tensor("rs_out_a", [T, 2048], F32),
              nc.dram_tensor("rs_out_b", [T, 2048], F32)]
    battn_bf = nc.dram_tensor("battn_bf", [1, 3 * D], BF16)
    lrow_d = nc.dram_tensor("lrow_d", [16, 128], F32)
    qkv_all = nc.dram_tensor("qkv_all", [S, 6144], BF16)

    groups = [[0, 1], [2, 3], [4, 5], [6, 7]]

    es = ExitStack()
    sb = lambda n, s, d=F32: es.enter_context(nc.sbuf_tensor(n, s, d))
    sem = lambda n: es.enter_context(nc.semaphore(n))

    sWd = sem("sWd")
    sXd = sem("sXd")
    sSt = sem("sSt")
    sPe = sem("sPe")
    sAc = sem("sAc")
    sDv = sem("sDv")
    sCc = sem("sCc")
    sPd = sem("sPd")
    sXo = sem("sXo")
    sW2 = [sem("sW20"), sem("sW21")]
    sL2 = [sem("sL20"), sem("sL21")]
    sxo4 = [sem(f"sxo{i}") for i in range(4)]
    sxi2 = [sem("sxi0"), sem("sxi1")]
    srs = sem("srs")
    slr = sem("slr")
    sbt = sem("sbt")
    sst4 = [sem(f"sst{i}") for i in range(4)]
    sxr = sem("sxr")
    scst = sem("scst")
    ALLSEMS = tuple([sWd, sXd, sSt, sPe, sAc, sDv, sCc, sPd, sXo]
                    + sW2 + sL2 + sxo4 + sxi2 + [srs, slr, sbt, sxr, scst] + sst4)

    p = P(nc)

    def phase_barrier(skip=()):
        vals = [(s, p.c(s)) for s in ALLSEMS if s not in skip]
        for eng in ENGS:
            for s, v in vals:
                p.wait(eng, s, v)

    ident_b = sb("ident_b", [128, 128], BF16)
    ident_f = sb("ident_f", [128, 128], F32)
    slopes_s = sb("slopes_s", [128, H])
    xnT = sb("xnT", [128, NT * FI * 128], BF16)
    wblk = sb("wblk", [128, 2 * FI * FB], BF16)
    st_sum = sb("st_sum", [128, NT])
    st_ssq = sb("st_ssq", [128, NT])
    st_mu = sb("st_mu", [128, NT])
    st_e2 = sb("st_e2", [128, NT])
    st_m2 = sb("st_m2", [128, NT])
    st_var = sb("st_var", [128, NT])
    st_ln = sb("st_ln", [128, NT])
    st_rstd = sb("st_rstd", [128, NT])

    WBW = FI * FB

    p.dma("pool", ident_b[:, :], ident_i[:, :], scst)
    p.dma("pool", slopes_s[:, :], slopes_i[:, :], scst)
    c_const = p.c(scst)
    c_identf = p.dma("sp", ident_f[:, :], ident_i[:, :], sXd)

    regctx = nc.sync.register(name="rankreg")
    rreg = regctx.__enter__()
    rv_holder = {}

    def _loadrank(E):
        E.reg_load(rreg, rank_i[0:1, 0:1])
        rv_holder["rv"] = E.snap(rreg, min_val=0, max_val=1)
        return E.nop()

    p.emit("sp", _loadrank)
    p.wait("pe", scst, c_const)

    def pred_dma(out, in_, sem_, half, eq):
        p.cnt[sem_] = p.cnt.get(sem_, 0) + 16
        c = p.cnt[sem_]

        def f(E, o=out, i=in_, s=sem_, h=half, e=eq):
            rv = rv_holder["rv"]
            cond = (rv == h) if e else (rv != h)
            return E.dma_start(out=o, in_=i, cond=cond).then_inc(s, 16)

        p.emit("sp", f)
        return c

    def prepass_tile(bufs, t, x_src, add_rs, ffn, track):
        xin, xn_b = bufs["xin"], bufs["xn_b"]
        tp_ps = bufs["tp_ps"]
        slot = t % 2
        xsl = xin[:, slot * D:(slot + 1) * D]
        xnsl = xn_b[:, slot * D:(slot + 1) * D]
        p.wait("sp", sDv, track["slot_free"][slot])
        cx = p.dma("sp", xsl, x_src[t * 128:(t + 1) * 128, :], sxi2[slot])
        if add_rs:
            rsl = xin[:, 2 * D:3 * D]
            p.wait("sp", sDv, track["rs_free"])
            if bufs.get("add_delta"):
                cr = p.dma("sp", rsl, delta1[t * 128:(t + 1) * 128, :], srs)
            else:
                p.dma("sp", xin[:, 2 * D:2 * D + 2048],
                      rs_out[0][t * 128:(t + 1) * 128, :], srs)
                cr = p.dma("sp", xin[:, 2 * D + 2048:3 * D],
                           rs_out[1][t * 128:(t + 1) * 128, :], srs)
            p.wait("dve", srs, cr)
            p.wait("dve", sxi2[slot], cx)
            c_add = p.opw("dve", lambda E, xs=xsl, rs=rsl, tt=t: E.scalar_tensor_tensor(
                xs, xs, 1.0, rs, op0=ALU.mult, op1=ALU.add,
                accum_out=st_sum[:, tt:tt + 1]), sDv)
            track["rs_free"] = c_add
            p.wait("sp", sDv, c_add)
            cst = p.dma("sp", x_res[t * 128:(t + 1) * 128, :], xsl, sxr)
            track["xres_st"][t] = cst
        else:
            p.wait("dve", sxi2[slot], cx)
            c_add = p.opw("dve", lambda E, xs=xsl, tt=t: E.reduce_sum(
                st_sum[:, tt:tt + 1], xs, axis=AX.X), sDv)
        p.wait("dve", sPe, track["xn_free"][slot])
        p.opw("dve", lambda E, xs=xsl, o=xnsl, tt=t: E.scalar_tensor_tensor(
            o, xs, 1.0, xs, op0=ALU.mult, op1=ALU.mult,
            accum_out=st_ssq[:, tt:tt + 1]), sDv)
        p.opw("dve", lambda E, tt=t: E.tensor_scalar(
            st_mu[:, tt:tt + 1], st_sum[:, tt:tt + 1], 1.0 / D, None, op0=ALU.mult), sDv)
        p.opw("dve", lambda E, tt=t: E.tensor_scalar(
            st_e2[:, tt:tt + 1], st_ssq[:, tt:tt + 1], 1.0 / D, None, op0=ALU.mult), sDv)
        p.opw("dve", lambda E, tt=t: E.tensor_tensor(
            st_m2[:, tt:tt + 1], st_mu[:, tt:tt + 1], st_mu[:, tt:tt + 1], op=ALU.mult), sDv)
        p.opw("dve", lambda E, tt=t: E.scalar_tensor_tensor(
            st_var[:, tt:tt + 1], st_m2[:, tt:tt + 1], -1.0, st_e2[:, tt:tt + 1],
            op0=ALU.mult, op1=ALU.add), sDv)
        c_var = p.opw("dve", lambda E, tt=t: E.tensor_scalar(
            st_var[:, tt:tt + 1], st_var[:, tt:tt + 1], 1.0, EPS,
            op0=ALU.mult, op1=ALU.add), sDv)
        p.wait("act", sDv, c_var)
        p.opw("act", lambda E, tt=t: E.activation(
            st_ln[:, tt:tt + 1], st_var[:, tt:tt + 1], AF.Ln), sAc)
        c_rstd = p.op("act", lambda E, tt=t: E.activation(
            st_rstd[:, tt:tt + 1], st_ln[:, tt:tt + 1], AF.Exp, scale=-0.5), sAc)
        p.wait("dve", sAc, c_rstd)
        c_xn = p.op("dve", lambda E, xs=xsl, o=xnsl, tt=t: E.tensor_scalar(
            o, xs, st_mu[:, tt:tt + 1], st_rstd[:, tt:tt + 1],
            op0=ALU.subtract, op1=ALU.mult), sDv)
        track["slot_free"][slot] = c_xn
        p.wait("pe", sDv, c_xn)
        last_ev = 0
        for g in range(4):
            bank = tp_ps[g % 2]
            p.wait("pe", sAc, track["tp_free"][g % 2])
            for i in range(8):
                fi = g * 8 + i
                cpe = p.op("pe", lambda E, b=bank, ii=i, f=fi, o=xnsl: E.transpose(
                    b[:, ii * 128:(ii + 1) * 128], o[:, f * 128:(f + 1) * 128],
                    ident_b[:, :]), sPe)
            p.wait("act", sPe, cpe)
            cev = p.op("act", lambda E, b=bank, tt=t, g_=g: E.copy(
                xnT[:, (tt * FI + g_ * 8) * 128:(tt * FI + g_ * 8 + 8) * 128], b[:, :]), sAc)
            track["tp_free"][g % 2] = cev
            last_ev = cev
        track["xn_free"][slot] = cpe
        track["xnT_done"][t] = last_ev
        if not ffn:
            return
        latb, gwtb, hg = bufs["latb"], bufs["gwtb"], bufs["hg"]
        hgacc = bufs["acc"][:, 0:ER + NE]
        p.wait("pe", sAc, last_ev)
        p.wait("pe", sDv, track["hg_free"])
        p.wait("pe", sWd, track["c_lora"])
        for fi in range(FI):
            lhsT = xnT[:, (t * FI + fi) * 128:(t * FI + fi + 1) * 128]
            p.op("pe", lambda E, l=lhsT, f=fi, a=hgacc: E.matmul(
                a[:, 0:ER], l, latb[:, f * ER:(f + 1) * ER],
                start=(f == 0), stop=(f == FI - 1), skip_group_check=True), sPe)
            c_hg = p.op("pe", lambda E, l=lhsT, f=fi, a=hgacc: E.matmul(
                a[:, ER:ER + NE], l, gwtb[:, f * NE:(f + 1) * NE],
                start=(f == 0), stop=(f == FI - 1), skip_group_check=True), sPe)
        ga, m1, m2c = bufs["ga"], bufs["m1"], bufs["m2"]
        p.wait("dve", sPe, c_hg)
        p.opw("dve", lambda E, a=hgacc: E.reduce_max(m1[:, 0:1], a[:, ER:ER + NE], axis=AX.X), sDv)
        p.opw("dve", lambda E, a=hgacc: E.tensor_scalar(
            ga[:, 0:NE], a[:, ER:ER + NE], m1[:, 0:1], None, op0=ALU.is_ge), sDv)
        p.opw("dve", lambda E, a=hgacc: E.scalar_tensor_tensor(
            ga[:, 32:32 + NE], ga[:, 0:NE], MASKNEG, a[:, ER:ER + NE],
            op0=ALU.mult, op1=ALU.add), sDv)
        p.opw("dve", lambda E: E.reduce_max(m2c[:, 0:1], ga[:, 32:32 + NE], axis=AX.X), sDv)
        p.opw("dve", lambda E: E.tensor_scalar(
            ga[:, 32:32 + NE], ga[:, 32:32 + NE], m2c[:, 0:1], None, op0=ALU.is_ge), sDv)
        c_d = p.opw("dve", lambda E: E.tensor_sub(m2c[:, 1:2], m1[:, 0:1], m2c[:, 0:1]), sDv)
        p.wait("act", sDv, c_d)
        c_e = p.op("act", lambda E: E.activation(m1[:, 1:2], m2c[:, 1:2], AF.Exp,
                                                 scale=-1.0), sAc)
        p.wait("dve", sAc, c_e)
        p.opw("dve", lambda E: E.tensor_scalar(m1[:, 2:3], m1[:, 1:2], 1.0, 1.0,
                                               op0=ALU.mult, op1=ALU.add), sDv)
        p.opw("dve", lambda E: E.reciprocal(m2c[:, 2:3], m1[:, 2:3]), sDv)
        p.opw("dve", lambda E: E.tensor_scalar(m1[:, 3:4], m2c[:, 2:3], 2.0, None,
                                               op0=ALU.mult), sDv)
        p.opw("dve", lambda E: E.tensor_tensor(m2c[:, 3:4], m1[:, 1:2], m1[:, 3:4],
                                               op=ALU.mult), sDv)
        p.opw("dve", lambda E: E.tensor_scalar(ga[:, 0:NE], ga[:, 0:NE], m1[:, 3:4], None,
                                               op0=ALU.mult), sDv)
        p.opw("dve", lambda E: E.scalar_tensor_tensor(
            ga[:, 0:NE], ga[:, 32:32 + NE], m2c[:, 3:4], ga[:, 0:NE],
            op0=ALU.mult, op1=ALU.add), sDv)
        gexp = ga[:, 0:NE].unsqueeze(2).broadcast_to([128, NE, 8])
        c_hgd = p.op("dve", lambda E, a=hgacc, ge=gexp: E.tensor_tensor(
            hg[:, :].rearrange("p (e r) -> p e r", r=8),
            a[:, 0:ER].rearrange("p (e r) -> p e r", r=8), ge, op=ALU.mult), sDv)
        track["hg_free"] = c_hgd
        hgT1, hgT2 = bufs["hgT1"], bufs["hgT2"]
        p.wait("pe", sDv, c_hgd)
        bank = tp_ps[0]
        p.wait("pe", sAc, track["tp_free"][0])
        p.op("pe", lambda E, b=bank: E.transpose(
            b[:, 0:128], hg[:, 0:128], ident_b[:, :]), sPe)
        c_t2 = p.op("pe", lambda E, b=bank: E.transpose(
            b[0:96, 128:256], hg[:, 128:224], ident_b[:, :]), sPe)
        p.wait("act", sPe, c_t2)
        p.op("act", lambda E, b=bank, tt=t: E.copy(
            hgT1[:, tt * 128:(tt + 1) * 128], b[:, 0:128]), sAc)
        c_ev2 = p.op("act", lambda E, b=bank, tt=t: E.copy(
            hgT2[:, tt * 128:(tt + 1) * 128], b[0:96, 128:256]), sAc)
        track["tp_free"][0] = c_ev2
        track["hgT_done"][t] = c_ev2

    def weight_sweep(bufs, phase, x_src, x_dst, track):
        acc = bufs["acc"]
        if phase == "qkv":
            nb_b = 6144 // FB
            order = list(range(nb_b, 2 * nb_b)) + list(range(nb_b))
            wsrc = wattn
        else:
            order = list(range(D // FB))
            wsrc = bwt
        stf, xold, stb = bufs.get("stf"), bufs.get("xold"), bufs.get("stb")
        battn_t = bufs.get("battn_t")
        hgT1, hgT2 = bufs.get("hgT1"), bufs.get("hgT2")
        lbp1, lbp2 = bufs.get("lbp1"), bufs.get("lbp2")
        p.wait("pe", sAc, max(track["xnT_done"]))
        if phase.startswith("ffn"):
            p.wait("pe", sAc, max(track["hgT_done"]))
        else:
            p.wait("dve", sbt, track["c_battn"])
        nslot = 4
        cnt = 0
        cc_store = None
        for bi, blk in enumerate(order):
            c0 = blk * FB
            wb = wblk[:, (bi % 2) * WBW:(bi % 2) * WBW + WBW]
            p.wait("pool", sPe, track["wblk_free"][bi % 2])
            cw_sem = sW2[bi % 2]
            cw = p.dma("pool", wb.rearrange("p (c f) -> p c f", c=FI),
                       wsrc[:, c0:c0 + FB].rearrange("(c p) f -> p c f", p=128), cw_sem)
            accsem = sAc if phase == "ffn_delta" else sDv
            for t in range(NT):
                r = t % NACC
                reg = acc[:, r * 512:r * 512 + FB]
                p.wait("pe", cw_sem, cw)
                p.wait("pe", accsem, track["acc_free"][r])
                for fi in range(FI):
                    lhsT = xnT[:, (t * FI + fi) * 128:(t * FI + fi + 1) * 128]
                    st = (fi == 0)
                    last = (fi == FI - 1) and not phase.startswith("ffn")
                    c_mm = p.op("pe", lambda E, rg=reg, l=lhsT, w=wb, f=fi, s=st, e=last:
                                E.matmul(rg, l, w[:, f * FB:(f + 1) * FB], start=s, stop=e,
                                         skip_group_check=True), sPe)
                if phase.startswith("ffn"):
                    p.op("pe", lambda E, rg=reg, tt=t, cc=c0: E.matmul(
                        rg, hgT1[:, tt * 128:(tt + 1) * 128], lbp1[:, cc:cc + FB],
                        start=False, stop=False, skip_group_check=True), sPe)
                    c_mm = p.op("pe", lambda E, rg=reg, tt=t, cc=c0: E.matmul(
                        rg, hgT2[:, tt * 128:(tt + 1) * 128], lbp2[:, cc:cc + FB],
                        start=False, stop=True, skip_group_check=True), sPe)
                track["wblk_free"][bi % 2] = c_mm
                slot = cnt % nslot
                cnt += 1
                if phase == "ffn_delta":
                    ssl = stf[:, slot * FB:(slot + 1) * FB]
                    p.wait("act", sPe, c_mm)
                    p.wait("act", sst4[slot], track["stf_free"][slot])
                    cev = p.op("act", lambda E, ss=ssl, rg=reg: E.activation(
                        ss, rg, AF.Copy, scale=0.5), sAc)
                    track["acc_free"][r] = cev
                    p.wait("sp", sAc, cev)
                    cst = p.dma("sp", x_dst[t * 128:(t + 1) * 128, c0:c0 + FB], ssl,
                                sst4[slot])
                    track["stf_free"][slot] = cst
                elif phase == "ffn":
                    xsl = xold[:, slot * FB:(slot + 1) * FB]
                    p.wait("act", sDv, track["xold_free"][slot])
                    if track["xres_wait"]:
                        p.wait("act", sxr, track["xres_wait"])
                    cxo = p.dma("act", xsl, x_src[t * 128:(t + 1) * 128, c0:c0 + FB],
                                sxo4[slot])
                    ssl = stf[:, slot * FB:(slot + 1) * FB]
                    p.wait("dve", sPe, c_mm)
                    p.wait("dve", sxo4[slot], cxo)
                    p.wait("dve", sst4[slot], track["stf_free"][slot])
                    cev = p.op("dve", lambda E, ss=ssl, rg=reg, xs=xsl: E.scalar_tensor_tensor(
                        ss, rg, 0.5, xs, op0=ALU.mult, op1=ALU.add), sDv)
                    track["acc_free"][r] = cev
                    track["xold_free"][slot] = cev
                    p.wait("sp", sDv, cev)
                    cst = p.dma("sp", x_dst[t * 128:(t + 1) * 128, c0:c0 + FB], ssl,
                                sst4[slot])
                    track["stf_free"][slot] = cst
                else:
                    ssl = stb[:, slot * FB:(slot + 1) * FB]
                    p.wait("dve", sPe, c_mm)
                    p.wait("dve", sst4[slot], track["stf_free"][slot])
                    cev = p.op("dve", lambda E, ss=ssl, rg=reg, cc=c0: E.scalar_tensor_tensor(
                        ss, rg, 1.0, battn_t[:, cc:cc + FB], op0=ALU.mult, op1=ALU.add), sDv)
                    track["acc_free"][r] = cev
                    p.wait("sp", sDv, cev)
                    if c0 < 6144:
                        dst = qkv_own[t * 128:(t + 1) * 128, c0:c0 + FB]
                    else:
                        dst = cc_in[t * 128:(t + 1) * 128, c0 - 6144:c0 - 6144 + FB]
                    cst = p.dma("sp", dst, ssl, sst4[slot])
                    track["stf_free"][slot] = cst
            if phase == "qkv" and bi == nb_b - 1:
                cc_store = [p.c(s_) for s_ in sst4]
        return cc_store

    def new_track(c_lora=0, c_battn=0, acc_init=0):
        return dict(slot_free={0: 0, 1: 0}, xn_free={0: 0, 1: 0}, tp_free={0: 0, 1: 0},
                    xnT_done=[0] * NT, hgT_done=[0] * NT, hg_free=0,
                    wblk_free={0: 0, 1: 0}, acc_free=[acc_init] * NACC,
                    xold_free=[0] * 4, stf_free=[0] * 4,
                    row_stored=[0] * NT, xres_st=[0] * NT, rs_free=0,
                    xres_wait=0, c_battn=c_battn, c_lora=c_lora)

    def ffn_sublayer(x_src, x_dst, add_rs, delta_mode=False):
        ph = ExitStack()
        psb = lambda n, s, d=F32: ph.enter_context(nc.sbuf_tensor(n, s, d))
        pps = lambda n, s, d=F32: ph.enter_context(nc.psum_tensor(n, s, d))
        sfx = "2" if add_rs else "1"
        nxin = 3 if add_rs else 2
        bufs = dict(
            xin=psb("f_xin" + sfx, [128, nxin * D]),
            xn_b=psb("f_xn" + sfx, [128, 2 * D], BF16),
            stf=psb("f_stf" + sfx, [128, 4 * FB]),
            xold=psb("f_xold" + sfx, [128, 4 * FB]),
            latb=psb("f_latb" + sfx, [128, FI * ER], BF16),
            gwtb=psb("f_gwtb" + sfx, [128, FI * NE], BF16),
            lbp1=psb("f_lbp1" + sfx, [128, D], BF16),
            lbp2=psb("f_lbp2" + sfx, [96, D], BF16),
            hgT1=psb("f_hgT1" + sfx, [128, NT * 128], BF16),
            hgT2=psb("f_hgT2" + sfx, [96, NT * 128], BF16),
            hg=psb("f_hg" + sfx, [128, ER], BF16),
            ga=psb("f_ga" + sfx, [128, 64]),
            m1=psb("f_m1" + sfx, [128, 4]),
            m2=psb("f_m2" + sfx, [128, 4]),
            tp_ps=[pps("f_tp0" + sfx, [128, 1024], BF16),
                   pps("f_tp1" + sfx, [128, 1024], BF16)],
            acc=pps("f_acc" + sfx, [128, NACC * 512]),
        )
        track = new_track()
        p.dma("pool", bufs["latb"][:, :].rearrange("p (c e) -> p c e", c=FI),
              lat[:, :].rearrange("(c p) e -> p c e", p=128), sWd)
        p.dma("pool", bufs["gwtb"][:, :].rearrange("p (c e) -> p c e", c=FI),
              gwt[:, :].rearrange("(c p) e -> p c e", p=128), sWd)
        p.dma("pool", bufs["lbp1"][:, :], lbp[0:128, :], sWd)
        track["c_lora"] = p.dma("pool", bufs["lbp2"][:, :], lbp[128:224, :], sWd)
        for t in range(NT):
            prepass_tile(bufs, t, x_src, add_rs, True, track)
        track["xres_wait"] = p.c(sxr) if add_rs else 0
        track["acc_free"] = [track["hg_free"]] * NACC
        if delta_mode:
            p.wait("pe", sDv, track["hg_free"])
            track["acc_free"] = [0] * NACC
        weight_sweep(bufs, "ffn_delta" if delta_mode else "ffn",
                     x_res if add_rs else x_src, x_dst, track)
        if DEBUG and not add_rs:
            phase_barrier()
            p.dma("sp", dbg_xnT[:, :], xnT[:, :], sSt)
            for nm, tt in (("mu", st_mu), ("rstd", st_rstd), ("sum", st_sum),
                           ("ssq", st_ssq), ("var", st_var), ("ln", st_ln),
                           ("e2", st_e2), ("m2", st_m2)):
                pass
            p.dma("sp", dbg_stats[:, 0:NT], st_mu[:, :], sSt)
            p.dma("sp", dbg_stats[:, NT:2 * NT], st_rstd[:, :], sSt)
            p.dma("sp", dbg_stats[:, 2 * NT:3 * NT], st_sum[:, :], sSt)
            p.dma("sp", dbg_stats[:, 3 * NT:4 * NT], st_ssq[:, :], sSt)
            p.dma("sp", dbg_stats[:, 4 * NT:5 * NT], st_var[:, :], sSt)
            p.dma("sp", dbg_stats[:, 5 * NT:6 * NT], st_ln[:, :], sSt)
            p.dma("sp", dbg_xin[:, :], bufs["xin"][:, 0:2 * D], sSt)
            p.dma("sp", dbg_xn[:, :], bufs["xn_b"][:, :], sSt)
            p.dma("sp", dbg_hgT1[:, :], bufs["hgT1"][:, :], sSt)
            p.dma("sp", dbg_wblk[:, :], wblk[:, :], sSt)
            p.wait("sp", sSt, p.c(sSt))
        res = dict(row_stored=list(track["row_stored"]),
                   st4=[p.c(s_) for s_ in sst4])
        ph.close()
        return res

    def attn_sublayer(prev):
        ph = ExitStack()
        psb = lambda n, s, d=F32: ph.enter_context(nc.sbuf_tensor(n, s, d))
        pps = lambda n, s, d=F32: ph.enter_context(nc.psum_tensor(n, s, d))
        bufs = dict(
            add_delta=True,
            xin=psb("a_xin", [128, 3 * D]),
            xn_b=psb("a_xn", [128, 2 * D], BF16),
            stb=psb("a_stb", [128, 4 * FB], BF16),
            battn_t=psb("a_battn", [128, 3 * D], BF16),
            tp_ps=[pps("a_tp0", [128, 1024], BF16), pps("a_tp1", [128, 1024], BF16)],
            acc=pps("a_acc", [128, NACC * 512]),
        )
        track = new_track()
        c_bcast = p.dma("pool", battn_bf[0:1, :], battn[0:1, :], sWd)
        p.wait("sp", sWd, c_bcast)
        track["c_battn"] = p.dma("sp", bufs["battn_t"][:, :],
                                 battn_bf[0:1, :].partition_broadcast(128).squeeze(1), sbt)
        for i_, s_ in enumerate(sst4):
            p.wait("sp", s_, prev["st4"][i_])
        for t in range(NT):
            prepass_tile(bufs, t, x0, True, False, track)
        cc_cnt = weight_sweep(bufs, "qkv", None, None, track)
        for i_, s_ in enumerate(sst4):
            p.wait("pool", s_, cc_cnt[i_])
        c_ag = p.op("pool", lambda E: E.collective_compute(
            "AllGather", ALU.bypass, replica_groups=groups,
            ins=[cc_in[:, :].opt()], outs=[cc_out[:, :].opt()]), sCc)
        qkv_all_stored = [p.c(s_) for s_ in sst4]
        for i_, s_ in enumerate(sst4):
            p.wait("sp", s_, qkv_all_stored[i_])
        p.wait("sp", sCc, p.c(sCc))

        for r_ in range(2):
            pred_dma(qkv_all[r_ * T:(r_ + 1) * T, :], qkv_own[:, :], sXd, r_, True)
            pr = 1 - r_
            pred_dma(qkv_all[pr * T:(pr + 1) * T, :], cc_out[pr * T:(pr + 1) * T, :],
                     sXd, r_, True)
        qkv_norm = p.c(sXd)
        ph.close()

        phase_barrier()
        ph2 = ExitStack()
        psb2 = lambda n, s, d=F32: ph2.enter_context(nc.sbuf_tensor(n, s, d))
        pps2 = lambda n, s, d=F32: ph2.enter_context(nc.psum_tensor(n, s, d))
        strip_m = psb2("c_stripm", [128, 2176])
        strip_h = psb2("c_strip", [128, 2176])
        c_stripm = p.dma("pool", strip_m[:, :], strip_i[:, :], scst)
        qtok = psb2("c_qtok", [128, 2 * S], BF16)
        ktok = psb2("c_ktok", [128, 2 * S], BF16)
        vtok = psb2("c_vtok", [128, 2 * S], BF16)
        qfm = psb2("c_qfm", [128, S], BF16)
        kfm = psb2("c_kfm", [128, S], BF16)
        sbS = psb2("c_S", [128, 2 * S])
        p_b = psb2("c_pb", [128, 2 * S], BF16)
        pt_sb = psb2("c_pt", [128, 2 * S], BF16)
        l_all = psb2("c_l", [128, NQ])
        linv_all = psb2("c_linv", [128, NQ])
        negm = psb2("c_negm", [128, 2])
        lT_sb = psb2("c_lT", [16, 128])
        linvrow = psb2("c_lrow", [128, S])
        avtmp = psb2("c_avtmp", [128, S], BF16)
        s_ps = pps2("c_sps", [128, 2048])
        av_ps = pps2("c_avps", [128, 1024])
        pt_ps = [pps2("c_ptps0", [128, 1024], BF16), pps2("c_ptps1", [128, 1024], BF16)]

        attnout = xnT
        tok_free = {0: 0, 1: 0}
        c_ptsb_free = {0: 0, 1: 0}
        c_ptsb_free_eng = {0: None, 1: None}
        c_norm_prev = 0
        c_lrow_prev = 0
        c_lt_prev = 0
        prev_comb = 0
        prev_exp = [0, 0]
        prev_pt = [0, 0]
        prev_av = 0
        prev_avev = [0, 0]
        c_cc = p.c(sCc)

        for h in range(H):
            hb = h % 2
            p.wait("dve", scst, c_stripm)
            p.opw("dve", lambda E, hh=h, sm=strip_m: E.tensor_scalar(
                strip_h[:, :], sm[:, :], slopes_s[:, hh:hh + 1], None,
                op0=ALU.mult), sDv)
            p.wait("sp", sXd, qkv_norm)
            p.wait("sp", sPe, tok_free[hb])
            lsem = sL2[hb]
            for (xt, base) in ((qtok, 0), (ktok, S), (vtok, 2 * S)):
                col = base + h * 128
                dst3 = xt[:, hb * S:(hb + 1) * S].rearrange("p (j d) -> p j d", d=128)
                p.dma("sp", dst3,
                      qkv_all[:, col:col + 128].rearrange("(j p) d -> p j d", p=128), lsem)
            c_load = p.c(lsem)
            p.wait("pe", lsem, c_load)
            for (src, dstf) in ((qtok, qfm), (ktok, kfm)):
                for g in range(2):
                    bank = pt_ps[g % 2]
                    fe = c_ptsb_free_eng[g % 2]
                    if fe is not None:
                        p.wait("pe", fe, c_ptsb_free[g % 2])
                    for i in range(8):
                        jj = g * 8 + i
                        cpe = p.op("pe", lambda E, b=bank, ii=i, s_=src, j_=jj, hb_=hb:
                                   E.transpose(
                            b[:, ii * 128:(ii + 1) * 128],
                            s_[:, hb_ * S + j_ * 128: hb_ * S + (j_ + 1) * 128],
                            ident_b[:, :]), sPe)
                    p.wait("act", sPe, cpe)
                    cev = p.op("act", lambda E, b=bank, d_=dstf, g_=g: E.copy(
                        d_[:, g_ * 1024:(g_ + 1) * 1024], b[:, :]), sAc)
                    c_ptsb_free[g % 2] = cev
                    c_ptsb_free_eng[g % 2] = sAc
            c_qk_done = p.c(sAc)
            p.wait("pe", sAc, c_qk_done)

            c_comb = [[0] * 4 for _ in range(NQ)]
            c_av = [0] * NQ
            c_avev = [0] * NQ
            c_sb_exp = [0] * NQ
            c_pt_read = [0] * NQ

            def emit_S_and_combine(j):
                kmax = 128 * (j + 1)
                nkc = (kmax + 511) // 512
                if j >= 2:
                    p.wait("pe", sDv, c_comb[j - 2][(128 * (j - 1) + 511) // 512 - 1])
                else:
                    p.wait("pe", sDv, prev_comb)
                off = 2048 - j * 128
                if j >= 2:
                    p.wait("dve", sAc, c_sb_exp[j - 2])
                else:
                    p.wait("dve", sAc, prev_exp[j % 2])
                for kc in range(nkc):
                    n = min(512, kmax - kc * 512)
                    sreg = (j % 2) * 1024 + (kc % 2) * 512
                    if kc >= 2:
                        p.wait("pe", sDv, c_comb[j][kc - 2])
                    cpe = p.op("pe", lambda E, sr=sreg, j_=j, k_=kc, n_=n: E.matmul(
                        s_ps[:, sr:sr + n_],
                        qfm[:, j_ * 128:(j_ + 1) * 128],
                        kfm[:, k_ * 512:k_ * 512 + n_], start=True, stop=True), sPe)
                    p.wait("dve", sPe, cpe)
                    c_comb[j][kc] = p.op("dve", lambda E, sr=sreg, j_=j, k_=kc, n_=n, o=off:
                                         E.scalar_tensor_tensor(
                        sbS[:, (j_ % 2) * S + k_ * 512:(j_ % 2) * S + k_ * 512 + n_],
                        s_ps[:, sr:sr + n_], SCALE,
                        strip_h[:, o + k_ * 512:o + k_ * 512 + n_],
                        op0=ALU.mult, op1=ALU.add), sDv)

            c_negm = [0] * NQ

            def emit_max_negm(j):
                kmax = 128 * (j + 1)
                if j >= 2:
                    p.wait("dve", sPe, c_pt_read[j - 2])
                else:
                    p.wait("dve", sPe, prev_pt[j % 2])
                p.wait("dve", sDv, c_comb[j][(kmax + 511) // 512 - 1])
                p.opw("dve", lambda E, j_=j, km=kmax: E.tensor_scalar(
                    p_b[:, (j_ % 2) * S:(j_ % 2) * S + km],
                    sbS[:, (j_ % 2) * S:(j_ % 2) * S + km], 1.0, MASKNEG,
                    op0=ALU.mult, op1=ALU.max, accum_out=l_all[:, j_:j_ + 1]), sDv)
                c_negm[j] = p.op("dve", lambda E, j_=j: E.tensor_scalar(
                    negm[:, j_ % 2:j_ % 2 + 1], l_all[:, j_:j_ + 1], -1.0, None,
                    op0=ALU.mult), sDv)

            def emit_exp(j):
                kmax = 128 * (j + 1)
                p.wait("act", sDv, c_negm[j])
                ce = p.op("act", lambda E, j_=j, km=kmax: E.activation(
                    p_b[:, (j_ % 2) * S:(j_ % 2) * S + km],
                    sbS[:, (j_ % 2) * S:(j_ % 2) * S + km], AF.Exp,
                    bias=negm[:, j_ % 2:j_ % 2 + 1],
                    accum_out=l_all[:, j_:j_ + 1]), sAc)
                c_sb_exp[j] = ce
                return ce

            def emit_pt_av(j, c_exp):
                nkt = j + 1
                p.wait("pe", sAc, c_exp)
                ngr = (nkt + 7) // 8
                for g in range(ngr):
                    bank = pt_ps[g % 2]
                    fe = c_ptsb_free_eng[g % 2]
                    if fe is not None:
                        p.wait("pe", fe, c_ptsb_free[g % 2])
                    lo, hi = g * 8, min(nkt, g * 8 + 8)
                    for kt in range(lo, hi):
                        cpe = p.op("pe", lambda E, b=bank, k_=kt, j_=j: E.transpose(
                            b[:, (k_ % 8) * 128:(k_ % 8 + 1) * 128],
                            p_b[:, (j_ % 2) * S + k_ * 128:(j_ % 2) * S + (k_ + 1) * 128],
                            ident_b[:, :]), sPe)
                    p.wait("act", sPe, cpe)
                    p.wait("act", sPe, c_av[j - 2] if j >= 2 else prev_av)
                    cev = p.op("act", lambda E, b=bank, lo_=lo, hi_=hi, j_=j: E.copy(
                        pt_sb[:, (j_ % 2) * S + lo_ * 128:(j_ % 2) * S + hi_ * 128],
                        b[:, 0:(hi_ - lo_) * 128]), sAc)
                    c_ptsb_free[g % 2] = cev
                    c_ptsb_free_eng[g % 2] = sAc
                    p.wait("pe", sAc, cev)
                    if g == 0:
                        p.wait("pe", sAc, c_avev[j - 2] if j >= 2 else prev_avev[j % 2])
                    for kt in range(lo, hi):
                        cav = p.op("pe", lambda E, k_=kt, j_=j, hb_=hb: E.matmul(
                            av_ps[:, (j_ % 2) * 512:(j_ % 2) * 512 + 128],
                            vtok[:, hb_ * S + k_ * 128:hb_ * S + (k_ + 1) * 128],
                            pt_sb[:, (j_ % 2) * S + k_ * 128:(j_ % 2) * S + (k_ + 1) * 128],
                            start=(k_ == 0), stop=(k_ == j_),
                            skip_group_check=True), sPe)
                c_pt_read[j] = cpe
                c_av[j] = cav
                return cav

            def emit_avevac(j):
                p.wait("act", sPe, c_av[j])
                p.wait("act", sDv, c_norm_prev)
                c_avev[j] = p.op("act", lambda E, j_=j: E.copy(
                    avtmp[:, j_ * 128:(j_ + 1) * 128],
                    av_ps[:, (j_ % 2) * 512:(j_ % 2) * 512 + 128]), sAc)

            emit_S_and_combine(0)
            emit_max_negm(0)
            emit_S_and_combine(1)
            for j in range(NQ):
                if j + 1 < NQ:
                    emit_max_negm(j + 1)
                if j >= 2:
                    emit_avevac(j - 2)
                ce = emit_exp(j)
                if j + 2 < NQ:
                    emit_S_and_combine(j + 2)
                emit_pt_av(j, ce)
            emit_avevac(NQ - 2)
            emit_avevac(NQ - 1)
            prev_comb = c_comb[NQ - 1][3]
            prev_exp = [c_sb_exp[NQ - 2], c_sb_exp[NQ - 1]]
            prev_pt = [c_pt_read[NQ - 2], c_pt_read[NQ - 1]]
            prev_av = c_av[NQ - 1]
            prev_avev = [c_avev[NQ - 2], c_avev[NQ - 1]]
            tok_free[hb] = c_av[NQ - 1]

            p.wait("dve", sAc, c_sb_exp[NQ - 1])
            p.wait("dve", sPe, c_lt_prev)
            c_li = p.opw("dve", lambda E: E.reciprocal(linv_all[:, :], l_all[:, :]), sDv)
            p.wait("pe", sDv, c_li)
            p.wait("pe", sDv, c_comb[NQ - 1][3])
            p.wait("pe", sXd, c_identf)
            c_lt = p.op("pe", lambda E: E.transpose(
                s_ps[0:16, 0:128], linv_all[:, :], ident_f[:, :]), sPe)
            c_lt_prev = c_lt
            p.wait("act", sPe, c_lt)
            p.wait("act", slr, c_lrow_prev)
            c_lte = p.op("act", lambda E: E.copy(lT_sb[:, :], s_ps[0:16, 0:128]), sAc)
            p.wait("sp", sAc, c_lte)
            c_lst = p.dma("sp", lrow_d[:, :], lT_sb[:, :], sSt)
            p.wait("sp", sSt, c_lst)
            p.wait("sp", sDv, c_norm_prev)
            c_lrow = p.dma("sp", linvrow[:, :].rearrange("p (j q) -> p j q", q=128),
                           lrow_d[:, :].partition_broadcast(128), slr)
            c_lrow_prev = c_lrow
            p.wait("dve", slr, c_lrow)
            p.wait("dve", sAc, c_avev[NQ - 1])
            c_norm_prev = p.op("dve", lambda E, hh=h: E.tensor_tensor(
                attnout[:, hh * S:(hh + 1) * S], avtmp[:, :],
                linvrow[:, :], op=ALU.mult), sDv)

        ph2.close()

        if DEBUG:
            phase_barrier()
            p.dma("sp", dbg_attno[:, :], attnout[:, :], sSt)
            p.dma("sp", dbg_small[:, 0:NQ], l_all[:, :], sSt)
            p.dma("sp", dbg_small[:, NQ:2 * NQ], linv_all[:, :], sSt)
            p.dma("sp", dbg_small[:, 32:32 + NQ],
                  linvrow[:, :].rearrange("p (j q) -> p j q", q=128)[:, :, 0:1].squeeze(2),
                  sSt, allow_slow_non_contiguous=True)
            p.wait("sp", sSt, p.c(sSt))
        phase_barrier()
        ph3 = ExitStack()
        psb3 = lambda n, s, d=F32: ph3.enter_context(nc.sbuf_tensor(n, s, d))
        pps3 = lambda n, s, d=F32: ph3.enter_context(nc.psum_tensor(n, s, d))
        stp = psb3("p_stp", [128, 8 * FB])
        bprojh = psb3("p_bh", [128, D])
        acc3 = pps3("p_acc", [128, 4096])
        c_bh = 0
        for half in range(2):
            if half == 1:
                p.wait("sp", sDv, c_bh)
            cb = p.dma("sp", stp[:, 0:2048],
                       bproj[0:1, half * 2048:(half + 1) * 2048].partition_broadcast(
                           128).squeeze(1), sPd)
            p.wait("dve", sPd, cb)
            c_bh = p.op("dve", lambda E, hh=half: E.tensor_scalar(
                bprojh[:, hh * 2048:(hh + 1) * 2048], stp[:, 0:2048], 0.5, None,
                op0=ALU.mult), sDv)
        NBP = D // FB
        PFI = 16
        wfree = {0: 0, 1: 0}
        accfree = [0] * 8
        stfree = [0] * 4
        cnt = 0
        for bi in range(NBP):
            c0 = bi * FB
            wb = wblk[:, (bi % 2) * WBW:(bi % 2) * WBW + PFI * FB]
            p.wait("pool", sPe, wfree[bi % 2])
            cw_sem = sW2[bi % 2]
            cw = p.dma("pool", wb.rearrange("p (c f) -> p c f", c=PFI),
                       wproj[:, c0:c0 + FB].rearrange("(c p) f -> p c f", p=128), cw_sem)
            for qj in range(NQ):
                reg = acc3[:, (qj % 8) * 512:(qj % 8) * 512 + FB]
                p.wait("pe", cw_sem, cw)
                p.wait("pe", sDv, accfree[qj % 8])
                for fi in range(PFI):
                    lhsT = attnout[:, (fi * NQ + qj) * 128:(fi * NQ + qj + 1) * 128]
                    c_mm = p.op("pe", lambda E, rg=reg, l=lhsT, w=wb, f=fi: E.matmul(
                        rg, l, w[:, f * FB:(f + 1) * FB], start=(f == 0),
                        stop=(f == PFI - 1), skip_group_check=True), sPe)
                wfree[bi % 2] = c_mm
                slot = cnt % 4
                cnt += 1
                p.wait("dve", sPe, c_mm)
                p.wait("dve", sDv, c_bh)
                p.wait("dve", sst4[slot], stfree[slot])
                cev = p.op("dve", lambda E, s_=slot, rg=reg, cc=c0: E.scalar_tensor_tensor(
                    stp[:, s_ * FB:(s_ + 1) * FB], rg, 1.0, bprojh[:, cc:cc + FB],
                    op0=ALU.mult, op1=ALU.add), sDv)
                accfree[qj % 8] = cev
                p.wait("sp", sDv, cev)
                rsd = rs_in[c0 // 2048]
                cc2 = c0 % 2048
                cst = p.dma("sp", rsd[qj * 128:(qj + 1) * 128, cc2:cc2 + FB],
                            stp[:, slot * FB:(slot + 1) * FB], sst4[slot])
                stfree[slot] = cst
        c_all_st4 = [p.c(s_) for s_ in sst4]
        if DEBUG:
            for i_, s_ in enumerate(sst4):
                p.wait("sp", s_, c_all_st4[i_])
            p.dma("sp", dbg_rsina[:, :], rs_in[0][:, :], sSt)
            p.dma("sp", dbg_rsinb[:, :], rs_in[1][:, :], sSt)
        for i_, s_ in enumerate(sst4):
            p.wait("pool", s_, c_all_st4[i_])
        for half in range(2):
            p.op("pool", lambda E, hh=half: E.collective_compute(
                "ReduceScatter", ALU.add, replica_groups=groups,
                ins=[rs_in[hh][:, :].opt()],
                outs=[rs_out[hh][:, :].opt()]), sCc)
        c_rs = p.c(sCc)
        ph3.close()
        return dict(c_rs=c_rs)

    d1 = ffn_sublayer(x0, delta1, add_rs=False, delta_mode=True)
    phase_barrier()
    if DEBUG:
        p.dma("sp", dbg_xres1[:, :], x_res[:, :], sSt)
    at = attn_sublayer(d1)
    phase_barrier()
    p.wait("sp", sCc, at["c_rs"])
    if DEBUG:
        p.dma("sp", dbg_qkvall[:, :], qkv_all[:, :], sSt)
        p.dma("sp", dbg_rsa[:, :], rs_out[0][:, :], sSt)
        p.dma("sp", dbg_rsb[:, :], rs_out[1][:, :], sSt)
    ffn_sublayer(x_res, y, add_rs=True)
    if DEBUG:
        p.dma("sp", dbg_xres2[:, :], x_res[:, :], sSt)
    p.wait("sp", sSt, p.c(sSt))

    with nc.Block() as block:
        @block.tensor
        def _(E):
            for f in p.q["pe"]:
                f(E)

        @block.scalar
        def _(E):
            for f in p.q["act"]:
                f(E)

        @block.vector
        def _(E):
            for f in p.q["dve"]:
                f(E)

        @block.gpsimd
        def _(E):
            for f in p.q["pool"]:
                f(E)

        @block.sync
        def _(E):
            for f in p.q["sp"]:
                f(E)

    es.close()
    return nc


def _host_inputs(L0_attn_hidden_states, L0_attn_w_attn, L0_attn_b_attn, L0_attn_w_proj,
                 L0_attn_b_proj, L0_ffn_base_W, L0_ffn_gate_W, L0_ffn_lora_A, L0_ffn_lora_B):
    x = np.ascontiguousarray(np.asarray(L0_attn_hidden_states, dtype=np.float32))
    w_attn = np.asarray(L0_attn_w_attn, dtype=np.float32)
    b_attn = np.asarray(L0_attn_b_attn, dtype=np.float32)
    w_proj = np.asarray(L0_attn_w_proj, dtype=np.float32)
    b_proj = np.asarray(L0_attn_b_proj, dtype=np.float32)
    base_W = np.asarray(L0_ffn_base_W, dtype=np.float32)
    gate_W = np.asarray(L0_ffn_gate_W, dtype=np.float32)
    lora_A = np.asarray(L0_ffn_lora_A, dtype=np.float32)
    lora_B = np.asarray(L0_ffn_lora_B, dtype=np.float32)

    bwt = np.ascontiguousarray(base_W.T)
    gwt = np.ascontiguousarray(gate_W.T)
    lat = np.ascontiguousarray(lora_A.reshape(ER, D).T)
    lbp = np.ascontiguousarray(lora_B.transpose(0, 2, 1).reshape(ER, D))

    u = np.arange(2176, dtype=np.float32)[None, :]
    i = np.arange(128, dtype=np.float32)[:, None]
    rel = u - 2048.0 - i
    strip = np.where(rel <= 0, rel, np.float32(MASKNEG)).astype(np.float32)
    ident = np.eye(128, dtype=np.float32)
    slopes_full = _alibi_slopes(32)

    in_maps = []
    for c in range(8):
        b, r = c // 2, c % 2
        my = np.arange(r * 16, r * 16 + 16)
        peer = np.arange((1 - r) * 16, (1 - r) * 16 + 16)

        def cols(heads):
            qs = np.concatenate([np.arange(h * 128, (h + 1) * 128) for h in heads])
            return np.concatenate([qs, D + qs, 2 * D + qs])

        perm = np.concatenate([cols(my), cols(peer)])
        in_maps.append(dict(
            x0=np.ascontiguousarray(x[b, r * T:(r + 1) * T, :]),
            wattn=np.ascontiguousarray(w_attn[:, perm]),
            battn=np.ascontiguousarray(b_attn[perm])[None, :],
            wproj=np.ascontiguousarray(w_proj[r * 2048:(r + 1) * 2048, :]),
            bproj=b_proj[None, :].copy(),
            bwt=bwt, gwt=gwt, lat=lat, lbp=lbp,
            slopes_i=np.ascontiguousarray(np.broadcast_to(
                slopes_full[my][None, :], (128, 16))),
            strip_i=strip, ident_i=ident,
            rank_i=np.array([[r]], dtype=np.int32),
        ))
    return in_maps


_NC_CACHE = {}


def kernel(**inputs):
    in_maps = _host_inputs(**inputs)
    if "nc" not in _NC_CACHE:
        _NC_CACHE["nc"] = build_nc()
    nc = _NC_CACHE["nc"]
    res = run_bass_kernel_spmd(nc, in_maps, core_ids=list(range(8)))
    out = np.empty((4, 2048, 4096), dtype=np.float32)
    for c in range(8):
        b, r = c // 2, c % 2
        out[b, r * T:(r + 1) * T, :] = res.results[c]["y"]
    return out
